# version 02
# baseline (speedup 1.0000x reference)
import numpy as np
import ml_dtypes
from contextlib import ExitStack
import concourse.bass as bass
import concourse.mybir as mybir
from concourse.bass_utils import run_bass_kernel_spmd

F32 = mybir.dt.float32
BF16 = mybir.dt.bfloat16
F16 = mybir.dt.float16
I32 = mybir.dt.int32
AF = mybir.ActivationFunctionType
ALU = mybir.AluOpType
AX = mybir.AxisListType
NC8 = list(range(8))
"""Global-tape emission framework for raw Bass.

Steps are appended in one global order (a valid topological order of the
dataflow). Each step runs on one engine and emits exactly ONE instruction
(fn=None -> waits only). Dependencies via use=/make= buffer names; waits
are derived with per-engine knowledge (vector clocks) so redundant waits
are elided.

DMA completions are tracked on dedicated per-engine slot semaphores
(NSLOTS round-robin). A slot is reused only after the issuing engine
waits for the previous completion on it, so every waited value is
unambiguous (at most one DMA in flight per slot).
Compute steps increment their engine's progress semaphore by 1 (engine
pipelines are FIFO so cumulative values are safe).
"""
NSLOTS = 6
ENGINES = ("sync", "gpsimd", "tensor", "vector", "scalar")
DMA_ENGS = ("sync", "gpsimd", "scalar", "vector", "tensor")


class Tape:
    def __init__(self, nc, es):
        self.nc = nc
        self._es = es
        self.steps = []
        self.sems = {}
        self.counts = {}
        self.known = {e: {} for e in ENGINES}
        self.know_at = {}
        self.last_w = {}
        self.readers = {}
        self.dma_slot = {e: 0 for e in ENGINES}
        for e in ENGINES:
            self.sems[e] = es.enter_context(nc.semaphore(f"sem_{e}"))
            self.counts[e] = 0
        for e in DMA_ENGS:
            for i in range(NSLOTS):
                n = f"{e}_d{i}"
                self.sems[n] = es.enter_context(nc.semaphore(n))
                self.counts[n] = 0

    def _need(self, eng, sem, val):
        if sem == eng == "tensor":
            return None
        if self.known[eng].get(sem, 0) >= val:
            return None
        return (sem, val)

    def step(self, eng, fn, use=(), make=(), dma=False, tag=None, raw=False, cc=False):
        waits = {}
        def add(ev):
            if ev is None:
                return
            w = self._need(eng, ev[0], ev[1])
            if w:
                waits[w[0]] = max(waits.get(w[0], 0), w[1])
        for b in use:
            add(self.last_w.get(b))
            if b.startswith("ps"):
                for r in self.readers.get(b, ()):
                    if r[0] != eng and not r[0].startswith(eng + "_"):
                        add(r)
        for b in make:
            add(self.last_w.get(b))
            for r in self.readers.get(b, ()):
                add(r)
        if raw:
            for s, v in waits.items():
                self.known[eng][s] = max(self.known[eng].get(s, 0), v)
            self.steps.append((eng, fn, waits, None, 0, tag))
            return None
        if cc:
            n = f"cc{len([s for s in self.sems if s.startswith('cc')])}"
            self.sems[n] = self._es.enter_context(self.nc.semaphore(n))
            self.counts[n] = 1
            ev_sem, ev_val = n, 1
            inc_sem, inc_n = n, 1
        elif dma:
            slot = self.dma_slot[eng]
            self.dma_slot[eng] = (slot + 1) % NSLOTS
            sname = f"{eng}_d{slot}"
            prev = self.counts[sname]
            if prev > 0 and self.known[eng].get(sname, 0) < prev:
                waits[sname] = max(waits.get(sname, 0), prev)
            self.counts[sname] = prev + 16
            ev_sem, ev_val = sname, prev + 16
            inc_sem, inc_n = sname, 16
        else:
            self.counts[eng] += 1
            ev_sem, ev_val = eng, self.counts[eng]
            inc_sem, inc_n = eng, 1
        for s, v in list(waits.items()):
            self.known[eng][s] = max(self.known[eng].get(s, 0), v)
            snap = self.know_at.get((s, v))
            if snap:
                for s2, v2 in snap.items():
                    if s2 != eng:
                        self.known[eng][s2] = max(self.known[eng].get(s2, 0), v2)
        snap = dict(self.known[eng])
        if dma:
            snap[eng] = self.counts[eng]
            snap[ev_sem] = ev_val
        else:
            snap[eng] = self.counts[eng]
        self.know_at[(ev_sem, ev_val)] = snap
        ev = (ev_sem, ev_val)
        for b in make:
            self.last_w[b] = ev
            self.readers[b] = []
        for b in use:
            self.readers.setdefault(b, []).append(ev)
        if fn is not None:
            self.steps.append((eng, fn, waits, inc_sem, inc_n, tag))
        else:
            self.steps.append((eng, None, waits, None, 0, tag))
        return ev

    def wait_bufs(self, eng, bufs):
        waits = {}
        for b in bufs:
            ev = self.last_w.get(b)
            if ev:
                w = self._need(eng, ev[0], ev[1])
                if w:
                    waits[w[0]] = max(waits.get(w[0], 0), w[1])
            for r in self.readers.get(b, ()):
                w = self._need(eng, r[0], r[1])
                if w:
                    waits[w[0]] = max(waits.get(w[0], 0), w[1])
        for s, v in waits.items():
            self.known[eng][s] = max(self.known[eng].get(s, 0), v)
        if waits:
            self.steps.append((eng, None, waits, None, 0, "wait_bufs"))

    def wait_all(self, eng):
        waits = {}
        for s, v in self.counts.items():
            if s != eng and v > 0 and self.known[eng].get(s, 0) < v:
                waits[s] = v
        for s, v in waits.items():
            self.known[eng][s] = max(self.known[eng].get(s, 0), v)
        if waits:
            self.steps.append((eng, None, waits, None, 0, "waitall"))

    def flush(self, block):
        per_eng = {e: [] for e in ENGINES}
        for eng, fn, waits, inc_sem, inc_n, tag in self.steps:
            per_eng[eng].append((fn, waits, inc_sem, inc_n, tag))
        tape = self
        def make_section(e):
            def section(engobj):
                for fn, w, inc_sem, inc_n, tag in per_eng[e]:
                    for s, v in sorted(w.items()):
                        if v > 0:
                            engobj.wait_ge(tape.sems[s], v)
                    if fn is not None:
                        instr = fn(engobj)
                        if inc_sem is None:
                            continue
                        if instr is None:
                            raise RuntimeError(f"step on {e} returned no instruction (tag={tag})")
                        instr.then_inc(tape.sems[inc_sem], inc_n)
            return section
        for e in ENGINES:
            if per_eng[e]:
                getattr(block, e)(make_section(e))



"""Launch 1: LN + in_proj + conv/silu + dt-softplus + A_cum + x*dt transpose.
Column layout: [halo(3) | own(1024)] everywhere (cols 0:3 halo, 3:1027 own).
PSUM column mapping: cols 0:1024 -> ps_a/ps_b [128,1024]; cols 1024:1027 -> ps_h slices."""


TOK, HALO = 1024, 4
TW = TOK + HALO
D, KT, OP = 2048, 16, 8512
NH, HD, INTER, STATE, CH = 64, 64, 4096, 128, 256

def emit_launch1(fctx):
    nc, t, IO = fctx
    nxbc=17; ngate=16; do_dt=True; dt_stage=99; s4copy=False; skip_stats=False; skip_mm=False
    xT = IO["xT"]
    ipwT = IO["ipwT"]
    convw = IO["convw"]
    convb = IO["convb"]
    dtb_in = IO["dtb"]
    alog_in = IO["alog"]
    selmat_in = IO["selmat"]
    eye_in = IO["eye"]
    o_xn = IO["o_xn"]
    o_gate = IO["o_gate"]
    o_xpre = IO["o_xpre"]
    o_xdtm = IO["o_xdtm"]
    o_B = IO["o_B"]
    o_C = IO["o_C"]
    o_acum = IO["o_acum"]
    o_dt = IO["o_dt"]

    es = ExitStack()
    def sb(name, shp, dt=F32): return es.enter_context(nc.sbuf_tensor("L1_" + name, shp, dt))
    def psa(name, shp, dt=F32): return es.enter_context(nc.psum_tensor("L1_" + name, shp, dt))
    x_s = [sb(f"x{k}", [128, TW]) for k in range(KT)]
    xb_s = [sb(f"xb{k}", [128, TW], BF16) for k in range(KT)]
    x2_s = [sb(f"x2_{i}", [128, TW], BF16) for i in range(2)]
    ones_col = sb("ones_col", [128, 1], BF16)
    ones_row1 = sb("ones_row1", [1, 128])
    statm = sb("statm", [1, TW]); stat2 = sb("stat2", [1, TW])
    statv = sb("statv", [1, TW]); statr = sb("statr", [1, TW])
    convw_s = sb("convw_s", [128, 136]); convb_s = sb("convb_s", [128, 34])
    dtb_s = sb("dtb_s", [64, 1]); alog_s = sb("alog_s", [64, 1]); negA_s = sb("negA_s", [64, 1])
    selmat_s = sb("selmat_s", [64, 32 * 128], BF16)
    ident = sb("ident", [128, 128], BF16)
    wbuf = [sb(f"wbuf{i}", [128, KT * 256], BF16) for i in range(2)]
    dt_s = sb("dt_s", [64, TOK]); dtbf_s = sb("dtbf_s", [64, TOK], BF16)
    adt_s = sb("adt_s", [64, TOK]); acum_s = sb("acum_s", [64, TOK])
    zero64 = sb("zero64", [64, TOK])
    dtw1 = sb("dtw1", [64, TOK]); dtw2 = sb("dtw2", [64, TOK])
    eps1 = sb("eps1", [1, 1]); one64 = sb("one64", [64, 1])
    tmpa = sb("tmpa", [128, TW]); tmpb = sb("tmpb", [128, TW])
    xbc_f = [sb(f"xbc_f{i}", [128, TW]) for i in range(2)]
    conv_acc = sb("conv_acc", [128, TOK]); conv_acc2 = sb("conv_acc2", [128, TOK])
    sil_bf = [sb(f"sil_bf{i}", [128, TOK], BF16) for i in range(2)]
    xdt_bf = sb("xdt_bf", [128, TOK], BF16)
    trevac = [sb(f"trevac{i}", [128, 128], BF16) for i in range(2)]
    gate_bf = [sb(f"gate_bf{i}", [128, TOK], BF16) for i in range(2)]

    ps_a = psa("ps_a", [128, TOK]); ps_b = psa("ps_b", [128, TOK])
    ps_h1 = psa("ps_h1", [128, 4]); ps_h2 = psa("ps_h2", [128, 4])
    ps_dtb = psa("ps_dtb", [128, 512])
    ps_tr = psa("ps_tr", [128, 128], BF16)

    if True:
        pass
        for k in range(KT):
            t.step("sync", lambda e, k=k: e.dma_start(out=x_s[k][:], in_=xT[k*128:(k+1)*128, :]),
                   make=[f"x{k}"], dma=True)
        t.step("gpsimd", lambda e: e.dma_start(out=convw_s[:], in_=convw), make=["convw"], dma=True)
        t.step("gpsimd", lambda e: e.dma_start(out=convb_s[:], in_=convb), make=["convb"], dma=True)
        t.step("gpsimd", lambda e: e.dma_start(out=dtb_s[:], in_=dtb_in), make=["dtb"], dma=True)
        t.step("gpsimd", lambda e: e.dma_start(out=alog_s[:], in_=alog_in), make=["alog"], dma=True)
        t.step("gpsimd", lambda e: e.dma_start(out=selmat_s[:], in_=selmat_in), make=["selmat"], dma=True)
        t.step("gpsimd", lambda e: e.dma_start(out=ident[:], in_=eye_in), make=["eye"], dma=True)
        t.step("gpsimd", lambda e: e.memset(ones_col[:], 1.0), make=["ones_col"])
        t.step("gpsimd", lambda e: e.memset(ones_row1[:], 1.0), make=["ones_row1"])
        t.step("gpsimd", lambda e: e.memset(zero64[:], 0.0), make=["zero64"])
        t.step("gpsimd", lambda e: e.memset(eps1[:], 1e-5), make=["eps1"])
        t.step("gpsimd", lambda e: e.memset(one64[:], 1.0), make=["one64"])

        def chunks_all(hoff):
            return [(0, 512, ps_a, "ps_a", 0), (512, 512, ps_a, "ps_a", 512), (1024, 4, ps_h1, "ps_h1", 0)]
        if skip_stats:
            for k in range(KT):
                t.step("vector", lambda e, k=k: e.tensor_copy(xb_s[k][:], x_s[k][:]),
                       use=[f"x{k}"], make=[f"xb{k}"])
        for k in range(() if skip_stats else range(KT)) if False else (range(0) if skip_stats else range(KT)):
            t.step("vector", lambda e, k=k: e.tensor_copy(xb_s[k][:], x_s[k][:]),
                   use=[f"x{k}"], make=[f"xb{k}"])
            xi = k % 2
            t.step("scalar", lambda e, k=k, xi=xi: e.activation(x2_s[xi][:], xb_s[k][:], AF.Square),
                   use=[f"xb{k}"], make=[f"x2_{xi}"])
            for (o, n, P, pn, po) in chunks_all(0):
                t.step("tensor", lambda e, k=k, o=o, n=n, P=P, po=po: e.matmul(
                        P[0:1, po:po+n], ones_col[:, 0:1], xb_s[k][:, o:o+n],
                        start=(k == 0), stop=(k == KT-1)),
                       use=["ones_col", f"xb{k}"], make=[pn])
            for (o, n, P, pn, po) in [(0, 512, ps_b, "ps_b", 0), (512, 512, ps_b, "ps_b", 512), (1024, 4, ps_h2, "ps_h2", 0)]:
                t.step("tensor", lambda e, k=k, o=o, n=n, P=P, po=po, xi=xi: e.matmul(
                        P[0:1, po:po+n], ones_col[:, 0:1], x2_s[xi][:, o:o+n],
                        start=(k == 0), stop=(k == KT-1)),
                       use=["ones_col", f"x2_{xi}"], make=[pn])
        if skip_stats:
            pass
        if not skip_stats:
            t.step("scalar", lambda e: e.activation(statm[0:1, 0:TOK], ps_a[0:1, :], AF.Copy, scale=1.0/D),
                   use=["ps_a"], make=["statm"])
            t.step("scalar", lambda e: e.activation(statm[0:1, TOK:TW], ps_h1[0:1, :], AF.Copy, scale=1.0/D),
                   use=["ps_h1"], make=["statm"])
            t.step("scalar", lambda e: e.activation(stat2[:], statm[:], AF.Square),
                   use=["statm"], make=["stat2"])
            t.step("vector", lambda e: e.scalar_tensor_tensor(statv[0:1, 0:TOK], ps_b[0:1, :], 1.0/D, stat2[0:1, 0:TOK], ALU.mult, ALU.subtract),
                   use=["ps_b", "stat2"], make=["statv"])
            t.step("vector", lambda e: e.scalar_tensor_tensor(statv[0:1, TOK:TW], ps_h2[0:1, :], 1.0/D, stat2[0:1, TOK:TW], ALU.mult, ALU.subtract),
                   use=["ps_h2", "stat2"], make=["statv"])
            t.step("scalar", lambda e: e.activation(statr[:], statv[:], AF.Sqrt, bias=eps1[0:1, 0:1]),
                   use=["statv", "eps1"], make=["statr"])
            t.step("vector", lambda e: e.reciprocal(statr[:], statr[:]), use=["statr"], make=["statr"])
            for (o, n, P, pn, po) in chunks_all(0):
                t.step("tensor", lambda e, o=o, n=n, P=P, po=po: e.matmul(
                        P[:, po:po+n], ones_row1[0:1, :], statm[0:1, o:o+n], start=True, stop=True),
                       use=["ones_row1", "statm"], make=[pn])
            for (o, n, P, pn, po) in [(0, 512, ps_b, "ps_b", 0), (512, 512, ps_b, "ps_b", 512), (1024, 4, ps_h2, "ps_h2", 0)]:
                t.step("tensor", lambda e, o=o, n=n, P=P, po=po: e.matmul(
                        P[:, po:po+n], ones_row1[0:1, :], statr[0:1, o:o+n], start=True, stop=True),
                       use=["ones_row1", "statr"], make=[pn])
            for k in range(KT):
                tb = [tmpa, tmpb][k % 2]; tbn = f"tmp{k%2}"
                t.step("vector", lambda e, k=k, tb=tb: e.tensor_sub(tb[:, 0:TOK], x_s[k][:, 0:TOK], ps_a[:]),
                       use=[f"x{k}", "ps_a"], make=[tbn])
                t.step("vector", lambda e, k=k, tb=tb: e.tensor_sub(tb[:, TOK:TW], x_s[k][:, TOK:TW], ps_h1[:]),
                       use=[f"x{k}", "ps_h1"], make=[tbn])
                t.step("vector", lambda e, k=k, tb=tb: e.tensor_mul(xb_s[k][:, 0:TOK], tb[:, 0:TOK], ps_b[:]),
                       use=[tbn, "ps_b"], make=[f"xb{k}"])
                t.step("vector", lambda e, k=k, tb=tb: e.tensor_mul(xb_s[k][:, TOK:TW], tb[:, TOK:TW], ps_h2[:]),
                       use=[tbn, "ps_h2"], make=[f"xb{k}"])
                t.step("sync", lambda e, k=k: e.dma_start(out=o_xn[k*128:(k+1)*128, :], in_=xb_s[k][:, HALO:TW]),
                       use=[f"xb{k}"], dma=True)

        def wdma(g, ocol0, ow):
            for k in range(KT):
                src = ipwT[k*128:(k+1)*128, ocol0:ocol0+ow]
                dst = wbuf[g % 2][:].rearrange("p (k o) -> p k o", k=KT)[:, k, 0:ow]
                t.step("gpsimd", lambda e, src=src, dst=dst: e.dma_start(out=dst, in_=src),
                       make=[f"wb{g%2}"], dma=True)

        def mm_group(g, ow, own_only):
            nt = (ow + 127) // 128
            for to in range(nt):
                mw = min(128, ow - to*128)
                P = [ps_a, ps_b][to]; pn = ["ps_a", "ps_b"][to]
                if own_only:
                    chunks = [(HALO, 512, P, pn, 0), (HALO+512, 512, P, pn, 512)]
                else:
                    chunks = [(0, 512, P, pn, 0), (512, 512, P, pn, 512), (1024, 4, [ps_h1, ps_h2][to], ["ps_h1", "ps_h2"][to], 0)]
                for k in range(KT):
                    lhs = wbuf[g % 2][:].rearrange("p (k o) -> p k o", k=KT)[:, k, to*128:to*128+mw]
                    for (ro, n, PP, ppn, po) in chunks:
                        t.step("tensor", lambda e, k=k, lhs=lhs, PP=PP, po=po, n=n, ro=ro, mw=mw: e.matmul(
                                PP[0:mw, po:po+n], lhs, xb_s[k][:, ro:ro+n],
                                start=(k == 0), stop=(k == KT-1)),
                               use=[f"wb{g%2}", f"xb{k}"], make=[ppn])

        if not do_dt:
            t.wait_all("sync"); t.flush(block); es.close(); return nc
        wdma(0, 8448, 64)
        if dt_stage < 2:
            t.wait_all("sync"); t.flush(block); es.close(); return nc
        if not skip_mm:
            mm_group(0, 64, own_only=True)
        if dt_stage < 3:
            t.wait_all("sync"); t.flush(block); es.close(); return nc
        if s4copy == "vcopy64":
            t.step("vector", lambda e: e.tensor_copy(dtw1[:], ps_a[0:64, 0:TOK]),
                   use=["ps_a"], make=["dtw1"])
        elif s4copy == "vcopyfull":
            t.step("vector", lambda e: e.tensor_copy(tmpa[:, 0:TOK], ps_a[:]),
                   use=["ps_a"], make=["tmp0"])
        elif s4copy == "scopy64":
            t.step("scalar", lambda e: e.activation(dtw1[:], ps_a[0:64, 0:TOK], AF.Copy),
                   use=["ps_a"], make=["dtw1"])
        elif s4copy == "vhalf":
            t.step("vector", lambda e: e.tensor_copy(dtw1[:, 0:512], ps_a[0:64, 0:512]),
                   use=["ps_a"], make=["dtw1"])
        else:
            t.step("vector", lambda e: e.tensor_scalar(dtw1[:], ps_a[0:64, 0:TOK], dtb_s[:, 0:1], 0.0, ALU.add, ALU.max),
                   use=["ps_a", "dtb"], make=["dtw1"])
        if dt_stage < 4:
            t.wait_all("sync"); t.flush(block); es.close(); return nc
        t.step("scalar", lambda e: e.activation(dtw2[:], ps_a[0:64, 0:TOK], AF.Abs, bias=dtb_s[:, 0:1]),
               use=["ps_a", "dtb"], make=["dtw2"])
        t.step("scalar", lambda e: e.activation(dtw2[:], dtw2[:], AF.Exp, scale=-1.0),
               use=["dtw2"], make=["dtw2"])
        t.step("scalar", lambda e: e.activation(dtw2[:], dtw2[:], AF.Ln, bias=one64[:, 0:1]),
               use=["dtw2", "one64"], make=["dtw2"])
        t.step("vector", lambda e: e.tensor_add(dt_s[:], dtw1[:], dtw2[:]),
               use=["dtw1", "dtw2"], make=["dt_s"])
        t.step("vector", lambda e: e.tensor_scalar_min(dt_s[:], dt_s[:], 100.0),
               use=["dt_s"], make=["dt_s"])
        if dt_stage < 5:
            t.wait_all("sync"); t.flush(block); es.close(); return nc
        t.step("vector", lambda e: e.tensor_copy(dtbf_s[:], dt_s[:]), use=["dt_s"], make=["dtbf"])
        t.step("sync", lambda e: e.dma_start(out=o_dt, in_=dt_s[:]), use=["dt_s"], dma=True)
        t.step("scalar", lambda e: e.activation(negA_s[:], alog_s[:], AF.Exp), use=["alog"], make=["negA"])
        t.step("vector", lambda e: e.tensor_scalar_mul(negA_s[:], negA_s[:], -1.0), use=["negA"], make=["negA"])
        t.step("vector", lambda e: e.tensor_scalar_mul(adt_s[:], dt_s[:], negA_s[:, 0:1]),
               use=["dt_s", "negA"], make=["adt"])
        if dt_stage < 6:
            t.wait_all("sync"); t.flush(block); es.close(); return nc
        for c in range(4):
            t.step("vector", lambda e, c=c: e.tensor_tensor_scan(
                    acum_s[:, c*CH:(c+1)*CH], adt_s[:, c*CH:(c+1)*CH], zero64[:, c*CH:(c+1)*CH],
                    0.0, ALU.add, ALU.add),
                   use=["adt", "zero64"], make=["acum"])
        t.step("sync", lambda e: e.dma_start(out=o_acum, in_=acum_s[:]), use=["acum"], dma=True)

        gi = 1
        for j in range(nxbc):
            wdma(gi, 4096 + 256*j, 256)
            mm_group(gi, 256, own_only=False)
            for to in range(2):
                tile_idx = 2*j + to
                P = [ps_a, ps_b][to]; pn = ["ps_a", "ps_b"][to]
                xf = xbc_f[to]; xfn = f"xbc{to}"
                t.step("vector", lambda e, P=P, xf=xf: e.tensor_copy(xf[:, 0:TOK], P[:]),
                       use=[pn], make=[xfn])
                PH = [ps_h1, ps_h2][to]; phn = ["ps_h1", "ps_h2"][to]
                t.step("vector", lambda e, xf=xf, PH=PH: e.tensor_copy(xf[:, TOK:TW], PH[:]),
                       use=[phn], make=[xfn])
                t.step("vector", lambda e, xf=xf, ti=tile_idx: e.tensor_scalar_mul(
                        conv_acc[:], xf[:, 1:1+TOK], convw_s[:, ti*4:ti*4+1]),
                       use=[xfn, "convw"], make=["cacc"])
                accs = [conv_acc, conv_acc2]
                for kk in range(1, 4):
                    src, dst = accs[(kk+1) % 2], accs[kk % 2]
                    sn = "cacc" if src is conv_acc else "cacc2"
                    dn = "cacc" if dst is conv_acc else "cacc2"
                    t.step("vector", lambda e, xf=xf, ti=tile_idx, kk=kk, src=src, dst=dst: e.scalar_tensor_tensor(
                            dst[:], xf[:, kk+1:kk+1+TOK], convw_s[:, ti*4+kk:ti*4+kk+1], src[:], ALU.mult, ALU.add),
                           use=[xfn, "convw", sn], make=[dn])
                sil = sil_bf[to]
                t.step("scalar", lambda e, ti=tile_idx, sil=sil: e.activation(
                        sil[:], conv_acc2[:], AF.Silu, bias=convb_s[:, ti:ti+1]),
                       use=["cacc2", "convb"], make=[f"sil{to}"])
                if j < 16:
                    chan0 = 256*j + 128*to
                    t.step("sync", lambda e, sil=sil, chan0=chan0: e.dma_start(
                            out=o_xpre[chan0:chan0+128, :], in_=sil[:]),
                           use=[f"sil{to}"], dma=True)
                    for no in (0, 512):
                        t.step("tensor", lambda e, ti=tile_idx, no=no: e.matmul(
                                ps_dtb[:], selmat_s[:, ti*128:(ti+1)*128], dtbf_s[:, no:no+512], start=True, stop=True),
                               use=["selmat", "dtbf"], make=["ps_dtb"])
                        t.step("vector", lambda e, sil=sil, no=no: e.tensor_mul(xdt_bf[:, no:no+512], sil[:, no:no+512], ps_dtb[:]),
                               use=[f"sil{to}", "ps_dtb"], make=["xdt_bf"])
                    for tt in range(8):
                        tv = trevac[tt % 2]; tvn = f"trevac{tt%2}"
                        t.step("tensor", lambda e, tt=tt: e.transpose(ps_tr[:], xdt_bf[:, tt*128:(tt+1)*128], ident[:]),
                               use=["xdt_bf", "eye"], make=["ps_tr"])
                        t.step("vector", lambda e, tv=tv: e.tensor_copy(tv[:], ps_tr[:]),
                               use=["ps_tr"], make=[tvn])
                        t.step("sync", lambda e, tv=tv, tt=tt, chan0=chan0: e.dma_start(
                                out=o_xdtm[tt*128:(tt+1)*128, chan0:chan0+128], in_=tv[:]),
                               use=[tvn], dma=True)
                else:
                    dst = o_B if to == 0 else o_C
                    t.step("sync", lambda e, sil=sil, dst=dst: e.dma_start(out=dst, in_=sil[:]),
                           use=[f"sil{to}"], dma=True)
            gi += 1
        for j in range(ngate):
            wdma(gi, 256*j, 256)
            mm_group(gi, 256, own_only=True)
            for to in range(2):
                P = [ps_a, ps_b][to]; pn = ["ps_a", "ps_b"][to]
                gb = gate_bf[to]
                t.step("vector", lambda e, P=P, gb=gb: e.tensor_copy(gb[:], P[:]),
                       use=[pn], make=[f"gb{to}"])
                t.step("sync", lambda e, gb=gb, j=j, to=to: e.dma_start(
                        out=o_gate[256*j+128*to:256*j+128*to+128, :], in_=gb[:]),
                       use=[f"gb{to}"], dma=True)
            gi += 1
    es.close()





"""Launch 2: mamba SSM local-chunk work.
Inputs per core: xdtm [1024,4096] bf16 (token-major x*dt), B,C [128,1024] bf16,
acum [64,1024] fp32, xpre [4096,1024] bf16, D_chan [128,32] fp32, masks.
Outputs: ypart [4096,1024] bf16 (Y_diag + D_res, feature-major), states [4,128,4096] fp32."""


TOK, CH, NCH, NH, HD, STATE, INTER = 1024, 256, 4, 64, 64, 128, 4096

def emit_launch2(fctx):
    nc, t, IO = fctx
    i_xdtm = IO["xdtm"]
    i_B = IO["Bm"]
    i_C = IO["Cm"]
    i_acum = IO["acum"]
    i_xpre = IO["xpre"]
    i_dchan = IO["dchan"]
    i_mask = IO["maskst"]
    i_eye = IO["eye"]
    o_ypart = IO["o_ypart"]
    o_states = IO["o_states"]
    o_T = IO["o_T"]

    es = ExitStack()
    def sb(name, shp, dt=F32): return es.enter_context(nc.sbuf_tensor("L2_" + name, shp, dt))
    def psa(name, shp, dt=F32): return es.enter_context(nc.psum_tensor("L2_" + name, shp, dt))
    xtm = [sb(f"xtm{i}", [128, INTER], BF16) for i in range(8)]
    xpre_s = [sb(f"xpre{i}", [128, TOK], BF16) for i in range(32)]
    B_s = sb("B_s", [128, TOK], BF16); C_s = sb("C_s", [128, TOK], BF16)
    Btm = [sb(f"Btm{i}", [128, 128], BF16) for i in range(8)]
    acum_s = sb("acum_s", [64, TOK])
    acumT = [sb(f"acumT{i}", [128, 64]) for i in range(8)]
    dec_fm = sb("dec_fm", [64, TOK])
    decT = [sb(f"decT{i}", [128, 64]) for i in range(8)]
    rowbuf = sb("rowbuf", [1, TOK])
    dchan_s = sb("dchan_s", [128, 32])
    mask_s = sb("mask_s", [128, 2 * CH])
    eye_s = sb("eye_s", [128, 128], BF16)
    eyef_s = sb("eyef_s", [128, 128])
    ones_row1 = sb("ones_row1", [1, 128])
    g0m = [sb(f"g0m{i}", [128, CH], BF16) for i in range(2)]
    argb = [sb(f"argb{i}", [128, CH]) for i in range(2)]
    Lb = [sb(f"Lb{i}", [128, CH], BF16) for i in range(2)]
    Mb = [sb(f"Mb{i}", [128, CH], BF16) for i in range(2)]
    Bd = [sb(f"Bd{i}", [128, 128], BF16) for i in range(2)]
    ybf = [sb(f"ybf{i}", [64, CH]) for i in range(2)]
    stch = sb("stch", [128, INTER])
    trv = [sb(f"trv{i}", [128, 128], BF16) for i in range(2)]
    trvf = [sb(f"trvf{i}", [128, 64]) for i in range(2)]

    ps_tr = psa("ps_tr", [128, 128], BF16)
    ps_trf = psa("ps_trf", [128, 64])
    ps_g0 = psa("ps_g0", [128, CH])
    ps_bc = psa("ps_bc", [128, CH])
    ps_y = [psa(f"ps_y{i}", [64, CH]) for i in range(2)]
    ps_st = [psa(f"ps_st{i}", [128, 64]) for i in range(2)]

    if True:
        pass
        for i in range(8):
            t.step("sync", lambda e, i=i: e.dma_start(out=xtm[i][:], in_=i_xdtm[i*128:(i+1)*128, :]),
                   make=[f"xtm{i}"], dma=True)
        for i in range(32):
            t.step("sync", lambda e, i=i: e.dma_start(out=xpre_s[i][:], in_=i_xpre[i*128:(i+1)*128, :]),
                   make=[f"xpre{i}"], dma=True)
        t.step("gpsimd", lambda e: e.dma_start(out=B_s[:], in_=i_B), make=["B"], dma=True)
        t.step("gpsimd", lambda e: e.dma_start(out=C_s[:], in_=i_C), make=["C"], dma=True)
        t.step("gpsimd", lambda e: e.dma_start(out=acum_s[:], in_=i_acum), make=["acum"], dma=True)
        t.step("gpsimd", lambda e: e.dma_start(out=dchan_s[:], in_=i_dchan), make=["dchan"], dma=True)
        t.step("gpsimd", lambda e: e.dma_start(out=mask_s[:], in_=i_mask), make=["mask"], dma=True)
        t.step("gpsimd", lambda e: e.dma_start(out=eye_s[:], in_=i_eye), make=["eye"], dma=True)
        t.step("gpsimd", lambda e: e.dma_start(out=eyef_s[:], in_=i_eye), make=["eyef"], dma=True)
        t.step("gpsimd", lambda e: e.memset(ones_row1[:], 1.0), make=["ones1"])
        for c in range(NCH):
            t.step("scalar", lambda e, c=c: e.activation(
                    dec_fm[:, c*CH:(c+1)*CH], acum_s[:, c*CH:(c+1)*CH], AF.Exp,
                    bias=acum_s[:, (c+1)*CH-1:(c+1)*CH], scale=-1.0),
                   use=["acum"], make=["dec"])
        for i in range(8):
            t.step("tensor", lambda e, i=i: e.transpose(ps_trf[:], acum_s[0:64, i*128:(i+1)*128], eyef_s[0:64, 0:64]),
                   use=["acum", "eyef"], make=["ps_trf"])
            t.step("vector", lambda e, i=i: e.tensor_copy(acumT[i][:], ps_trf[:]),
                   use=["ps_trf"], make=[f"acumT{i}"])
            t.step("tensor", lambda e, i=i: e.transpose(ps_trf[:], dec_fm[0:64, i*128:(i+1)*128], eyef_s[0:64, 0:64]),
                   use=["dec", "eyef"], make=["ps_trf"])
            t.step("vector", lambda e, i=i: e.tensor_copy(decT[i][:], ps_trf[:]),
                   use=["ps_trf"], make=[f"decT{i}"])
            t.step("tensor", lambda e, i=i: e.transpose(ps_tr[:], B_s[:, i*128:(i+1)*128], eye_s[:]),
                   use=["B", "eye"], make=["ps_tr"])
            t.step("vector", lambda e, i=i: e.tensor_copy(Btm[i][:], ps_tr[:]),
                   use=["ps_tr"], make=[f"Btm{i}"])
        for c in range(NCH):
            st0 = 2 * c
            for s in range(2):
                t.step("tensor", lambda e, c=c, s=s: e.matmul(
                        ps_g0[:], B_s[:, c*CH+s*128:c*CH+(s+1)*128], C_s[:, c*CH:(c+1)*CH],
                        start=True, stop=True),
                       use=["B", "C"], make=["ps_g0"])
                t.step("vector", lambda e, s=s: e.tensor_mul(g0m[s][:], ps_g0[:], mask_s[:, s*CH:(s+1)*CH]),
                       use=["ps_g0", "mask"], make=[f"g0m{s}"])
            for h in range(NH):
                t.step("gpsimd", lambda e, h=h: e.dma_start(out=rowbuf[:], in_=acum_s[h:h+1, :]),
                       use=["acum"], make=["rowb"], dma=True)
                t.step("tensor", lambda e, c=c: e.matmul(
                        ps_bc[:], ones_row1[0:1, :], rowbuf[0:1, c*CH:(c+1)*CH], start=True, stop=True),
                       use=["ones1", "rowb"], make=["ps_bc"])
                for s in range(2):
                    tt = st0 + s
                    t.step("vector", lambda e, s=s, tt=tt, h=h: e.scalar_tensor_tensor(
                            argb[s][:], ps_bc[:], acumT[tt][:, h:h+1], mask_s[:, s*CH:(s+1)*CH],
                            ALU.subtract, ALU.mult),
                           use=["ps_bc", f"acumT{tt}", "mask"], make=[f"arg{s}"])
                    t.step("scalar", lambda e, s=s: e.activation(Lb[s][:], argb[s][:], AF.Exp),
                           use=[f"arg{s}"], make=[f"L{s}"])
                    t.step("vector", lambda e, s=s: e.tensor_mul(Mb[s][:], Lb[s][:], g0m[s][:]),
                           use=[f"L{s}", f"g0m{s}"], make=[f"M{s}"])
                pyi = h % 2
                for s in range(2):
                    tt = st0 + s
                    t.step("tensor", lambda e, s=s, tt=tt, h=h, pyi=pyi: e.matmul(
                            ps_y[pyi][:], xtm[tt][:, h*64:(h+1)*64], Mb[s][:],
                            start=(s == 0), stop=(s == 1)),
                           use=[f"xtm{tt}", f"M{s}"], make=[f"ps_y{pyi}"])
                ctile = h // 2; cbase = (h % 2) * 64
                t.step("vector", lambda e, h=h, c=c, pyi=pyi, ctile=ctile, cbase=cbase: e.scalar_tensor_tensor(
                        ybf[pyi][:], xpre_s[ctile][cbase:cbase+64, c*CH:(c+1)*CH],
                        dchan_s[cbase:cbase+64, ctile:ctile+1], ps_y[pyi][:],
                        ALU.mult, ALU.add),
                       use=[f"xpre{ctile}", "dchan", f"ps_y{pyi}"], make=[f"ybf{pyi}"])
                t.step("sync", lambda e, h=h, c=c, pyi=pyi: e.dma_start(
                        out=o_ypart[h*64:(h+1)*64, c*CH:(c+1)*CH], in_=ybf[pyi][:]),
                       use=[f"ybf{pyi}"], dma=True)
                psi = h % 2
                for s in range(2):
                    tt = st0 + s
                    t.step("vector", lambda e, s=s, tt=tt, h=h: e.tensor_scalar_mul(
                            Bd[s][:], Btm[tt][:], decT[tt][:, h:h+1]),
                           use=[f"Btm{tt}", f"decT{tt}"], make=[f"Bd{s}"])
                    t.step("tensor", lambda e, s=s, tt=tt, h=h, psi=psi: e.matmul(
                            ps_st[psi][:], Bd[s][:], xtm[tt][:, h*64:(h+1)*64],
                            start=(s == 0), stop=(s == 1)),
                           use=[f"Bd{s}", f"xtm{tt}"], make=[f"ps_st{psi}"])
                t.step("vector", lambda e, h=h, psi=psi: e.tensor_copy(stch[:, h*64:(h+1)*64], ps_st[psi][:]),
                       use=[f"ps_st{psi}"], make=["stch"])
            t.step("sync", lambda e, c=c: e.dma_start(out=o_states[c], in_=stch[:]),
                   use=["stch"], dma=True)
            t.step("sync", lambda e, c=c: e.dma_start(out=o_T[:, c:c+1], in_=acum_s[:, (c+1)*CH-1:(c+1)*CH], allow_slow_non_contiguous=True),
                   use=["acum"], dma=True)
    es.close()





"""Launch 4: MoE expert-parallel. Core c = group c, all 8192 tokens."""


S, D, E, I2, NG = 8192, 2048, 32, 1536, 8
TC = 512
NTC = S // TC

def emit_launch4(fctx):
    nc, t, IO = fctx
    i_xn = IO["xnfull"]
    i_gwT = IO["gwT"]
    i_gupT = IO["gupT"]
    i_dnT = IO["dnT"]
    i_mg = IO["mgf"]
    i_de = IO["dei"]
    i_cid = IO["cid"]
    i_cidf = IO["cidf"]
    i_eye = IO["eye"]
    gusc = nc.dram_tensor("gusc", [D, I2], F32)
    dnsc = nc.dram_tensor("dnsc", [768, D], F32)
    o_contrib = IO["o_contrib"]

    es = ExitStack()
    def sb(name, shp, dt=F32): return es.enter_context(nc.sbuf_tensor("L4_" + name, shp, dt))
    def psa(name, shp, dt=F32): return es.enter_context(nc.psum_tensor("L4_" + name, shp, dt))
    xn_s = [sb(f"xn{k}", [128, TC], BF16) for k in range(16)]
    gw_s = sb("gw_s", [128, 16 * E], BF16)
    mg_s = sb("mg_s", [1, E]); cid_s = sb("cid_s", [1, 1], I32); cidf_s = sb("cidf_s", [1, 1])
    de_s = sb("de_s", [1, NG], I32)
    gsel = sb("gsel", [1, E]); gselB = sb("gselB", [128, E], BF16)
    probs = sb("probs", [128, E]); porig = sb("porig", [128, E])
    prow = sb("prow", [128, 1]); psum_r = sb("psum_r", [128, 1])
    mcol = sb("mcol", [128, 1]); scol = sb("scol", [128, 1]); rscol = sb("rscol", [128, 1])
    maskb = sb("maskb", [128, E]); ptop = sb("ptop", [128, E], BF16)
    mask8 = sb("mask8", [128, E], BF16)
    wcol = sb("wcol", [128, 1]); wrow = sb("wrow", [1, S])
    ones_row1 = sb("ones_row1", [1, 128]); ones_colb = sb("ones_colb", [128, 1], BF16)
    eye_s = sb("eye_s", [128, 128])
    wbuf = [sb(f"wbuf{i}", [128, TC], BF16) for i in range(2)]
    sg_sb = [sb(f"sg{i}", [128, TC], BF16) for i in range(6)]
    up_sb = [sb(f"up{i}", [128, TC], BF16) for i in range(6)]
    act_sb = [sb(f"act{i}", [128, TC], BF16) for i in range(6)]
    ev_sb = [sb(f"ev{i}", [128, TC]) for i in range(2)]

    ps_h = [psa(f"ps_h{i}", [128, TC]) for i in range(4)]
    ps_dn = [psa(f"ps_dn{i}", [128, TC]) for i in range(2)]
    gsem = es.enter_context(nc.semaphore("gsem"))

    if True:
        pass
        t.step("gpsimd", lambda e: e.dma_start(out=mg_s[:], in_=i_mg), make=["mg"], dma=True)
        t.step("gpsimd", lambda e: e.dma_start(out=cid_s[:], in_=i_cid), make=["cid"], dma=True)
        t.step("gpsimd", lambda e: e.dma_start(out=cidf_s[:], in_=i_cidf), make=["cidf"], dma=True)
        t.step("gpsimd", lambda e: e.dma_start(out=de_s[:], in_=i_de), make=["de"], dma=True)
        t.step("gpsimd", lambda e: e.dma_start(out=eye_s[:], in_=i_eye), make=["eye"], dma=True)
        src = i_gwT.rearrange("(k p) e -> p k e", p=128)
        dst = gw_s[:].rearrange("p (k e) -> p k e", k=16)
        t.step("gpsimd", lambda e: e.dma_start(out=dst, in_=src), make=["gw"], dma=True)
        t.step("gpsimd", lambda e: e.memset(ones_row1[:], 1.0), make=["ones1"])
        t.step("gpsimd", lambda e: e.memset(ones_colb[:], 1.0), make=["onesc"])
        t.step("vector", lambda e: e.tensor_single_scalar(gsel[:], mg_s[:], cidf_s[0:1, 0:1], ALU.is_equal),
               use=["mg", "cidf"], make=["gsel"])
        t.step("tensor", lambda e: e.matmul(ps_h[0][0:128, 0:E], ones_row1[0:1, :], gsel[0:1, :], start=True, stop=True),
               use=["ones1", "gsel"], make=["ps_h0"])
        t.step("vector", lambda e: e.tensor_copy(gselB[:], ps_h[0][0:128, 0:E]),
               use=["ps_h0"], make=["gselB"])
        def gather(e):
            last = None
            with e.register("cidr") as cidr, e.register("der") as der:
                e.reg_load(cidr, cid_s[0:1, 0:1])
                for g in range(NG):
                    with e.If_eq(cidr, g):
                        e.reg_load(der, de_s[0:1, g:g+1])
                for ex in range(E):
                    with e.If_eq(der, ex):
                        for q in range(4):
                            e.dma_start(out=gusc[q*512:(q+1)*512, :], in_=i_gupT[ex, q*512:(q+1)*512, :]).then_inc(gsem, 16)
                        for q in range(2):
                            last = e.dma_start(out=dnsc[q*384:(q+1)*384, :], in_=i_dnT[ex, q*384:(q+1)*384, :]).then_inc(gsem, 16)
            return last
        t.step("gpsimd", gather, use=["cid", "de"], raw=True, tag="gather")
        t.step("gpsimd", lambda e: e.wait_ge(gsem, 96), raw=True, tag="gwait")

        for tc in range(NTC):
            for k in range(16):
                t.step("sync", lambda e, k=k, tc=tc: e.dma_start(out=xn_s[k][:], in_=i_xn[k*128:(k+1)*128, tc*TC:(tc+1)*TC]),
                       make=[f"xn{k}"], dma=True)
            for tt in range(4):
                for k in range(16):
                    t.step("tensor", lambda e, k=k, tt=tt: e.matmul(
                            ps_h[1][0:128, 0:E], xn_s[k][:, tt*128:(tt+1)*128],
                            gw_s[:].rearrange("p (k e) -> p k e", k=16)[:, k, :],
                            start=(k == 0), stop=(k == 15)),
                           use=[f"xn{k}", "gw"], make=["ps_h1"])
                t.step("vector", lambda e: e.reduce_max(prow[:], ps_h[1][0:128, 0:E], AX.X),
                       use=["ps_h1"], make=["prow"])
                t.step("vector", lambda e: e.tensor_scalar_mul(prow[:], prow[:], -1.0),
                       use=["prow"], make=["prow"])
                t.step("scalar", lambda e: e.activation(probs[:], ps_h[1][0:128, 0:E], AF.Exp, bias=prow[:, 0:1]),
                       use=["ps_h1", "prow"], make=["probs"])
                t.step("vector", lambda e: e.reduce_sum(psum_r[:], probs[:], AX.X),
                       use=["probs"], make=["psr"])
                t.step("vector", lambda e: e.reciprocal(psum_r[:], psum_r[:]), use=["psr"], make=["psr"])
                t.step("vector", lambda e: e.tensor_scalar_mul(probs[:], probs[:], psum_r[:, 0:1]),
                       use=["probs", "psr"], make=["probs"])
                t.step("vector", lambda e: e.tensor_copy(porig[:], probs[:]), use=["probs"], make=["porig"])
                t.step("gpsimd", lambda e: e.memset(scol[:], 0.0), make=["scol"])
                for it in range(8):
                    t.step("vector", lambda e: e.reduce_max(mcol[:], probs[:], AX.X),
                           use=["probs"], make=["mcol"])
                    t.step("vector", lambda e: e.tensor_add(scol[:], scol[:], mcol[:]),
                           use=["scol", "mcol"], make=["scol"])
                    t.step("vector", lambda e: e.tensor_scalar(maskb[:], probs[:], mcol[:, 0:1], None, ALU.is_ge),
                           use=["probs", "mcol"], make=["maskb"])
                    t.step("vector", lambda e: e.scalar_tensor_tensor(probs[:], maskb[:], 1e30, probs[:], ALU.mult, ALU.subtract),
                           use=["maskb", "probs"], make=["probs"]) if False else None
                    t.step("vector", lambda e: e.scalar_tensor_tensor(probs[:], maskb[:], -1e30, probs[:], ALU.mult, ALU.add),
                           use=["maskb", "probs"], make=["probs"])
                t.step("vector", lambda e: e.tensor_single_scalar(mask8[:], probs[:], -1e29, ALU.is_lt),
                       use=["probs"], make=["mask8"])
                t.step("vector", lambda e: e.reciprocal(rscol[:], scol[:]), use=["scol"], make=["rscol"])
                t.step("vector", lambda e: e.scalar_tensor_tensor(ptop[:], porig[:], rscol[:, 0:1], mask8[:], ALU.mult, ALU.mult),
                       use=["porig", "rscol", "mask8"], make=["ptop"])
                t.step("vector", lambda e: e.tensor_tensor_reduce(wcol[:], ptop[:], gselB[:], ALU.mult),
                       use=["ptop", "gselB"], make=["wcol"]) if hasattr(nc.vector, "tensor_tensor_reduce_XXX") else \
                t.step("vector", lambda e: e.tensor_mul(maskb[:], ptop[:], gselB[:]), use=["ptop", "gselB"], make=["maskb"])
                t.step("vector", lambda e: e.reduce_sum(wcol[:], maskb[:], AX.X), use=["maskb"], make=["wcol"])
                t.step("tensor", lambda e: e.transpose(ps_h[2][0:1, 0:128], wcol[:], eye_s[:]),
                       use=["wcol", "eye"], make=["ps_h2"])
                t.step("scalar", lambda e, tc=tc, tt=tt: e.activation(wrow[0:1, tc*TC+tt*128:tc*TC+(tt+1)*128], ps_h[2][0:1, 0:128], AF.Copy),
                       use=["ps_h2"], make=["wrow"])
            t.step("tensor", lambda e, tc=tc: e.matmul(ps_h[3][:], ones_row1[0:1, :], wrow[0:1, tc*TC:(tc+1)*TC], start=True, stop=True),
                   use=["ones1", "wrow"], make=["ps_h3"])
            for og in range(6):
                for k in range(16):
                    wsrc = gusc.ap()[k*128:(k+1)*128, og*256:(og+1)*256]
                    t.step("gpsimd", lambda e, wsrc=wsrc, og=og: e.dma_start(out=wbuf[og % 2][:, 0:256], in_=wsrc),
                           make=[f"wb{og%2}"], dma=True)
                    for to in range(2):
                        ot = og * 2 + to
                        t.step("tensor", lambda e, k=k, to=to, og=og: e.matmul(
                                ps_h[to][:], wbuf[og % 2][:, to*128:(to+1)*128],
                                xn_s[k][:], start=(k == 0), stop=(k == 15)),
                               use=[f"wb{og%2}", f"xn{k}"], make=[f"ps_h{to}"])
                for to in range(2):
                    ot = og * 2 + to
                    if ot < 6:
                        t.step("scalar", lambda e, ot=ot, to=to: e.activation(sg_sb[ot][:], ps_h[to][:], AF.Silu),
                               use=[f"ps_h{to}"], make=[f"sg{ot}"])
                    else:
                        t.step("vector", lambda e, ot=ot, to=to: e.tensor_copy(up_sb[ot-6][:], ps_h[to][:]),
                               use=[f"ps_h{to}"], make=[f"up{ot-6}"])
            for u in range(6):
                t.step("vector", lambda e, u=u: e.tensor_mul(act_sb[u][:], sg_sb[u][:], up_sb[u][:]),
                       use=[f"sg{u}", f"up{u}"], make=[f"act{u}"])
                t.step("vector", lambda e, u=u: e.tensor_mul(act_sb[u][:], act_sb[u][:], ps_h[3][:]),
                       use=[f"act{u}", "ps_h3"], make=[f"act{u}"])
            for og in range(8):
                for k in range(6):
                    wsrc = dnsc.ap()[k*128:(k+1)*128, og*256:(og+1)*256]
                    t.step("gpsimd", lambda e, wsrc=wsrc, og=og: e.dma_start(out=wbuf[og % 2][:, 0:256], in_=wsrc),
                           make=[f"wb{og%2}"], dma=True)
                    for to in range(2):
                        t.step("tensor", lambda e, k=k, to=to, og=og: e.matmul(
                                ps_dn[to][:], wbuf[og % 2][:, to*128:(to+1)*128],
                                act_sb[k][:], start=(k == 0), stop=(k == 5)),
                               use=[f"wb{og%2}", f"act{k}"], make=[f"ps_dn{to}"])
                for to in range(2):
                    ot = og * 2 + to
                    t.step("vector", lambda e, to=to: e.tensor_copy(ev_sb[to][:], ps_dn[to][:]),
                           use=[f"ps_dn{to}"], make=[f"ev{to}"])
                    t.step("sync", lambda e, ot=ot, tc=tc, to=to: e.dma_start(
                            out=o_contrib[ot*128:(ot+1)*128, tc*TC:(tc+1)*TC], in_=ev_sb[to][:]),
                           use=[f"ev{to}"], dma=True)
    es.close()





"""Launch 6: x1f = x1 + sum(contribs); LN -> xn1; Q/K (RoPE) feature-major; V token-major."""


TOK, D, KT = 1024, 2048, 16

def emit_launch6(fctx):
    nc, t, IO = fctx
    i_x1 = IO["x1own"]
    i_cn = IO["contribs"]
    i_wq = IO["WqT"]
    i_wk = IO["WkT"]
    i_wv = IO["WvT"]
    i_cos = IO["cosm"]
    i_sin = IO["sinm"]
    o_x1f = IO["o_x1f"]
    o_xn1 = IO["o_xn1"]
    o_q = IO["o_q"]
    o_k = IO["o_k"]
    o_v = IO["o_v"]

    es = ExitStack()
    def sb(name, shp, dt=F32): return es.enter_context(nc.sbuf_tensor("L6_" + name, shp, dt))
    def psa(name, shp, dt=F32): return es.enter_context(nc.psum_tensor("L6_" + name, shp, dt))
    x1_s = [sb(f"x1_{k}", [128, TOK]) for k in range(KT)]
    xb_s = [sb(f"xb{k}", [128, TOK], BF16) for k in range(KT)]
    cbuf = [sb(f"cbuf{i}", [128, TOK]) for i in range(2)]
    x2_s = [sb(f"x2_{i}", [128, TOK], BF16) for i in range(2)]
    tmpa = sb("tmpa", [128, TOK]); tmpb = sb("tmpb", [128, TOK])
    ones_col = sb("ones_col", [128, 1], BF16); ones_row1 = sb("ones_row1", [1, 128])
    statm = sb("statm", [1, TOK]); stat2 = sb("stat2", [1, TOK])
    statv = sb("statv", [1, TOK]); statr = sb("statr", [1, TOK])
    eps1 = sb("eps1", [1, 1])
    cos_s = sb("cos_s", [128, TOK]); sin_s = sb("sin_s", [128, TOK])
    wbuf = [sb(f"wbuf{i}", [128, 256], BF16) for i in range(2)]
    ra = sb("ra", [64, 512]); rb = sb("rb", [64, 512])
    qbf = sb("qbf", [128, TOK], BF16)
    vbf = [sb(f"vbf{i}", [128, 512], BF16) for i in range(2)]

    ps_a = psa("ps_a", [128, TOK]); ps_b = psa("ps_b", [128, TOK])
    ps_p = [psa("ps_p0", [128, TOK])]
    ps_v = psa("ps_v", [128, 512])

    if True:
        pass
        t.step("gpsimd", lambda e: e.dma_start(out=cos_s[:], in_=i_cos), make=["cos"], dma=True)
        t.step("gpsimd", lambda e: e.dma_start(out=sin_s[:], in_=i_sin), make=["sin"], dma=True)
        t.step("gpsimd", lambda e: e.memset(ones_col[:], 1.0), make=["onesc"])
        t.step("gpsimd", lambda e: e.memset(ones_row1[:], 1.0), make=["ones1"])
        t.step("gpsimd", lambda e: e.memset(eps1[:], 1e-5), make=["eps1"])
        for k in range(KT):
            t.step("sync", lambda e, k=k: e.dma_start(out=x1_s[k][:], in_=i_x1[k*128:(k+1)*128, :]),
                   make=[f"x1_{k}"], dma=True)
            for g in range(1):
                cb = cbuf[g % 2]
                t.step("sync", lambda e, k=k, g=g, cb=cb: e.dma_start(out=cb[:], in_=i_cn[g, k*128:(k+1)*128, :]),
                       make=[f"cb{g%2}"], dma=True)
                t.step("vector", lambda e, k=k, cb=cb, g=g: e.tensor_add(x1_s[k][:], x1_s[k][:], cb[:]),
                       use=[f"x1_{k}", f"cb{g%2}"], make=[f"x1_{k}"])
            t.step("sync", lambda e, k=k: e.dma_start(out=o_x1f[k*128:(k+1)*128, :], in_=x1_s[k][:]),
                   use=[f"x1_{k}"], dma=True)
        for k in range(KT):
            t.step("vector", lambda e, k=k: e.tensor_copy(xb_s[k][:], x1_s[k][:]),
                   use=[f"x1_{k}"], make=[f"xb{k}"])
            xi = k % 2
            t.step("scalar", lambda e, k=k, xi=xi: e.activation(x2_s[xi][:], xb_s[k][:], AF.Square),
                   use=[f"xb{k}"], make=[f"x2_{xi}"])
            for (o, n) in [(0, 512), (512, 512)]:
                t.step("tensor", lambda e, k=k, o=o, n=n: e.matmul(
                        ps_a[0:1, o:o+n], ones_col[:, 0:1], xb_s[k][:, o:o+n],
                        start=(k == 0), stop=(k == KT-1)),
                       use=["onesc", f"xb{k}"], make=["ps_a"])
                t.step("tensor", lambda e, k=k, o=o, n=n, xi=xi: e.matmul(
                        ps_b[0:1, o:o+n], ones_col[:, 0:1], x2_s[xi][:, o:o+n],
                        start=(k == 0), stop=(k == KT-1)),
                       use=["onesc", f"x2_{xi}"], make=["ps_b"])
        t.step("scalar", lambda e: e.activation(statm[:], ps_a[0:1, :], AF.Copy, scale=1.0/D),
               use=["ps_a"], make=["statm"])
        t.step("scalar", lambda e: e.activation(stat2[:], statm[:], AF.Square),
               use=["statm"], make=["stat2"])
        t.step("vector", lambda e: e.scalar_tensor_tensor(statv[:], ps_b[0:1, :], 1.0/D, stat2[:], ALU.mult, ALU.subtract),
               use=["ps_b", "stat2"], make=["statv"])
        t.step("scalar", lambda e: e.activation(statr[:], statv[:], AF.Sqrt, bias=eps1[0:1, 0:1]),
               use=["statv", "eps1"], make=["statr"])
        t.step("vector", lambda e: e.reciprocal(statr[:], statr[:]), use=["statr"], make=["statr"])
        for (o, n) in [(0, 512), (512, 512)]:
            t.step("tensor", lambda e, o=o, n=n: e.matmul(ps_a[:, o:o+n], ones_row1[0:1, :], statm[0:1, o:o+n], start=True, stop=True),
                   use=["ones1", "statm"], make=["ps_a"])
            t.step("tensor", lambda e, o=o, n=n: e.matmul(ps_b[:, o:o+n], ones_row1[0:1, :], statr[0:1, o:o+n], start=True, stop=True),
                   use=["ones1", "statr"], make=["ps_b"])
        for k in range(KT):
            tb = [tmpa, tmpb][k % 2]; tbn = f"tmp{k%2}"
            t.step("vector", lambda e, k=k, tb=tb: e.tensor_sub(tb[:], x1_s[k][:], ps_a[:]),
                   use=[f"x1_{k}", "ps_a"], make=[tbn])
            t.step("vector", lambda e, k=k, tb=tb: e.tensor_mul(xb_s[k][:], tb[:], ps_b[:]),
                   use=[tbn, "ps_b"], make=[f"xb{k}"])
            t.step("sync", lambda e, k=k: e.dma_start(out=o_xn1[k*128:(k+1)*128, :], in_=xb_s[k][:]),
                   use=[f"xb{k}"], dma=True)
        def proj_rope(wT, nrows, out_ap, name):
            nt = nrows // 128
            for ot in range(nt):
                pi = 0
                for k in range(KT):
                    t.step("gpsimd", lambda e, k=k, ot=ot, pi=pi, wT=wT: e.dma_start(
                            out=wbuf[pi][:, 0:128], in_=wT[k*128:(k+1)*128, ot*128:(ot+1)*128]),
                           make=[f"wb{pi}"], dma=True)
                    for (o, n) in [(0, 512), (512, 512)]:
                        t.step("tensor", lambda e, k=k, o=o, n=n, pi=pi: e.matmul(
                                ps_p[pi][:, o:o+n], wbuf[pi][:, 0:128], xb_s[k][:, o:o+n],
                                start=(k == 0), stop=(k == KT-1)),
                               use=[f"wb{pi}", f"xb{k}"], make=[f"ps_p{pi}"])
                for half in range(2):
                    hb, ob = (0, 64) if half else (64, 0)
                    ob = half * 64; hb = 64 - ob
                    for (o, n) in [(0, 512), (512, 512)]:
                        t.step("vector", lambda e, ob=ob, o=o, n=n, pi=pi: e.tensor_mul(
                                ra[:, 0:n], ps_p[pi][ob:ob+64, o:o+n], cos_s[ob:ob+64, o:o+n]),
                               use=[f"ps_p{pi}", "cos"], make=["ra"])
                        t.step("vector", lambda e, ob=ob, hb=hb, o=o, n=n, pi=pi: e.tensor_mul(
                                rb[:, 0:n], ps_p[pi][hb:hb+64, o:o+n], sin_s[ob:ob+64, o:o+n]),
                               use=[f"ps_p{pi}", "sin"], make=["rb"])
                        op = ALU.subtract if half == 0 else ALU.add
                        t.step("vector", lambda e, ob=ob, o=o, n=n, op=op: e.tensor_tensor(
                                qbf[ob:ob+64, o:o+n], ra[:, 0:n], rb[:, 0:n], op),
                               use=["ra", "rb"], make=["qbf"])
                t.step("sync", lambda e, ot=ot, out_ap=out_ap: e.dma_start(
                        out=out_ap[ot*128:(ot+1)*128, :], in_=qbf[:]),
                       use=["qbf"], dma=True)
        proj_rope(i_wq, 2048, o_q, "q")
        proj_rope(i_wk, 512, o_k, "k")
        for tt in range(8):
            for k in range(KT):
                t.step("gpsimd", lambda e, k=k: e.dma_start(out=wbuf[0][:, 0:256], in_=i_wv[k*128:(k+1)*128, 0:256]),
                       make=["wb0"], dma=True)
                t.step("gpsimd", lambda e, k=k: e.dma_start(out=wbuf[1][:, 0:256], in_=i_wv[k*128:(k+1)*128, 256:512]),
                       make=["wb1"], dma=True)
                t.step("tensor", lambda e, k=k, tt=tt: e.matmul(
                        ps_v[:, 0:256], xb_s[k][:, tt*128:(tt+1)*128], wbuf[0][:, 0:256],
                        start=(k == 0), stop=False),
                       use=["wb0", f"xb{k}"], make=["ps_v"])
                t.step("tensor", lambda e, k=k, tt=tt: e.matmul(
                        ps_v[:, 256:512], xb_s[k][:, tt*128:(tt+1)*128], wbuf[1][:, 0:256],
                        start=False, stop=(k == KT-1)),
                       use=["wb1", f"xb{k}"], make=["ps_v"])
            vb = vbf[tt % 2]
            t.step("vector", lambda e, vb=vb: e.tensor_copy(vb[:], ps_v[:]),
                   use=["ps_v"], make=[f"vb{tt%2}"])
            t.step("sync", lambda e, tt=tt, vb=vb: e.dma_start(out=o_v[tt*128:(tt+1)*128, :], in_=vb[:]),
                   use=[f"vb{tt%2}"], dma=True)
    es.close()





"""Launch 7: attention (flash, bound-stabilized, full-key-range uniform w/ mask) + Wo."""


TOK, D, S, H, KVH, DH = 1024, 2048, 8192, 16, 4, 128
QC = 512
NKT = S // 128
SCALE = float(DH ** -0.5)

def emit_launch7(fctx):
    nc, t, IO = fctx
    i_q = IO["qown"]
    i_k = IO["Kf"]
    i_v = IO["Vtm"]
    i_mask = IO["maskf"]
    i_wo = IO["WoT"]
    attnraw = nc.dram_tensor("attnraw", [2048, TOK], BF16)
    o_attnp = IO["o_attnp"]

    es = ExitStack()
    def sb(name, shp, dt=F32): return es.enter_context(nc.sbuf_tensor("L7_" + name, shp, dt))
    def psa(name, shp, dt=F32): return es.enter_context(nc.psum_tensor("L7_" + name, shp, dt))
    kbuf = [sb("kbuf0", [128, S], BF16)]
    kch = sb("kch", [128, 512], BF16)
    V_s = [sb(f"V{i}", [128, 512], BF16) for i in range(NKT)]
    q_s = sb("q_s", [128, QC * H], BF16)
    mk_s = [sb(f"mk{i}", [128, QC], BF16) for i in range(NKT)]
    sq_b = sb("sq_b", [128, 1024], BF16)
    nrow = sb("nrow", [1, 512])
    ntmp = sb("ntmp", [1, 1])
    one1 = sb("one1", [1, 1]); sc1 = sb("sc1", [1, 1])
    ones_col = sb("ones_col", [128, 1], BF16); ones_row1 = sb("ones_row1", [1, 128])
    negm = [sb(f"negm{i}", [128, 1]) for i in range(H)]
    qmax2 = sb("qmax2", [1, H]); kmax2 = sb("kmax2", [1, KVH])
    mh = sb("mh", [1, 1])
    Pb = sb("Pb", [128, QC], BF16)
    Pm = [sb(f"Pm{i}", [128, QC], BF16) for i in range(2)]
    srow = sb("srow", [1, QC])
    oev = sb("oev", [128, QC])
    abf = sb("abf", [128, QC], BF16)
    att_s = [sb(f"att{i}", [128, TOK], BF16) for i in range(2)]
    wbuf = [sb(f"wbuf{i}", [128, 128], BF16) for i in range(2)]
    evp = [sb(f"evp{i}", [128, 512]) for i in range(2)]

    ps_s = [psa(f"ps_s{i}", [128, QC]) for i in range(2)]
    ps_o = psa("ps_o", [128, QC])
    ps_sum = psa("ps_sum", [1, QC])
    ps_bc = psa("ps_bc", [128, QC])
    ps_n = psa("ps_n", [1, 512])
    ps_w = [psa(f"ps_w{i}", [128, 512]) for i in range(2)]

    if True:
        pass
        for i in range(NKT):
            t.step("sync", lambda e, i=i: e.dma_start(out=V_s[i][:], in_=i_v[i*128:(i+1)*128, :]),
                   make=[f"V{i}"], dma=True)
        t.step("gpsimd", lambda e: e.memset(ones_col[:], 1.0), make=["onesc"])
        t.step("gpsimd", lambda e: e.memset(ones_row1[:], 1.0), make=["ones1"])
        for kv in range(KVH):
            for qch in range(S // 512):
                t.step("sync", lambda e, kv=kv, qch=qch: e.dma_start(
                        out=kch[:], in_=i_k[kv*128:(kv+1)*128, qch*512:(qch+1)*512]),
                       make=["kch"], dma=True)
                t.step("scalar", lambda e: e.activation(sq_b[:, 0:512], kch[:], AF.Square),
                       use=["kch"], make=["sqb"])
                t.step("tensor", lambda e, qch=qch: e.matmul(ps_n[:], ones_col[:, 0:1],
                        sq_b[:, 0:512], start=True, stop=True),
                       use=["onesc", "sqb"], make=["ps_n"])
                t.step("scalar", lambda e, qch=qch: e.activation(nrow[0:1, 0:512], ps_n[:], AF.Copy),
                       use=["ps_n"], make=["nrow"])
                t.step("vector", lambda e: e.reduce_max(ntmp[:], nrow[:], AX.X),
                       use=["nrow"], make=["ntmp"])
                if qch == 0:
                    t.step("vector", lambda e, kv=kv: e.tensor_copy(kmax2[0:1, kv:kv+1], ntmp[:]),
                           use=["ntmp"], make=["kmax2"])
                else:
                    t.step("vector", lambda e, kv=kv: e.tensor_max(kmax2[0:1, kv:kv+1], kmax2[0:1, kv:kv+1], ntmp[:]),
                           use=["ntmp", "kmax2"], make=["kmax2"])
        for h in range(H):
            t.step("gpsimd", lambda e, h=h: e.dma_start(out=sq_b[:, 0:TOK], in_=i_q[h*128:(h+1)*128, :]),
                   make=["sqb"], dma=True)
            t.step("scalar", lambda e: e.activation(sq_b[:, 0:TOK], sq_b[:, 0:TOK], AF.Square),
                   use=["sqb"], make=["sqb"])
            for qch in range(2):
                t.step("tensor", lambda e, qch=qch: e.matmul(ps_n[:], ones_col[:, 0:1],
                        sq_b[:, qch*512:(qch+1)*512], start=True, stop=True),
                       use=["onesc", "sqb"], make=["ps_n"])
                t.step("scalar", lambda e, qch=qch: e.activation(nrow[0:1, 0:512], ps_n[:], AF.Copy),
                       use=["ps_n"], make=["nrow"])
                t.step("vector", lambda e: e.reduce_max(ntmp[:], nrow[:], AX.X),
                       use=["nrow"], make=["ntmp"])
                if qch == 0:
                    t.step("vector", lambda e, h=h: e.tensor_copy(qmax2[0:1, h:h+1], ntmp[:]),
                           use=["ntmp"], make=["qmax2"])
                else:
                    t.step("vector", lambda e, h=h: e.tensor_max(qmax2[0:1, h:h+1], qmax2[0:1, h:h+1], ntmp[:]),
                           use=["ntmp", "qmax2"], make=["qmax2"])
            kv = h // 4
            t.step("vector", lambda e, h=h, kv=kv: e.tensor_mul(mh[:], qmax2[0:1, h:h+1], kmax2[0:1, kv:kv+1]),
                   use=["qmax2", "kmax2"], make=["mh"])
            t.step("scalar", lambda e: e.activation(mh[:], mh[:], AF.Sqrt), use=["mh"], make=["mh"])
            t.step("vector", lambda e: e.tensor_scalar_mul(mh[:], mh[:], -SCALE), use=["mh"], make=["mh"])
            t.step("tensor", lambda e: e.matmul(ps_bc[:, 0:1], ones_row1[0:1, :], mh[0:1, 0:1], start=True, stop=True),
                   use=["ones1", "mh"], make=["ps_bc"])
            t.step("vector", lambda e, h=h: e.tensor_copy(negm[h][:], ps_bc[:, 0:1]),
                   use=["ps_bc"], make=[f"negm{h}"])
        for qc in range(TOK // QC):
            t.step("gpsimd", lambda e, qc=qc: e.dma_start(
                    out=q_s[:].rearrange("p (h q) -> p h q", h=H),
                    in_=i_q.rearrange("(h p) t -> p h t", p=128)[:, :, qc*QC:(qc+1)*QC]),
                   make=["q_s"], dma=True)
            for i in range(NKT):
                t.step("sync", lambda e, i=i, qc=qc: e.dma_start(out=mk_s[i][:], in_=i_mask[i*128:(i+1)*128, qc*QC:(qc+1)*QC]),
                       make=[f"mk{i}"], dma=True)
            for kv in range(KVH):
              kb = kbuf[0]
              for rr in range(8):
                  t.step("sync", lambda e, kv=kv, kb=kb, rr=rr: e.dma_start(
                          out=kb[:, rr*1024:(rr+1)*1024], in_=i_k[kv*128:(kv+1)*128, rr*1024:(rr+1)*1024]),
                         make=["kb0"], dma=True)
              for h in range(4*kv, 4*kv+4):
                for kt in range(NKT):
                    si = kt % 2
                    t.step("tensor", lambda e, kt=kt, h=h, kv=kv, si=si, qc=qc, kb=kb: e.matmul(
                            ps_s[si][:], kb[:, kt*128:(kt+1)*128],
                            q_s[:].rearrange("p (h q) -> p h q", h=H)[:, h, :],
                            start=True, stop=True),
                           use=["kb0", "q_s"], make=[f"ps_s{si}"])
                    t.step("scalar", lambda e, si=si, h=h: e.activation(
                            Pb[:], ps_s[si][:], AF.Exp, bias=negm[h][:, 0:1], scale=SCALE),
                           use=[f"ps_s{si}", f"negm{h}"], make=["Pb"])
                    t.step("vector", lambda e, kt=kt, si=si: e.tensor_mul(Pm[si][:], Pb[:], mk_s[kt][:]),
                           use=["Pb", f"mk{kt}"], make=[f"Pm{si}"])
                    t.step("tensor", lambda e, kt=kt, si=si: e.matmul(
                            ps_sum[:], ones_col[:, 0:1], Pm[si][:],
                            start=(kt == 0), stop=(kt == NKT-1)),
                           use=["onesc", f"Pm{si}"], make=["ps_sum"])
                    t.step("tensor", lambda e, kt=kt, kv=kv, si=si: e.matmul(
                            ps_o[:], V_s[kt][:, kv*128:(kv+1)*128], Pm[si][:],
                            start=(kt == 0), stop=(kt == NKT-1)),
                           use=[f"V{kt}", f"Pm{si}"], make=["ps_o"])
                t.step("scalar", lambda e: e.activation(srow[:], ps_sum[:], AF.Copy),
                       use=["ps_sum"], make=["srow"])
                t.step("vector", lambda e: e.reciprocal(srow[:], srow[:]), use=["srow"], make=["srow"])
                t.step("tensor", lambda e: e.matmul(ps_bc[:], ones_row1[0:1, :], srow[0:1, :], start=True, stop=True),
                       use=["ones1", "srow"], make=["ps_bc"])
                t.step("vector", lambda e: e.tensor_copy(oev[:], ps_o[:]), use=["ps_o"], make=["oev"])
                t.step("vector", lambda e: e.tensor_mul(abf[:], oev[:], ps_bc[:]),
                       use=["oev", "ps_bc"], make=["abf"])
                t.step("sync", lambda e, h=h, qc=qc: e.dma_start(
                        out=attnraw.ap()[h*128:(h+1)*128, qc*QC:(qc+1)*QC], in_=abf[:]),
                       use=["abf"], dma=True)
        for ot in range(16):
            for k in range(16):
                ai = k % 2
                t.step("sync", lambda e, k=k, ai=ai: e.dma_start(out=att_s[ai][:], in_=attnraw.ap()[k*128:(k+1)*128, :]),
                       make=[f"att{ai}"], dma=True)
                t.step("gpsimd", lambda e, k=k, ot=ot: e.dma_start(
                        out=wbuf[k % 2][:], in_=i_wo[k*128:(k+1)*128, ot*128:(ot+1)*128]),
                       make=[f"wbuf{k%2}"], dma=True)
                for ni, (o, n) in enumerate([(0, 512), (512, 512)]):
                    t.step("tensor", lambda e, k=k, o=o, n=n, ni=ni, ai=ai: e.matmul(
                            ps_w[ni][:], wbuf[k % 2][:], att_s[ai][:, o:o+n],
                            start=(k == 0), stop=(k == 15)),
                           use=[f"wbuf{k%2}", f"att{ai}"], make=[f"ps_w{ni}"])
            for ni, (o, n) in enumerate([(0, 512), (512, 512)]):
                t.step("vector", lambda e, ni=ni: e.tensor_copy(evp[ni][:], ps_w[ni][:]),
                       use=[f"ps_w{ni}"], make=[f"evp{ni}"])
                t.step("sync", lambda e, ot=ot, o=o, n=n, ni=ni: e.dma_start(
                        out=o_attnp[ot*128:(ot+1)*128, o:o+n], in_=evp[ni][:]),
                       use=[f"evp{ni}"], dma=True)
    es.close()



"""Launch 8: L1 FFN (W0/gelu-exact/W1 with f16 rounds) + final residual sum."""


TOK, D, N = 1024, 2048, 8192

def emit_launch8(fctx):
    nc, t, IO = fctx
    i_xn1 = IO["xn1"]
    i_w0 = IO["W0T"]
    i_b0 = IO["b0c"]
    i_w1 = IO["W1T"]
    i_b1 = IO["b1c"]
    i_x1f = IO["x1f"]
    i_att = IO["attnp"]
    o_out = IO["o_out"]

    es = ExitStack()
    def sb(name, shp, dt=F32): return es.enter_context(nc.sbuf_tensor("L8_" + name, shp, dt))
    def psa(name, shp, dt=F32): return es.enter_context(nc.psum_tensor("L8_" + name, shp, dt))
    xb_s = [sb(f"xb{k}", [128, TOK], BF16) for k in range(16)]
    h1_s = [sb(f"h1_{i}", [128, TOK], BF16) for i in range(64)]
    h16 = [sb(f"h16_{i}", [128, TOK], F16) for i in range(2)]
    b0_s = sb("b0_s", [128, 64]); b1_s = sb("b1_s", [128, 16])
    wbuf = [sb(f"wbuf{i}", [128, 128], BF16) for i in range(2)]
    e16 = [sb(f"e16_{i}", [128, TOK], F16) for i in range(2)]
    resid = [sb(f"res{i}", [128, TOK]) for i in range(2)]
    rb = [sb(f"rb{i}", [128, TOK]) for i in range(2)]
    outb = [sb(f"outb{i}", [128, TOK]) for i in range(2)]
    ps_p = [psa(f"ps_p{i}", [128, TOK]) for i in range(2)]

    if True:
        pass
        t.step("gpsimd", lambda e: e.dma_start(out=b0_s[:], in_=i_b0), make=["b0"], dma=True)
        t.step("gpsimd", lambda e: e.dma_start(out=b1_s[:], in_=i_b1), make=["b1"], dma=True)
        for k in range(16):
            t.step("sync", lambda e, k=k: e.dma_start(out=xb_s[k][:], in_=i_xn1[k*128:(k+1)*128, :]),
                   make=[f"xb{k}"], dma=True)
        for ot in range(64):
            pi = ot % 2
            for k in range(16):
                t.step("gpsimd", lambda e, k=k, ot=ot, pi=pi: e.dma_start(
                        out=wbuf[pi][:], in_=i_w0[k*128:(k+1)*128, ot*128:(ot+1)*128]),
                       make=[f"wb{pi}"], dma=True)
                for (o, n) in [(0, 512), (512, 512)]:
                    t.step("tensor", lambda e, k=k, o=o, n=n, pi=pi: e.matmul(
                            ps_p[pi][:, o:o+n], wbuf[pi][:], xb_s[k][:, o:o+n],
                            start=(k == 0), stop=(k == 15)),
                           use=[f"wb{pi}", f"xb{k}"], make=[f"ps_p{pi}"])
            bc = ot // 4; br = (ot % 4) * 32
            t.step("scalar", lambda e, ot=ot, pi=pi: e.activation(
                    h16[pi][:], ps_p[pi][:], AF.Gelu, bias=b0_s[:, ot:ot+1]),
                   use=[f"ps_p{pi}", "b0"], make=[f"h16_{pi}"])
            t.step("vector", lambda e, ot=ot, pi=pi: e.tensor_copy(h1_s[ot][:], h16[pi][:]),
                   use=[f"h16_{pi}"], make=[f"h1_{ot}"])
        for ot in range(16):
            pi = ot % 2
            for k in range(64):
                t.step("gpsimd", lambda e, k=k, ot=ot, pi=pi: e.dma_start(
                        out=wbuf[pi][:], in_=i_w1[k*128:(k+1)*128, ot*128:(ot+1)*128]),
                       make=[f"wb{pi}"], dma=True)
                for (o, n) in [(0, 512), (512, 512)]:
                    t.step("tensor", lambda e, k=k, o=o, n=n, pi=pi: e.matmul(
                            ps_p[pi][:, o:o+n], wbuf[pi][:], h1_s[k][:, o:o+n],
                            start=(k == 0), stop=(k == 63)),
                           use=[f"wb{pi}", f"h1_{k}"], make=[f"ps_p{pi}"])
            t.step("vector", lambda e, ot=ot, pi=pi: e.tensor_scalar_add(
                    e16[pi][:], ps_p[pi][:], b1_s[:, ot:ot+1]),
                   use=[f"ps_p{pi}", "b1"], make=[f"e16_{pi}"])
            t.step("sync", lambda e, ot=ot, pi=pi: e.dma_start(out=resid[pi][:], in_=i_x1f[ot*128:(ot+1)*128, :]),
                   make=[f"res{pi}"], dma=True)
            t.step("sync", lambda e, ot=ot, pi=pi: e.dma_start(out=rb[pi][:], in_=i_att[ot*128:(ot+1)*128, :]),
                   make=[f"rb{pi}"], dma=True)
            t.step("vector", lambda e, pi=pi: e.tensor_add(resid[pi][:], resid[pi][:], rb[pi][:]),
                   use=[f"res{pi}", f"rb{pi}"], make=[f"res{pi}"])
            t.step("vector", lambda e, pi=pi: e.tensor_add(outb[pi][:], resid[pi][:], e16[pi][:]),
                   use=[f"res{pi}", f"e16_{pi}"], make=[f"out{pi}"])
            t.step("sync", lambda e, ot=ot, pi=pi: e.dma_start(out=o_out[ot*128:(ot+1)*128, :], in_=outb[pi][:]),
                   use=[f"out{pi}"], dma=True)
    es.close()





class BlockedCols:
    def __init__(self, ap3, B):
        self.ap3 = ap3
        self.B = B
    def __getitem__(self, idx):
        rs, cs = idx
        r = cs.start // self.B
        assert cs.stop <= (r + 1) * self.B, (cs, self.B)
        return self.ap3[rs, r, cs.start - r * self.B: cs.stop - r * self.B]


def emit_l3_fused(fctx):
    nc, t, IO = fctx
    i_ccout = IO["ccst_out"]
    i_ccin = IO["ccst_in"]
    i_flags = IO["flags"]
    i_C = IO["Cm"]; i_acum = IO["acum"]; i_ypart = IO["ypart"]; i_gate = IO["gate"]
    i_normw = IO["normw"]; i_opwT = IO["opwT"]; i_xT = IO["xTown"]; i_rep = IO["repmat"]
    o_x1 = IO["o_x1"]
    TOK, CH, NCH, NH, INTER, D = 1024, 256, 4, 64, 4096, 2048

    es = ExitStack()
    def sb(name, shp, dt=F32): return es.enter_context(nc.sbuf_tensor(name, shp, dt))
    def psa(name, shp, dt=F32): return es.enter_context(nc.psum_tensor(name, shp, dt))
    P = sb("f3P", [128, INTER])
    stbuf = [sb(f"f3st{i}", [128, INTER]) for i in range(2)]
    Tcol = sb("f3Tcol", [64, 1]); eT = sb("f3eT", [64, 1])
    flags_s = sb("f3flags", [1, 32]); flagB = sb("f3flagB", [128, 32])
    repf_s = sb("f3rep", [64, INTER])
    reprow = sb("f3reprow", [1, INTER])
    C_s = sb("f3C", [128, TOK], BF16)
    acum_s = sb("f3acum", [64, TOK]); eacum_s = sb("f3eacum", [64, TOK])
    erow = sb("f3erow", [1, TOK])
    ones_row1 = sb("f3ones1", [1, 128])
    ones_colb = sb("f3onesc", [128, 1], BF16)
    Csc = [sb(f"f3Csc{i}", [128, CH], BF16) for i in range(2)]
    Pbf = [sb(f"f3Pbf{i}", [128, 64], BF16) for i in range(2)]
    ych = [sb(f"f3ych{i}", [128, CH]) for i in range(32)]
    ypbuf = [sb(f"f3ypb{i}", [128, CH]) for i in range(2)]
    gbuf = [sb(f"f3gb{i}", [128, CH], BF16) for i in range(2)]
    sgb = [sb(f"f3sg{i}", [128, CH]) for i in range(2)]
    x2b = [sb(f"f3x2{i}", [128, CH], BF16) for i in range(2)]
    ynb = [sb(f"f3yn{i}", [128, CH], BF16) for i in range(32)]
    rmsrow = sb("f3rms", [1, CH]); eps1 = sb("f3eps", [1, 1])
    normw_s = sb("f3nw", [128, 32])
    wbuf = [sb(f"f3wb{i}", [128, 128], BF16) for i in range(2)]
    xtb = [sb(f"f3xt{i}", [128, CH]) for i in range(2)]
    x1b = [sb(f"f3x1{i}", [128, CH]) for i in range(2)]
    Ptmp = sb("f3Ptmp", [128, 512])
    ps_rep = psa("f3ps_rep", [1, 512])
    ps_bc = psa("f3ps_bc", [128, 512])
    ps_yo = [psa(f"f3ps_yo{i}", [64, CH]) for i in range(2)]
    ps_rms = psa("f3ps_rms", [1, CH])
    ps_bcr = psa("f3ps_bcr", [128, CH])
    ps_o = [psa(f"f3ps_o{i}", [128, CH]) for i in range(2)]

    t.step("gpsimd", lambda e: e.dma_start(out=flags_s[:], in_=i_flags), make=["f3flags"], dma=True)
    t.step("gpsimd", lambda e: e.dma_start(out=C_s[:], in_=i_C), make=["C"], dma=True)
    t.step("gpsimd", lambda e: e.dma_start(out=acum_s[:], in_=i_acum), make=["acum"], dma=True)
    t.step("gpsimd", lambda e: e.dma_start(out=repf_s[:], in_=i_rep), make=["rep"], dma=True)
    t.step("gpsimd", lambda e: e.dma_start(out=normw_s[:], in_=i_normw), make=["normw"], dma=True)
    t.step("gpsimd", lambda e: e.memset(ones_row1[:], 1.0), make=["ones1"])
    t.step("gpsimd", lambda e: e.memset(ones_colb[:], 1.0), make=["onesc"])
    t.step("gpsimd", lambda e: e.memset(eps1[:], 1e-5), make=["eps1"])
    t.step("gpsimd", lambda e: e.memset(P[:], 0.0), make=["P"])
    t.step("scalar", lambda e: e.activation(eacum_s[:], acum_s[:], AF.Exp), use=["acum"], make=["eacum"])
    t.step("tensor", lambda e: e.matmul(ps_bc[:, 0:32], ones_row1[0:1, :], flags_s[0:1, :], start=True, stop=True),
           use=["ones1", "f3flags"], make=["ps_bc"])
    t.step("vector", lambda e: e.tensor_copy(flagB[:], ps_bc[:, 0:32]), use=["ps_bc"], make=["flagB"])

    def p_update(j, own_k=None):
        sbn = f"f3s{j%2}"; stb = stbuf[j % 2]
        if own_k is None:
            r, kk = j // 4, j % 4
            t.step("sync", lambda e, r=r, kk=kk, stb=stb: e.dma_start(
                    out=stb[:], in_=i_ccout[r*576 + kk*128 : r*576 + (kk+1)*128, :]), make=[sbn], dma=True)
            t.step("sync", lambda e, r=r, kk=kk: e.dma_start(
                    out=Tcol[:], in_=i_ccout[r*576 + 512 : r*576 + 576, kk:kk+1], allow_slow_non_contiguous=True), make=["Tcol"], dma=True)
            t.step("vector", lambda e, j=j: e.tensor_scalar_mul(Tcol[:], Tcol[:], flagB[0:64, j:j+1]),
                   use=["Tcol", "flagB"], make=["Tcol"])
        else:
            kk = own_k
            t.step("sync", lambda e, kk=kk, stb=stb: e.dma_start(
                    out=stb[:], in_=i_ccin[kk*128:(kk+1)*128, :]), make=[sbn], dma=True)
            t.step("vector", lambda e, kk=kk: e.tensor_copy(Tcol[:], acum_s[:, (kk+1)*CH-1:(kk+1)*CH]),
                   use=["acum"], make=["Tcol"])
        t.step("scalar", lambda e: e.activation(eT[:], Tcol[:, 0:1], AF.Exp), use=["Tcol"], make=["eT"])
        for q in range(8):
            t.step("tensor", lambda e, q=q: e.matmul(ps_rep[:], eT[:, 0:1], repf_s[:, q*512:(q+1)*512],
                                                     start=True, stop=True),
                   use=["eT", "rep"], make=["ps_rep"])
            t.step("scalar", lambda e, q=q: e.activation(reprow[0:1, q*512:(q+1)*512], ps_rep[:], AF.Copy),
                   use=["ps_rep"], make=["reprow"])
        for q in range(8):
            t.step("tensor", lambda e, q=q: e.matmul(ps_bc[:], ones_row1[0:1, :],
                                                     reprow[0:1, q*512:(q+1)*512], start=True, stop=True),
                   use=["ones1", "reprow"], make=["ps_bc"])
            t.step("vector", lambda e, q=q: e.tensor_mul(Ptmp[:], P[:, q*512:(q+1)*512], ps_bc[:]),
                   use=["P", "ps_bc"], make=["Ptmp"])
            if own_k is None:
                t.step("vector", lambda e, q=q, j=j: e.scalar_tensor_tensor(
                        P[:, q*512:(q+1)*512], stb[:, q*512:(q+1)*512], flagB[:, j:j+1], Ptmp[:],
                        ALU.mult, ALU.add),
                       use=["Ptmp", sbn, "flagB"], make=["P"])
            else:
                t.step("vector", lambda e, q=q: e.tensor_add(P[:, q*512:(q+1)*512], Ptmp[:], stb[:, q*512:(q+1)*512]),
                       use=["Ptmp", sbn], make=["P"])

    for j in range(32):
        p_update(j)

    for k in range(NCH):
        for h in range(64):
            hi = h % 2
            t.step("gpsimd", lambda e, h=h: e.dma_start(out=erow[:], in_=eacum_s[h:h+1, :]),
                   use=["eacum"], make=["erow"], dma=True)
            t.step("tensor", lambda e, k=k: e.matmul(ps_bc[:, 0:CH], ones_row1[0:1, :],
                                                     erow[0:1, k*CH:(k+1)*CH], start=True, stop=True),
                   use=["ones1", "erow"], make=["ps_bc"])
            t.step("vector", lambda e, k=k, hi=hi: e.tensor_mul(Csc[hi][:], C_s[:, k*CH:(k+1)*CH], ps_bc[:, 0:CH]),
                   use=["C", "ps_bc"], make=[f"Csc{hi}"])
            t.step("vector", lambda e, h=h, hi=hi: e.tensor_copy(Pbf[hi][:], P[:, h*64:(h+1)*64]),
                   use=["P"], make=[f"Pbf{hi}"])
            t.step("tensor", lambda e, hi=hi: e.matmul(ps_yo[hi][:], Pbf[hi][:], Csc[hi][:], start=True, stop=True),
                   use=[f"Pbf{hi}", f"Csc{hi}"], make=[f"ps_yo{hi}"])
            ctile = h // 2; cbase = (h % 2) * 64
            ypb = ypbuf[hi]
            t.step("sync", lambda e, h=h, k=k, ypb=ypb: e.dma_start(
                    out=ypb[0:64, :], in_=i_ypart[h*64:(h+1)*64, k*CH:(k+1)*CH]),
                   make=[f"ypb{hi}"], dma=True)
            t.step("vector", lambda e, hi=hi, ctile=ctile, cbase=cbase, ypb=ypb: e.tensor_add(
                    ych[ctile][cbase:cbase+64, :], ypb[0:64, :], ps_yo[hi][:]),
                   use=[f"ypb{hi}", f"ps_yo{hi}"], make=[f"ych{ctile}"])
        for i in range(32):
            gi = i % 2
            t.step("sync", lambda e, i=i, k=k, gi=gi: e.dma_start(
                    out=gbuf[gi][:], in_=i_gate[i*128:(i+1)*128, k*CH:(k+1)*CH]),
                   make=[f"gb{gi}"], dma=True)
            t.step("scalar", lambda e, gi=gi: e.activation(sgb[gi][:], gbuf[gi][:], AF.Silu),
                   use=[f"gb{gi}"], make=[f"sg{gi}"])
            t.step("vector", lambda e, i=i, gi=gi: e.tensor_mul(ych[i][:], ych[i][:], sgb[gi][:]),
                   use=[f"ych{i}", f"sg{gi}"], make=[f"ych{i}"])
            t.step("scalar", lambda e, i=i, gi=gi: e.activation(x2b[gi][:], ych[i][:], AF.Square),
                   use=[f"ych{i}"], make=[f"x2{gi}"])
            t.step("tensor", lambda e, i=i, gi=gi: e.matmul(ps_rms[:], ones_colb[:, 0:1], x2b[gi][:],
                                                            start=(i == 0), stop=(i == 31)),
                   use=["onesc", f"x2{gi}"], make=["ps_rms"])
        t.step("scalar", lambda e: e.activation(rmsrow[:], ps_rms[:], AF.Copy, scale=1.0/INTER),
               use=["ps_rms"], make=["rmsrow"])
        t.step("scalar", lambda e: e.activation(rmsrow[:], rmsrow[:], AF.Sqrt, bias=eps1[0:1, 0:1]),
               use=["rmsrow", "eps1"], make=["rmsrow"])
        t.step("vector", lambda e: e.reciprocal(rmsrow[:], rmsrow[:]), use=["rmsrow"], make=["rmsrow"])
        t.step("tensor", lambda e: e.matmul(ps_bcr[:], ones_row1[0:1, :], rmsrow[0:1, :], start=True, stop=True),
               use=["ones1", "rmsrow"], make=["ps_bcr"])
        for i in range(32):
            t.step("vector", lambda e, i=i: e.scalar_tensor_tensor(
                    ynb[i][:], ych[i][:], normw_s[:, i:i+1], ps_bcr[:], ALU.mult, ALU.mult),
                   use=[f"ych{i}", "normw", "ps_bcr"], make=[f"ynb{i}"])
        for og in range(8):
            for to in range(2):
                ot = og * 2 + to
                for kk in range(32):
                    t.step("gpsimd", lambda e, ot=ot, kk=kk, to=to: e.dma_start(
                            out=wbuf[to][:], in_=i_opwT[kk*128:(kk+1)*128, ot*128:(ot+1)*128]),
                           make=[f"wb{to}"], dma=True)
                    t.step("tensor", lambda e, kk=kk, to=to: e.matmul(
                            ps_o[to][:], wbuf[to][:], ynb[kk][:], start=(kk == 0), stop=(kk == 31)),
                           use=[f"wb{to}", f"ynb{kk}"], make=[f"ps_o{to}"])
                t.step("sync", lambda e, ot=ot, k=k, to=to: e.dma_start(
                        out=xtb[to][:], in_=i_xT[ot*128:(ot+1)*128, k*CH:(k+1)*CH]),
                       make=[f"xt{to}"], dma=True)
                t.step("vector", lambda e, to=to: e.tensor_add(x1b[to][:], xtb[to][:], ps_o[to][:]),
                       use=[f"xt{to}", f"ps_o{to}"], make=[f"x1{to}"])
                t.step("sync", lambda e, ot=ot, k=k, to=to: e.dma_start(
                        out=o_x1[ot*128:(ot+1)*128, k*CH:(k+1)*CH], in_=x1b[to][:]),
                       use=[f"x1{to}"], dma=True)
        p_update(32 + k, own_k=k)
    es.close()


def full_barrier(t):
    for e in ("sync", "gpsimd", "tensor", "vector", "scalar"):
        t.wait_all(e)


def build_fused():
    nc = bass.Bass("TRN2", target_bir_lowering=False)
    dr = lambda n, s, d: nc.dram_tensor(n, s, d, kind="ExternalInput").ap()
    IO = {}
    IO["xT"] = dr("xT", [2048, 1028], F32)
    IO["ipwT"] = dr("ipwT", [2048, 8512], F32)
    IO["convw"] = dr("convw", [128, 136], F32)
    IO["convb"] = dr("convb", [128, 34], F32)
    IO["dtb"] = dr("dtb", [64, 1], F32)
    IO["alog"] = dr("alog", [64, 1], F32)
    IO["selmat"] = dr("selmat", [64, 4096], F32)
    IO["eye"] = dr("eye", [128, 128], F32)
    IO["dchan"] = dr("dchan", [128, 32], F32)
    IO["maskst"] = dr("maskst", [128, 512], F32)
    IO["flags"] = dr("flags", [1, 32], F32)
    IO["normw"] = dr("normw", [128, 32], F32)
    IO["opwT"] = dr("opwT", [4096, 2048], F32)
    IO["repmat"] = dr("repmat", [64, 4096], F32)
    IO["gwT"] = dr("gwT", [2048, 32], F32)
    IO["gupT"] = dr("gupT", [32, 2048, 1536], F32)
    IO["dnT"] = dr("dnT", [32, 768, 2048], F32)
    IO["mgf"] = dr("mgf", [1, 32], F32)
    IO["dei"] = dr("dei", [1, 8], I32)
    IO["cid"] = dr("cid", [1, 1], I32)
    IO["cidf"] = dr("cidf", [1, 1], F32)
    IO["WqT"] = dr("WqT", [2048, 2048], F32)
    IO["WkT"] = dr("WkT", [2048, 512], F32)
    IO["WvT"] = dr("WvT", [2048, 512], F32)
    IO["cosm"] = dr("cosm", [128, 1024], F32)
    IO["sinm"] = dr("sinm", [128, 1024], F32)
    IO["maskf"] = dr("maskf", [8192, 1024], BF16)
    IO["WoT"] = dr("WoT", [2048, 2048], F32)
    IO["W0T"] = dr("W0T", [2048, 8192], F32)
    IO["b0c"] = dr("b0c", [128, 64], F32)
    IO["W1T"] = dr("W1T", [8192, 2048], F32)
    IO["b1c"] = dr("b1c", [128, 16], F32)
    IO["o_out"] = nc.dram_tensor("o_out", [2048, 1024], F32, kind="ExternalOutput").ap()
    xn_d = nc.dram_tensor("xn_d", [2048, 1024], BF16)
    xnall_d = nc.dram_tensor("xnall_d", [16384, 1024], BF16)
    gate_d = nc.dram_tensor("gate_d", [4096, 1024], BF16)
    xpre_d = nc.dram_tensor("xpre_d", [4096, 1024], BF16)
    xdtm_d = nc.dram_tensor("xdtm_d", [1024, 4096], BF16)
    B_d = nc.dram_tensor("B_d", [128, 1024], BF16)
    C_d = nc.dram_tensor("C_d", [128, 1024], BF16)
    acum_d = nc.dram_tensor("acum_d", [64, 1024], F32)
    dt_d = nc.dram_tensor("dt_d", [64, 1024], F32)
    ypart_d = nc.dram_tensor("ypart_d", [4096, 1024], F32)
    ccst_in = nc.dram_tensor("ccst_in", [576, 4096], F32)
    ccst_out = nc.dram_tensor("ccst_out", [576 * 8, 4096], F32)
    x1_d = nc.dram_tensor("x1_d", [2048, 1024], F32)
    contrib_in = nc.dram_tensor("contrib_in", [16384, 1024], F32)
    moe_d = nc.dram_tensor("moe_d", [2048, 1024], F32)
    x1f_d = nc.dram_tensor("x1f_d", [2048, 1024], F32)
    xn1_d = nc.dram_tensor("xn1_d", [2048, 1024], BF16)
    q_d = nc.dram_tensor("q_d", [2048, 1024], BF16)
    k_d = nc.dram_tensor("k_d", [512, 1024], BF16)
    v_d = nc.dram_tensor("v_d", [1024, 512], BF16)
    kall_d = nc.dram_tensor("kall_d", [4096, 1024], BF16)
    vall_d = nc.dram_tensor("vall_d", [8192, 512], BF16)
    attnp_d = nc.dram_tensor("attnp_d", [2048, 1024], F32)

    rg = [list(range(8))]
    es = ExitStack()
    with nc.Block() as block:
        t = Tape(nc, es)
        io1 = dict(IO)
        io1.update({"o_xn": xn_d.ap(), "o_gate": gate_d.ap(), "o_xpre": xpre_d.ap(),
                    "o_xdtm": xdtm_d.ap(), "o_B": B_d.ap(), "o_C": C_d.ap(),
                    "o_acum": acum_d.ap(), "o_dt": dt_d.ap()})
        emit_launch1((nc, t, io1))
        full_barrier(t)
        t.step("gpsimd", lambda e: e.collective_compute(
                "AllGather", ALU.bypass, rg, [xn_d.ap().opt()], [xnall_d.ap().opt()]),
               make=["xnall"], tag="ag_xn", cc=True)
        zz = es.enter_context(nc.sbuf_tensor("mz_zz", [64, 512], F32))
        t.step("gpsimd", lambda e: e.memset(zz[:], 0.0), make=["mzz"])
        for zq in range(8):
            t.step("gpsimd", lambda e, zq=zq: e.dma_start(out=ccst_in.ap()[512:576, zq*512:(zq+1)*512], in_=zz[:]),
                   use=["mzz"], dma=True)
        io2 = dict(IO)
        io2.update({"xdtm": xdtm_d.ap(), "Bm": B_d.ap(), "Cm": C_d.ap(), "acum": acum_d.ap(),
                    "xpre": xpre_d.ap(),
                    "o_ypart": ypart_d.ap(),
                    "o_states": ccst_in.ap()[0:512, :].rearrange("(c p) n -> c p n", p=128),
                    "o_T": ccst_in.ap()[512:576, 0:4]})
        emit_launch2((nc, t, io2))
        full_barrier(t)
        t.step("gpsimd", lambda e: e.collective_compute(
                "AllGather", ALU.bypass, rg, [ccst_in.ap().opt()], [ccst_out.ap().opt()]),
               make=["ccstout"], tag="ag_st", cc=True)
        full_barrier(t)
        io3 = dict(IO)
        io3.update({"ccst_out": ccst_out.ap(), "ccst_in": ccst_in.ap()[0:512, :],
                    "Cm": C_d.ap(), "acum": acum_d.ap(), "ypart": ypart_d.ap(),
                    "gate": gate_d.ap(), "xTown": IO["xT"][:, 4:1028], "o_x1": x1_d.ap()})
        emit_l3_fused((nc, t, io3))
        full_barrier(t)
        io4 = dict(IO)
        io4.update({"xnfull": BlockedCols(xnall_d.ap().rearrange("(r d) t -> d r t", r=8), 1024),
                    "o_contrib": BlockedCols(contrib_in.ap().rearrange("(r d) t -> d r t", r=8), 1024)})
        emit_launch4((nc, t, io4))
        full_barrier(t)
        t.step("gpsimd", lambda e: e.collective_compute(
                "ReduceScatter", ALU.add, rg, [contrib_in.ap().opt()], [moe_d.ap().opt()]),
               make=["moed"], tag="rs_moe", cc=True)
        full_barrier(t)
        io6 = dict(IO)
        io6.update({"x1own": x1_d.ap(),
                    "contribs": moe_d.ap().rearrange("(g d) t -> g d t", g=1),
                    "o_x1f": x1f_d.ap(), "o_xn1": xn1_d.ap(), "o_q": q_d.ap(),
                    "o_k": k_d.ap(), "o_v": v_d.ap()})
        emit_launch6((nc, t, io6))
        full_barrier(t)
        t.step("gpsimd", lambda e: e.collective_compute(
                "AllGather", ALU.bypass, rg, [k_d.ap().opt()], [kall_d.ap().opt()]),
               make=["kall"], tag="ag_k", cc=True)
        t.step("gpsimd", lambda e: e.collective_compute(
                "AllGather", ALU.bypass, rg, [v_d.ap().opt()], [vall_d.ap().opt()]),
               make=["vall"], tag="ag_v", cc=True)
        full_barrier(t)
        io7 = dict(IO)
        io7.update({"qown": q_d.ap(),
                    "Kf": BlockedCols(kall_d.ap().rearrange("(r kd) t -> kd r t", r=8), 1024),
                    "Vtm": vall_d.ap(), "o_attnp": attnp_d.ap()})
        emit_launch7((nc, t, io7))
        full_barrier(t)
        io8 = dict(IO)
        io8.update({"xn1": xn1_d.ap(), "x1f": x1f_d.ap(), "attnp": attnp_d.ap()})
        emit_launch8((nc, t, io8))
        full_barrier(t)
        t.flush(block)
    es.close()
    return nc

_CACHE = {}

def _stage(inp):
    f32 = np.float32
    x = np.asarray(inp['L0_attn_hidden_states'][0], f32)
    xTf = np.ascontiguousarray(x.T)
    ipwT = np.ascontiguousarray(np.asarray(inp['L0_attn_in_proj_w'], f32).T)
    cw = np.asarray(inp['L0_attn_conv_w'], f32)[:, 0, :]
    convw = np.ascontiguousarray(cw.reshape(34, 128, 4).transpose(1, 0, 2).reshape(128, 136))
    convb = np.ascontiguousarray(np.asarray(inp['L0_attn_conv_b'], f32).reshape(34, 128).T)
    dtb = np.ascontiguousarray(np.asarray(inp['L0_attn_dt_bias'], f32).reshape(64, 1))
    alog = np.ascontiguousarray(np.asarray(inp['L0_attn_A_log'], f32).reshape(64, 1))
    sel = np.zeros((64, 32 * 128), f32)
    for ti in range(32):
        for m in range(128):
            sel[2 * ti + (m // 64), ti * 128 + m] = 1.0
    eye = np.eye(128, dtype=f32)
    D = np.asarray(inp['L0_attn_D'], f32)
    dchan = np.ascontiguousarray(np.repeat(D, 64).reshape(32, 128).T)
    maskst = np.zeros((128, 512), f32)
    for s in range(2):
        for sr in range(128):
            maskst[sr, s*256:(s+1)*256] = (np.arange(256) >= s * 128 + sr)
    nw = np.asarray(inp['L0_attn_norm_w'], f32)
    normw = np.ascontiguousarray(nw.reshape(32, 128).T)
    opwT = np.ascontiguousarray(np.asarray(inp['L0_attn_out_proj_w'], f32).T)
    rep = np.zeros((64, 4096), f32)
    for col in range(4096):
        rep[col // 64, col] = 1.0
    gwT = np.ascontiguousarray(np.asarray(inp['L0_ffn_gate_weight'], f32).T)
    gupT = np.ascontiguousarray(np.asarray(inp['L0_ffn_gate_up_proj'], f32).transpose(0, 2, 1))
    dnT = np.ascontiguousarray(np.asarray(inp['L0_ffn_down_proj'], f32).transpose(0, 2, 1))
    mgf = np.asarray(inp['L0_ffn_merge_groups']).astype(f32).reshape(1, 32)
    de = np.asarray(inp['L0_ffn_dominant_experts']).astype(np.int32).reshape(1, 8)
    WqT = np.ascontiguousarray(np.asarray(inp['L1_attn_Wq'], f32).T)
    WkT = np.ascontiguousarray(np.asarray(inp['L1_attn_Wk'], f32).T)
    WvT = np.ascontiguousarray(np.asarray(inp['L1_attn_Wv'], f32).T)
    WoT = np.ascontiguousarray(np.asarray(inp['L1_attn_Wo'], f32).T)
    inv = (1.0 / 10000.0 ** (np.arange(0, 128, 2, dtype=f32) / 128)).astype(f32)
    freqs = np.arange(8192, dtype=f32)[:, None] * inv[None, :]
    cosf = np.repeat(np.cos(freqs), 2, axis=-1).astype(f32)
    sinf = np.repeat(np.sin(freqs), 2, axis=-1).astype(f32)
    W0T = np.ascontiguousarray(np.asarray(inp['L1_ffn_W0'], f32).T)
    b0c = np.ascontiguousarray(np.asarray(inp['L1_ffn_bias0'], f32).reshape(64, 128).T)
    W1T = np.ascontiguousarray(np.asarray(inp['L1_ffn_W1'], f32).T)
    b1c = np.ascontiguousarray(np.asarray(inp['L1_ffn_bias1'], f32).reshape(16, 128).T)
    kk = np.arange(8192)[:, None]
    maps = []
    for c in range(8):
        t0 = c * 1024
        xs = np.zeros((2048, 1028), f32)
        if c > 0:
            xs[:, 0:4] = xTf[:, t0-4:t0]
        xs[:, 4:] = xTf[:, t0:t0+1024]
        flags = np.zeros((1, 32), f32); flags[0, :4*c] = 1.0
        qq = np.arange(t0, t0 + 1024)[None, :]
        maskf = (kk <= qq + 128).astype(ml_dtypes.bfloat16)
        maps.append({
            "xT": np.ascontiguousarray(xs), "ipwT": ipwT, "convw": convw, "convb": convb,
            "dtb": dtb, "alog": alog, "selmat": sel, "eye": eye, "dchan": dchan,
            "maskst": maskst, "flags": flags, "normw": normw, "opwT": opwT, "repmat": rep,
            "gwT": gwT, "gupT": gupT, "dnT": dnT, "mgf": mgf, "dei": de,
            "cid": np.array([[c]], np.int32), "cidf": np.array([[c]], f32),
            "WqT": WqT, "WkT": WkT, "WvT": WvT, "WoT": WoT,
            "cosm": np.ascontiguousarray(cosf[t0:t0+1024].T), "sinm": np.ascontiguousarray(sinf[t0:t0+1024].T),
            "maskf": maskf, "W0T": W0T, "b0c": b0c, "W1T": W1T, "b1c": b1c,
        })
    return maps


def kernel(**inp):
    if "nc" not in _CACHE:
        _CACHE["nc"] = build_fused()
    nc = _CACHE["nc"]
    maps = _stage(inp)
    res = run_bass_kernel_spmd(nc, maps, core_ids=NC8)
    out = np.concatenate([np.asarray(res.results[c]["o_out"]) for c in range(8)], axis=1)
    return np.ascontiguousarray(out.T).reshape(1, 8192, 2048).astype(np.float32)


# version 03 (final)
# speedup vs baseline: 1.8300x; 1.8300x over previous
import numpy as np
import ml_dtypes
from contextlib import ExitStack
import concourse.bass as bass
import concourse.mybir as mybir
from concourse.bass_utils import run_bass_kernel_spmd

F32 = mybir.dt.float32
BF16 = mybir.dt.bfloat16
F16 = mybir.dt.float16
I32 = mybir.dt.int32
AF = mybir.ActivationFunctionType
ALU = mybir.AluOpType
AX = mybir.AxisListType
NC8 = list(range(8))
"""Global-tape emission framework for raw Bass.

Steps are appended in one global order (a valid topological order of the
dataflow). Each step runs on one engine and emits exactly ONE instruction
(fn=None -> waits only). Dependencies via use=/make= buffer names; waits
are derived with per-engine knowledge (vector clocks) so redundant waits
are elided.

DMA completions are tracked on dedicated per-engine slot semaphores
(NSLOTS round-robin). A slot is reused only after the issuing engine
waits for the previous completion on it, so every waited value is
unambiguous (at most one DMA in flight per slot).
Compute steps increment their engine's progress semaphore by 1 (engine
pipelines are FIFO so cumulative values are safe).
"""
NSLOTS = 6
ENGINES = ("sync", "gpsimd", "tensor", "vector", "scalar")
DMA_ENGS = ("sync", "gpsimd", "scalar", "vector", "tensor")


class Tape:
    def __init__(self, nc, es):
        self.nc = nc
        self._es = es
        self.steps = []
        self.sems = {}
        self.counts = {}
        self.known = {e: {} for e in ENGINES}
        self.know_at = {}
        self.last_w = {}
        self.readers = {}
        self.dma_slot = {e: 0 for e in ENGINES}
        for e in ENGINES:
            self.sems[e] = es.enter_context(nc.semaphore(f"sem_{e}"))
            self.counts[e] = 0
        for e in DMA_ENGS:
            for i in range(NSLOTS):
                n = f"{e}_d{i}"
                self.sems[n] = es.enter_context(nc.semaphore(n))
                self.counts[n] = 0

    def _need(self, eng, sem, val):
        if sem == eng == "tensor":
            return None
        if self.known[eng].get(sem, 0) >= val:
            return None
        return (sem, val)

    def step(self, eng, fn, use=(), make=(), dma=False, tag=None, raw=False, cc=False):
        waits = {}
        def add(ev):
            if ev is None:
                return
            w = self._need(eng, ev[0], ev[1])
            if w:
                waits[w[0]] = max(waits.get(w[0], 0), w[1])
        for b in use:
            add(self.last_w.get(b))
            if b.startswith("ps"):
                for r in self.readers.get(b, ()):
                    if r[0] != eng and not r[0].startswith(eng + "_"):
                        add(r)
        for b in make:
            add(self.last_w.get(b))
            for r in self.readers.get(b, ()):
                add(r)
        if raw:
            for s, v in waits.items():
                self.known[eng][s] = max(self.known[eng].get(s, 0), v)
            self.steps.append((eng, fn, waits, None, 0, tag))
            return None
        if cc:
            n = f"cc{len([s for s in self.sems if s.startswith('cc')])}"
            self.sems[n] = self._es.enter_context(self.nc.semaphore(n))
            self.counts[n] = 1
            ev_sem, ev_val = n, 1
            inc_sem, inc_n = n, 1
        elif dma:
            slot = self.dma_slot[eng]
            self.dma_slot[eng] = (slot + 1) % NSLOTS
            sname = f"{eng}_d{slot}"
            prev = self.counts[sname]
            if prev > 0 and self.known[eng].get(sname, 0) < prev:
                waits[sname] = max(waits.get(sname, 0), prev)
            self.counts[sname] = prev + 16
            ev_sem, ev_val = sname, prev + 16
            inc_sem, inc_n = sname, 16
        else:
            self.counts[eng] += 1
            ev_sem, ev_val = eng, self.counts[eng]
            inc_sem, inc_n = eng, 1
        for s, v in list(waits.items()):
            self.known[eng][s] = max(self.known[eng].get(s, 0), v)
            snap = self.know_at.get((s, v))
            if snap:
                for s2, v2 in snap.items():
                    if s2 != eng:
                        self.known[eng][s2] = max(self.known[eng].get(s2, 0), v2)
        snap = dict(self.known[eng])
        if dma:
            snap[eng] = self.counts[eng]
            snap[ev_sem] = ev_val
        else:
            snap[eng] = self.counts[eng]
        self.know_at[(ev_sem, ev_val)] = snap
        ev = (ev_sem, ev_val)
        for b in make:
            self.last_w[b] = ev
            self.readers[b] = []
        for b in use:
            self.readers.setdefault(b, []).append(ev)
        if fn is not None:
            self.steps.append((eng, fn, waits, inc_sem, inc_n, tag))
        else:
            self.steps.append((eng, None, waits, None, 0, tag))
        return ev

    def wait_bufs(self, eng, bufs):
        waits = {}
        for b in bufs:
            ev = self.last_w.get(b)
            if ev:
                w = self._need(eng, ev[0], ev[1])
                if w:
                    waits[w[0]] = max(waits.get(w[0], 0), w[1])
            for r in self.readers.get(b, ()):
                w = self._need(eng, r[0], r[1])
                if w:
                    waits[w[0]] = max(waits.get(w[0], 0), w[1])
        for s, v in waits.items():
            self.known[eng][s] = max(self.known[eng].get(s, 0), v)
        if waits:
            self.steps.append((eng, None, waits, None, 0, "wait_bufs"))

    def wait_all(self, eng):
        waits = {}
        for s, v in self.counts.items():
            if s != eng and v > 0 and self.known[eng].get(s, 0) < v:
                waits[s] = v
        for s, v in waits.items():
            self.known[eng][s] = max(self.known[eng].get(s, 0), v)
        if waits:
            self.steps.append((eng, None, waits, None, 0, "waitall"))

    def flush(self, block):
        per_eng = {e: [] for e in ENGINES}
        for eng, fn, waits, inc_sem, inc_n, tag in self.steps:
            per_eng[eng].append((fn, waits, inc_sem, inc_n, tag))
        tape = self
        def make_section(e):
            def section(engobj):
                for fn, w, inc_sem, inc_n, tag in per_eng[e]:
                    for s, v in sorted(w.items()):
                        if v > 0:
                            engobj.wait_ge(tape.sems[s], v)
                    if fn is not None:
                        instr = fn(engobj)
                        if inc_sem is None:
                            continue
                        if instr is None:
                            raise RuntimeError(f"step on {e} returned no instruction (tag={tag})")
                        instr.then_inc(tape.sems[inc_sem], inc_n)
            return section
        for e in ENGINES:
            if per_eng[e]:
                getattr(block, e)(make_section(e))



"""Launch 1: LN + in_proj + conv/silu + dt-softplus + A_cum + x*dt transpose.
Column layout: [halo(3) | own(1024)] everywhere (cols 0:3 halo, 3:1027 own).
PSUM column mapping: cols 0:1024 -> ps_a/ps_b [128,1024]; cols 1024:1027 -> ps_h slices."""


TOK, HALO = 1024, 4
TW = TOK + HALO
D, KT, OP = 2048, 16, 8512
NH, HD, INTER, STATE, CH = 64, 64, 4096, 128, 256

def emit_launch1(fctx):
    nc, t, IO = fctx
    nxbc=17; ngate=16; do_dt=True; dt_stage=99; s4copy=False; skip_stats=False; skip_mm=False
    xT = IO["xT"]
    ipwT = IO["ipwT"]
    convw = IO["convw"]
    convb = IO["convb"]
    dtb_in = IO["dtb"]
    alog_in = IO["alog"]
    selmat_in = IO["selmat"]
    eye_in = IO["eye"]
    o_xn = IO["o_xn"]
    o_gate = IO["o_gate"]
    o_xpre = IO["o_xpre"]
    o_xdtm = IO["o_xdtm"]
    o_B = IO["o_B"]
    o_C = IO["o_C"]
    o_acum = IO["o_acum"]
    o_dt = IO["o_dt"]

    es = ExitStack()
    def sb(name, shp, dt=F32): return es.enter_context(nc.sbuf_tensor("L1_" + name, shp, dt))
    def psa(name, shp, dt=F32): return es.enter_context(nc.psum_tensor("L1_" + name, shp, dt))
    x_s = [sb(f"x{k}", [128, TW]) for k in range(KT)]
    xb_s = [sb(f"xb{k}", [128, TW], BF16) for k in range(KT)]
    x2_s = [sb(f"x2_{i}", [128, TW], BF16) for i in range(2)]
    ones_col = sb("ones_col", [128, 1], BF16)
    ones_row1 = sb("ones_row1", [1, 128])
    statm = sb("statm", [1, TW]); stat2 = sb("stat2", [1, TW])
    statv = sb("statv", [1, TW]); statr = sb("statr", [1, TW])
    convw_s = sb("convw_s", [128, 136]); convb_s = sb("convb_s", [128, 34])
    dtb_s = sb("dtb_s", [64, 1]); alog_s = sb("alog_s", [64, 1]); negA_s = sb("negA_s", [64, 1])
    selmat_s = sb("selmat_s", [64, 32 * 128], BF16)
    ident = sb("ident", [128, 128], BF16)
    wbuf = [sb(f"wbuf{i}", [128, KT * 256], BF16) for i in range(2)]
    dt_s = sb("dt_s", [64, TOK]); dtbf_s = sb("dtbf_s", [64, TOK], BF16)
    adt_s = sb("adt_s", [64, TOK]); acum_s = sb("acum_s", [64, TOK])
    zero64 = sb("zero64", [64, TOK])
    dtw1 = sb("dtw1", [64, TOK]); dtw2 = sb("dtw2", [64, TOK])
    eps1 = sb("eps1", [1, 1]); one64 = sb("one64", [64, 1])
    tmpa = sb("tmpa", [128, TW]); tmpb = sb("tmpb", [128, TW])
    xbc_f = [sb(f"xbc_f{i}", [128, TW]) for i in range(2)]
    conv_acc = sb("conv_acc", [128, TOK]); conv_acc2 = sb("conv_acc2", [128, TOK])
    sil_bf = [sb(f"sil_bf{i}", [128, TOK], BF16) for i in range(2)]
    xdt_bf = sb("xdt_bf", [128, TOK], BF16)
    trevac = [sb(f"trevac{i}", [128, 128], BF16) for i in range(2)]
    gate_bf = [sb(f"gate_bf{i}", [128, TOK], BF16) for i in range(2)]

    ps_a = psa("ps_a", [128, TOK]); ps_b = psa("ps_b", [128, TOK])
    ps_h1 = psa("ps_h1", [128, 4]); ps_h2 = psa("ps_h2", [128, 4])
    ps_dtb = psa("ps_dtb", [128, 512])
    ps_tr = psa("ps_tr", [128, 128], BF16)

    if True:
        pass
        for k in range(KT):
            t.step("sync", lambda e, k=k: e.dma_start(out=x_s[k][:], in_=xT[k*128:(k+1)*128, :]),
                   make=[f"x{k}"], dma=True)
        t.step("gpsimd", lambda e: e.dma_start(out=convw_s[:], in_=convw), make=["convw"], dma=True)
        t.step("gpsimd", lambda e: e.dma_start(out=convb_s[:], in_=convb), make=["convb"], dma=True)
        t.step("gpsimd", lambda e: e.dma_start(out=dtb_s[:], in_=dtb_in), make=["dtb"], dma=True)
        t.step("gpsimd", lambda e: e.dma_start(out=alog_s[:], in_=alog_in), make=["alog"], dma=True)
        t.step("gpsimd", lambda e: e.dma_start(out=selmat_s[:], in_=selmat_in), make=["selmat"], dma=True)
        t.step("gpsimd", lambda e: e.dma_start(out=ident[:], in_=eye_in), make=["eye"], dma=True)
        t.step("gpsimd", lambda e: e.memset(ones_col[:], 1.0), make=["ones_col"])
        t.step("gpsimd", lambda e: e.memset(ones_row1[:], 1.0), make=["ones_row1"])
        t.step("gpsimd", lambda e: e.memset(zero64[:], 0.0), make=["zero64"])
        t.step("gpsimd", lambda e: e.memset(eps1[:], 1e-5), make=["eps1"])
        t.step("gpsimd", lambda e: e.memset(one64[:], 1.0), make=["one64"])

        def chunks_all(hoff):
            return [(0, 512, ps_a, "ps_a", 0), (512, 512, ps_a, "ps_a", 512), (1024, 4, ps_h1, "ps_h1", 0)]
        if skip_stats:
            for k in range(KT):
                t.step("vector", lambda e, k=k: e.tensor_copy(xb_s[k][:], x_s[k][:]),
                       use=[f"x{k}"], make=[f"xb{k}"])
        for k in range(() if skip_stats else range(KT)) if False else (range(0) if skip_stats else range(KT)):
            t.step("vector", lambda e, k=k: e.tensor_copy(xb_s[k][:], x_s[k][:]),
                   use=[f"x{k}"], make=[f"xb{k}"])
            xi = k % 2
            t.step("scalar", lambda e, k=k, xi=xi: e.activation(x2_s[xi][:], xb_s[k][:], AF.Square),
                   use=[f"xb{k}"], make=[f"x2_{xi}"])
            for (o, n, P, pn, po) in chunks_all(0):
                t.step("tensor", lambda e, k=k, o=o, n=n, P=P, po=po: e.matmul(
                        P[0:1, po:po+n], ones_col[:, 0:1], xb_s[k][:, o:o+n],
                        start=(k == 0), stop=(k == KT-1)),
                       use=["ones_col", f"xb{k}"], make=[pn])
            for (o, n, P, pn, po) in [(0, 512, ps_b, "ps_b", 0), (512, 512, ps_b, "ps_b", 512), (1024, 4, ps_h2, "ps_h2", 0)]:
                t.step("tensor", lambda e, k=k, o=o, n=n, P=P, po=po, xi=xi: e.matmul(
                        P[0:1, po:po+n], ones_col[:, 0:1], x2_s[xi][:, o:o+n],
                        start=(k == 0), stop=(k == KT-1)),
                       use=["ones_col", f"x2_{xi}"], make=[pn])
        if skip_stats:
            pass
        if not skip_stats:
            t.step("scalar", lambda e: e.activation(statm[0:1, 0:TOK], ps_a[0:1, :], AF.Copy, scale=1.0/D),
                   use=["ps_a"], make=["statm"])
            t.step("scalar", lambda e: e.activation(statm[0:1, TOK:TW], ps_h1[0:1, :], AF.Copy, scale=1.0/D),
                   use=["ps_h1"], make=["statm"])
            t.step("scalar", lambda e: e.activation(stat2[:], statm[:], AF.Square),
                   use=["statm"], make=["stat2"])
            t.step("vector", lambda e: e.scalar_tensor_tensor(statv[0:1, 0:TOK], ps_b[0:1, :], 1.0/D, stat2[0:1, 0:TOK], ALU.mult, ALU.subtract),
                   use=["ps_b", "stat2"], make=["statv"])
            t.step("vector", lambda e: e.scalar_tensor_tensor(statv[0:1, TOK:TW], ps_h2[0:1, :], 1.0/D, stat2[0:1, TOK:TW], ALU.mult, ALU.subtract),
                   use=["ps_h2", "stat2"], make=["statv"])
            t.step("scalar", lambda e: e.activation(statr[:], statv[:], AF.Sqrt, bias=eps1[0:1, 0:1]),
                   use=["statv", "eps1"], make=["statr"])
            t.step("vector", lambda e: e.reciprocal(statr[:], statr[:]), use=["statr"], make=["statr"])
            for (o, n, P, pn, po) in chunks_all(0):
                t.step("tensor", lambda e, o=o, n=n, P=P, po=po: e.matmul(
                        P[:, po:po+n], ones_row1[0:1, :], statm[0:1, o:o+n], start=True, stop=True),
                       use=["ones_row1", "statm"], make=[pn])
            for (o, n, P, pn, po) in [(0, 512, ps_b, "ps_b", 0), (512, 512, ps_b, "ps_b", 512), (1024, 4, ps_h2, "ps_h2", 0)]:
                t.step("tensor", lambda e, o=o, n=n, P=P, po=po: e.matmul(
                        P[:, po:po+n], ones_row1[0:1, :], statr[0:1, o:o+n], start=True, stop=True),
                       use=["ones_row1", "statr"], make=[pn])
            for k in range(KT):
                tb = [tmpa, tmpb][k % 2]; tbn = f"tmp{k%2}"
                t.step("vector", lambda e, k=k, tb=tb: e.tensor_sub(tb[:, 0:TOK], x_s[k][:, 0:TOK], ps_a[:]),
                       use=[f"x{k}", "ps_a"], make=[tbn])
                t.step("vector", lambda e, k=k, tb=tb: e.tensor_sub(tb[:, TOK:TW], x_s[k][:, TOK:TW], ps_h1[:]),
                       use=[f"x{k}", "ps_h1"], make=[tbn])
                t.step("vector", lambda e, k=k, tb=tb: e.tensor_mul(xb_s[k][:, 0:TOK], tb[:, 0:TOK], ps_b[:]),
                       use=[tbn, "ps_b"], make=[f"xb{k}"])
                t.step("vector", lambda e, k=k, tb=tb: e.tensor_mul(xb_s[k][:, TOK:TW], tb[:, TOK:TW], ps_h2[:]),
                       use=[tbn, "ps_h2"], make=[f"xb{k}"])
                t.step("sync", lambda e, k=k: e.dma_start(out=o_xn[k*128:(k+1)*128, :], in_=xb_s[k][:, HALO:TW]),
                       use=[f"xb{k}"], dma=True)

        def wdma(g, ocol0, ow):
            for k in range(KT):
                src = ipwT[k*128:(k+1)*128, ocol0:ocol0+ow]
                dst = wbuf[g % 2][:].rearrange("p (k o) -> p k o", k=KT)[:, k, 0:ow]
                t.step("gpsimd", lambda e, src=src, dst=dst: e.dma_start(out=dst, in_=src),
                       make=[f"wb{g%2}"], dma=True)

        def mm_group(g, ow, own_only):
            nt = (ow + 127) // 128
            for to in range(nt):
                mw = min(128, ow - to*128)
                P = [ps_a, ps_b][to]; pn = ["ps_a", "ps_b"][to]
                if own_only:
                    chunks = [(HALO, 512, P, pn, 0), (HALO+512, 512, P, pn, 512)]
                else:
                    chunks = [(0, 512, P, pn, 0), (512, 512, P, pn, 512), (1024, 4, [ps_h1, ps_h2][to], ["ps_h1", "ps_h2"][to], 0)]
                for k in range(KT):
                    lhs = wbuf[g % 2][:].rearrange("p (k o) -> p k o", k=KT)[:, k, to*128:to*128+mw]
                    for (ro, n, PP, ppn, po) in chunks:
                        t.step("tensor", lambda e, k=k, lhs=lhs, PP=PP, po=po, n=n, ro=ro, mw=mw: e.matmul(
                                PP[0:mw, po:po+n], lhs, xb_s[k][:, ro:ro+n],
                                start=(k == 0), stop=(k == KT-1)),
                               use=[f"wb{g%2}", f"xb{k}"], make=[ppn])

        if not do_dt:
            t.wait_all("sync"); t.flush(block); es.close(); return nc
        wdma(0, 8448, 64)
        if dt_stage < 2:
            t.wait_all("sync"); t.flush(block); es.close(); return nc
        if not skip_mm:
            mm_group(0, 64, own_only=True)
        if dt_stage < 3:
            t.wait_all("sync"); t.flush(block); es.close(); return nc
        if s4copy == "vcopy64":
            t.step("vector", lambda e: e.tensor_copy(dtw1[:], ps_a[0:64, 0:TOK]),
                   use=["ps_a"], make=["dtw1"])
        elif s4copy == "vcopyfull":
            t.step("vector", lambda e: e.tensor_copy(tmpa[:, 0:TOK], ps_a[:]),
                   use=["ps_a"], make=["tmp0"])
        elif s4copy == "scopy64":
            t.step("scalar", lambda e: e.activation(dtw1[:], ps_a[0:64, 0:TOK], AF.Copy),
                   use=["ps_a"], make=["dtw1"])
        elif s4copy == "vhalf":
            t.step("vector", lambda e: e.tensor_copy(dtw1[:, 0:512], ps_a[0:64, 0:512]),
                   use=["ps_a"], make=["dtw1"])
        else:
            t.step("vector", lambda e: e.tensor_scalar(dtw1[:], ps_a[0:64, 0:TOK], dtb_s[:, 0:1], 0.0, ALU.add, ALU.max),
                   use=["ps_a", "dtb"], make=["dtw1"])
        if dt_stage < 4:
            t.wait_all("sync"); t.flush(block); es.close(); return nc
        t.step("scalar", lambda e: e.activation(dtw2[:], ps_a[0:64, 0:TOK], AF.Abs, bias=dtb_s[:, 0:1]),
               use=["ps_a", "dtb"], make=["dtw2"])
        t.step("scalar", lambda e: e.activation(dtw2[:], dtw2[:], AF.Exp, scale=-1.0),
               use=["dtw2"], make=["dtw2"])
        t.step("scalar", lambda e: e.activation(dtw2[:], dtw2[:], AF.Ln, bias=one64[:, 0:1]),
               use=["dtw2", "one64"], make=["dtw2"])
        t.step("vector", lambda e: e.tensor_add(dt_s[:], dtw1[:], dtw2[:]),
               use=["dtw1", "dtw2"], make=["dt_s"])
        t.step("vector", lambda e: e.tensor_scalar_min(dt_s[:], dt_s[:], 100.0),
               use=["dt_s"], make=["dt_s"])
        if dt_stage < 5:
            t.wait_all("sync"); t.flush(block); es.close(); return nc
        t.step("vector", lambda e: e.tensor_copy(dtbf_s[:], dt_s[:]), use=["dt_s"], make=["dtbf"])
        t.step("sync", lambda e: e.dma_start(out=o_dt, in_=dt_s[:]), use=["dt_s"], dma=True)
        t.step("scalar", lambda e: e.activation(negA_s[:], alog_s[:], AF.Exp), use=["alog"], make=["negA"])
        t.step("vector", lambda e: e.tensor_scalar_mul(negA_s[:], negA_s[:], -1.0), use=["negA"], make=["negA"])
        t.step("vector", lambda e: e.tensor_scalar_mul(adt_s[:], dt_s[:], negA_s[:, 0:1]),
               use=["dt_s", "negA"], make=["adt"])
        if dt_stage < 6:
            t.wait_all("sync"); t.flush(block); es.close(); return nc
        for c in range(4):
            t.step("vector", lambda e, c=c: e.tensor_tensor_scan(
                    acum_s[:, c*CH:(c+1)*CH], adt_s[:, c*CH:(c+1)*CH], zero64[:, c*CH:(c+1)*CH],
                    0.0, ALU.add, ALU.add),
                   use=["adt", "zero64"], make=["acum"])
        t.step("sync", lambda e: e.dma_start(out=o_acum, in_=acum_s[:]), use=["acum"], dma=True)

        gi = 1
        for j in range(nxbc):
            wdma(gi, 4096 + 256*j, 256)
            mm_group(gi, 256, own_only=False)
            for to in range(2):
                tile_idx = 2*j + to
                P = [ps_a, ps_b][to]; pn = ["ps_a", "ps_b"][to]
                xf = xbc_f[to]; xfn = f"xbc{to}"
                t.step("vector", lambda e, P=P, xf=xf: e.tensor_copy(xf[:, 0:TOK], P[:]),
                       use=[pn], make=[xfn])
                PH = [ps_h1, ps_h2][to]; phn = ["ps_h1", "ps_h2"][to]
                t.step("vector", lambda e, xf=xf, PH=PH: e.tensor_copy(xf[:, TOK:TW], PH[:]),
                       use=[phn], make=[xfn])
                t.step("vector", lambda e, xf=xf, ti=tile_idx: e.tensor_scalar_mul(
                        conv_acc[:], xf[:, 1:1+TOK], convw_s[:, ti*4:ti*4+1]),
                       use=[xfn, "convw"], make=["cacc"])
                accs = [conv_acc, conv_acc2]
                for kk in range(1, 4):
                    src, dst = accs[(kk+1) % 2], accs[kk % 2]
                    sn = "cacc" if src is conv_acc else "cacc2"
                    dn = "cacc" if dst is conv_acc else "cacc2"
                    t.step("vector", lambda e, xf=xf, ti=tile_idx, kk=kk, src=src, dst=dst: e.scalar_tensor_tensor(
                            dst[:], xf[:, kk+1:kk+1+TOK], convw_s[:, ti*4+kk:ti*4+kk+1], src[:], ALU.mult, ALU.add),
                           use=[xfn, "convw", sn], make=[dn])
                sil = sil_bf[to]
                t.step("scalar", lambda e, ti=tile_idx, sil=sil: e.activation(
                        sil[:], conv_acc2[:], AF.Silu, bias=convb_s[:, ti:ti+1]),
                       use=["cacc2", "convb"], make=[f"sil{to}"])
                if j < 16:
                    chan0 = 256*j + 128*to
                    t.step("sync", lambda e, sil=sil, chan0=chan0: e.dma_start(
                            out=o_xpre[chan0:chan0+128, :], in_=sil[:]),
                           use=[f"sil{to}"], dma=True)
                    for no in (0, 512):
                        t.step("tensor", lambda e, ti=tile_idx, no=no: e.matmul(
                                ps_dtb[:], selmat_s[:, ti*128:(ti+1)*128], dtbf_s[:, no:no+512], start=True, stop=True),
                               use=["selmat", "dtbf"], make=["ps_dtb"])
                        t.step("vector", lambda e, sil=sil, no=no: e.tensor_mul(xdt_bf[:, no:no+512], sil[:, no:no+512], ps_dtb[:]),
                               use=[f"sil{to}", "ps_dtb"], make=["xdt_bf"])
                    for tt in range(8):
                        tv = trevac[tt % 2]; tvn = f"trevac{tt%2}"
                        t.step("tensor", lambda e, tt=tt: e.transpose(ps_tr[:], xdt_bf[:, tt*128:(tt+1)*128], ident[:]),
                               use=["xdt_bf", "eye"], make=["ps_tr"])
                        t.step("vector", lambda e, tv=tv: e.tensor_copy(tv[:], ps_tr[:]),
                               use=["ps_tr"], make=[tvn])
                        t.step("sync", lambda e, tv=tv, tt=tt, chan0=chan0: e.dma_start(
                                out=o_xdtm[tt*128:(tt+1)*128, chan0:chan0+128], in_=tv[:]),
                               use=[tvn], dma=True)
                else:
                    dst = o_B if to == 0 else o_C
                    t.step("sync", lambda e, sil=sil, dst=dst: e.dma_start(out=dst, in_=sil[:]),
                           use=[f"sil{to}"], dma=True)
            gi += 1
        for j in range(ngate):
            wdma(gi, 256*j, 256)
            mm_group(gi, 256, own_only=True)
            for to in range(2):
                P = [ps_a, ps_b][to]; pn = ["ps_a", "ps_b"][to]
                gb = gate_bf[to]
                t.step("vector", lambda e, P=P, gb=gb: e.tensor_copy(gb[:], P[:]),
                       use=[pn], make=[f"gb{to}"])
                t.step("sync", lambda e, gb=gb, j=j, to=to: e.dma_start(
                        out=o_gate[256*j+128*to:256*j+128*to+128, :], in_=gb[:]),
                       use=[f"gb{to}"], dma=True)
            gi += 1
    es.close()





"""Launch 2: mamba SSM local-chunk work.
Inputs per core: xdtm [1024,4096] bf16 (token-major x*dt), B,C [128,1024] bf16,
acum [64,1024] fp32, xpre [4096,1024] bf16, D_chan [128,32] fp32, masks.
Outputs: ypart [4096,1024] bf16 (Y_diag + D_res, feature-major), states [4,128,4096] fp32."""


TOK, CH, NCH, NH, HD, STATE, INTER = 1024, 256, 4, 64, 64, 128, 4096

def emit_launch2(fctx):
    nc, t, IO = fctx
    i_xdtm = IO["xdtm"]
    i_B = IO["Bm"]
    i_C = IO["Cm"]
    i_acum = IO["acum"]
    i_xpre = IO["xpre"]
    i_dchan = IO["dchan"]
    i_mask = IO["maskst"]
    i_eye = IO["eye"]
    o_ypart = IO["o_ypart"]
    o_states = IO["o_states"]
    o_T = IO["o_T"]

    es = ExitStack()
    def sb(name, shp, dt=F32): return es.enter_context(nc.sbuf_tensor("L2_" + name, shp, dt))
    def psa(name, shp, dt=F32): return es.enter_context(nc.psum_tensor("L2_" + name, shp, dt))
    xtm = [sb(f"xtm{i}", [128, INTER], BF16) for i in range(8)]
    xpre_s = [sb(f"xpre{i}", [128, TOK], BF16) for i in range(32)]
    B_s = sb("B_s", [128, TOK], BF16); C_s = sb("C_s", [128, TOK], BF16)
    Btm = [sb(f"Btm{i}", [128, 128], BF16) for i in range(8)]
    acum_s = sb("acum_s", [64, TOK])
    acumT = [sb(f"acumT{i}", [128, 64]) for i in range(8)]
    dec_fm = sb("dec_fm", [64, TOK])
    decT = [sb(f"decT{i}", [128, 64]) for i in range(8)]
    rowbuf = sb("rowbuf", [1, TOK])
    dchan_s = sb("dchan_s", [128, 32])
    mask_s = sb("mask_s", [128, 2 * CH])
    eye_s = sb("eye_s", [128, 128], BF16)
    eyef_s = sb("eyef_s", [128, 128])
    ones_row1 = sb("ones_row1", [1, 128])
    g0m = [sb(f"g0m{i}", [128, CH], BF16) for i in range(2)]
    argb = [sb(f"argb{i}", [128, CH]) for i in range(2)]
    Lb = [sb(f"Lb{i}", [128, CH], BF16) for i in range(2)]
    Mb = [sb(f"Mb{i}", [128, CH], BF16) for i in range(2)]
    Bd = [sb(f"Bd{i}", [128, 128], BF16) for i in range(2)]
    ybf = [sb(f"ybf{i}", [64, CH]) for i in range(2)]
    stch = sb("stch", [128, INTER])
    trv = [sb(f"trv{i}", [128, 128], BF16) for i in range(2)]
    trvf = [sb(f"trvf{i}", [128, 64]) for i in range(2)]

    ps_tr = psa("ps_tr", [128, 128], BF16)
    ps_trf = psa("ps_trf", [128, 64])
    ps_g0 = psa("ps_g0", [128, CH])
    ps_bc = psa("ps_bc", [128, CH])
    ps_y = [psa(f"ps_y{i}", [64, CH]) for i in range(2)]
    ps_st = [psa(f"ps_st{i}", [128, 64]) for i in range(2)]

    if True:
        pass
        for i in range(8):
            t.step("sync", lambda e, i=i: e.dma_start(out=xtm[i][:], in_=i_xdtm[i*128:(i+1)*128, :]),
                   make=[f"xtm{i}"], dma=True)
        for i in range(32):
            t.step("sync", lambda e, i=i: e.dma_start(out=xpre_s[i][:], in_=i_xpre[i*128:(i+1)*128, :]),
                   make=[f"xpre{i}"], dma=True)
        t.step("gpsimd", lambda e: e.dma_start(out=B_s[:], in_=i_B), make=["B"], dma=True)
        t.step("gpsimd", lambda e: e.dma_start(out=C_s[:], in_=i_C), make=["C"], dma=True)
        t.step("gpsimd", lambda e: e.dma_start(out=acum_s[:], in_=i_acum), make=["acum"], dma=True)
        t.step("gpsimd", lambda e: e.dma_start(out=dchan_s[:], in_=i_dchan), make=["dchan"], dma=True)
        t.step("gpsimd", lambda e: e.dma_start(out=mask_s[:], in_=i_mask), make=["mask"], dma=True)
        t.step("gpsimd", lambda e: e.dma_start(out=eye_s[:], in_=i_eye), make=["eye"], dma=True)
        t.step("gpsimd", lambda e: e.dma_start(out=eyef_s[:], in_=i_eye), make=["eyef"], dma=True)
        t.step("gpsimd", lambda e: e.memset(ones_row1[:], 1.0), make=["ones1"])
        for c in range(NCH):
            t.step("scalar", lambda e, c=c: e.activation(
                    dec_fm[:, c*CH:(c+1)*CH], acum_s[:, c*CH:(c+1)*CH], AF.Exp,
                    bias=acum_s[:, (c+1)*CH-1:(c+1)*CH], scale=-1.0),
                   use=["acum"], make=["dec"])
        for i in range(8):
            t.step("tensor", lambda e, i=i: e.transpose(ps_trf[:], acum_s[0:64, i*128:(i+1)*128], eyef_s[0:64, 0:64]),
                   use=["acum", "eyef"], make=["ps_trf"])
            t.step("vector", lambda e, i=i: e.tensor_copy(acumT[i][:], ps_trf[:]),
                   use=["ps_trf"], make=[f"acumT{i}"])
            t.step("tensor", lambda e, i=i: e.transpose(ps_trf[:], dec_fm[0:64, i*128:(i+1)*128], eyef_s[0:64, 0:64]),
                   use=["dec", "eyef"], make=["ps_trf"])
            t.step("vector", lambda e, i=i: e.tensor_copy(decT[i][:], ps_trf[:]),
                   use=["ps_trf"], make=[f"decT{i}"])
            t.step("tensor", lambda e, i=i: e.transpose(ps_tr[:], B_s[:, i*128:(i+1)*128], eye_s[:]),
                   use=["B", "eye"], make=["ps_tr"])
            t.step("vector", lambda e, i=i: e.tensor_copy(Btm[i][:], ps_tr[:]),
                   use=["ps_tr"], make=[f"Btm{i}"])
        for c in range(NCH):
            st0 = 2 * c
            for s in range(2):
                t.step("tensor", lambda e, c=c, s=s: e.matmul(
                        ps_g0[:], B_s[:, c*CH+s*128:c*CH+(s+1)*128], C_s[:, c*CH:(c+1)*CH],
                        start=True, stop=True),
                       use=["B", "C"], make=["ps_g0"])
                t.step("vector", lambda e, s=s: e.tensor_mul(g0m[s][:], ps_g0[:], mask_s[:, s*CH:(s+1)*CH]),
                       use=["ps_g0", "mask"], make=[f"g0m{s}"])
            for h in range(NH):
                t.step("gpsimd", lambda e, h=h: e.dma_start(out=rowbuf[:], in_=acum_s[h:h+1, :]),
                       use=["acum"], make=["rowb"], dma=True)
                t.step("tensor", lambda e, c=c: e.matmul(
                        ps_bc[:], ones_row1[0:1, :], rowbuf[0:1, c*CH:(c+1)*CH], start=True, stop=True),
                       use=["ones1", "rowb"], make=["ps_bc"])
                for s in range(2):
                    tt = st0 + s
                    t.step("vector", lambda e, s=s, tt=tt, h=h: e.scalar_tensor_tensor(
                            argb[s][:], ps_bc[:], acumT[tt][:, h:h+1], mask_s[:, s*CH:(s+1)*CH],
                            ALU.subtract, ALU.mult),
                           use=["ps_bc", f"acumT{tt}", "mask"], make=[f"arg{s}"])
                    t.step("scalar", lambda e, s=s: e.activation(Lb[s][:], argb[s][:], AF.Exp),
                           use=[f"arg{s}"], make=[f"L{s}"])
                    t.step("vector", lambda e, s=s: e.tensor_mul(Mb[s][:], Lb[s][:], g0m[s][:]),
                           use=[f"L{s}", f"g0m{s}"], make=[f"M{s}"])
                pyi = h % 2
                for s in range(2):
                    tt = st0 + s
                    t.step("tensor", lambda e, s=s, tt=tt, h=h, pyi=pyi: e.matmul(
                            ps_y[pyi][:], xtm[tt][:, h*64:(h+1)*64], Mb[s][:],
                            start=(s == 0), stop=(s == 1)),
                           use=[f"xtm{tt}", f"M{s}"], make=[f"ps_y{pyi}"])
                ctile = h // 2; cbase = (h % 2) * 64
                t.step("vector", lambda e, h=h, c=c, pyi=pyi, ctile=ctile, cbase=cbase: e.scalar_tensor_tensor(
                        ybf[pyi][:], xpre_s[ctile][cbase:cbase+64, c*CH:(c+1)*CH],
                        dchan_s[cbase:cbase+64, ctile:ctile+1], ps_y[pyi][:],
                        ALU.mult, ALU.add),
                       use=[f"xpre{ctile}", "dchan", f"ps_y{pyi}"], make=[f"ybf{pyi}"])
                t.step("sync", lambda e, h=h, c=c, pyi=pyi: e.dma_start(
                        out=o_ypart[h*64:(h+1)*64, c*CH:(c+1)*CH], in_=ybf[pyi][:]),
                       use=[f"ybf{pyi}"], dma=True)
                psi = h % 2
                for s in range(2):
                    tt = st0 + s
                    t.step("vector", lambda e, s=s, tt=tt, h=h: e.tensor_scalar_mul(
                            Bd[s][:], Btm[tt][:], decT[tt][:, h:h+1]),
                           use=[f"Btm{tt}", f"decT{tt}"], make=[f"Bd{s}"])
                    t.step("tensor", lambda e, s=s, tt=tt, h=h, psi=psi: e.matmul(
                            ps_st[psi][:], Bd[s][:], xtm[tt][:, h*64:(h+1)*64],
                            start=(s == 0), stop=(s == 1)),
                           use=[f"Bd{s}", f"xtm{tt}"], make=[f"ps_st{psi}"])
                t.step("vector", lambda e, h=h, psi=psi: e.tensor_copy(stch[:, h*64:(h+1)*64], ps_st[psi][:]),
                       use=[f"ps_st{psi}"], make=["stch"])
            t.step("sync", lambda e, c=c: e.dma_start(out=o_states[c], in_=stch[:]),
                   use=["stch"], dma=True)
            t.step("sync", lambda e, c=c: e.dma_start(out=o_T[:, c:c+1], in_=acum_s[:, (c+1)*CH-1:(c+1)*CH], allow_slow_non_contiguous=True),
                   use=["acum"], dma=True)
    es.close()





"""Launch 4: MoE expert-parallel. Core c = group c, all 8192 tokens."""


S, D, E, I2, NG = 8192, 2048, 32, 1536, 8
TC = 512
NTC = S // TC

def emit_launch4(fctx):
    nc, t, IO = fctx
    i_xn = IO["xnfull"]
    i_gwT = IO["gwT"]
    i_gupT = IO["gupT"]
    i_dnT = IO["dnT"]
    i_mg = IO["mgf"]
    i_de = IO["dei"]
    i_cid = IO["cid"]
    i_cidf = IO["cidf"]
    i_eye = IO["eye"]
    gusc = nc.dram_tensor("gusc", [D, I2], F32)
    dnsc = nc.dram_tensor("dnsc", [768, D], F32)
    o_contrib = IO["o_contrib"]

    es = ExitStack()
    def sb(name, shp, dt=F32): return es.enter_context(nc.sbuf_tensor("L4_" + name, shp, dt))
    def psa(name, shp, dt=F32): return es.enter_context(nc.psum_tensor("L4_" + name, shp, dt))
    xn_s = [sb(f"xn{k}", [128, TC], BF16) for k in range(16)]
    gw_s = sb("gw_s", [128, 16 * E], BF16)
    mg_s = sb("mg_s", [1, E]); cid_s = sb("cid_s", [1, 1], I32); cidf_s = sb("cidf_s", [1, 1])
    de_s = sb("de_s", [1, NG], I32)
    gsel = sb("gsel", [1, E]); gselB = sb("gselB", [128, E], BF16)
    probs = sb("probs", [128, E]); porig = sb("porig", [128, E])
    prow = sb("prow", [128, 1]); psum_r = sb("psum_r", [128, 1])
    mcol = sb("mcol", [128, 1]); scol = sb("scol", [128, 1]); rscol = sb("rscol", [128, 1])
    maskb = sb("maskb", [128, E]); ptop = sb("ptop", [128, E], BF16)
    mask8 = sb("mask8", [128, E], BF16)
    wcol = sb("wcol", [128, 1]); wrow = sb("wrow", [1, S])
    ones_row1 = sb("ones_row1", [1, 128]); ones_colb = sb("ones_colb", [128, 1], BF16)
    eye_s = sb("eye_s", [128, 128])
    guw = sb("guw", [128, 16 * I2], BF16)
    dnw = sb("dnw", [128, 6 * D], BF16)
    sg_sb = [sb(f"sg{i}", [128, TC], BF16) for i in range(6)]
    up_sb = [sb(f"up{i}", [128, TC], BF16) for i in range(6)]
    act_sb = [sb(f"act{i}", [128, TC], BF16) for i in range(6)]
    ev_sb = [sb(f"ev{i}", [128, TC]) for i in range(2)]

    ps_h = [psa(f"ps_h{i}", [128, TC]) for i in range(4)]
    ps_dn = [psa(f"ps_dn{i}", [128, TC]) for i in range(2)]
    gsem = es.enter_context(nc.semaphore("gsem"))

    if True:
        pass
        t.step("gpsimd", lambda e: e.dma_start(out=mg_s[:], in_=i_mg), make=["mg"], dma=True)
        t.step("gpsimd", lambda e: e.dma_start(out=cid_s[:], in_=i_cid), make=["cid"], dma=True)
        t.step("gpsimd", lambda e: e.dma_start(out=cidf_s[:], in_=i_cidf), make=["cidf"], dma=True)
        t.step("gpsimd", lambda e: e.dma_start(out=de_s[:], in_=i_de), make=["de"], dma=True)
        t.step("gpsimd", lambda e: e.dma_start(out=eye_s[:], in_=i_eye), make=["eye"], dma=True)
        src = i_gwT.rearrange("(k p) e -> p k e", p=128)
        dst = gw_s[:].rearrange("p (k e) -> p k e", k=16)
        t.step("gpsimd", lambda e: e.dma_start(out=dst, in_=src), make=["gw"], dma=True)
        t.step("gpsimd", lambda e: e.memset(ones_row1[:], 1.0), make=["ones1"])
        t.step("gpsimd", lambda e: e.memset(ones_colb[:], 1.0), make=["onesc"])
        t.step("vector", lambda e: e.tensor_single_scalar(gsel[:], mg_s[:], cidf_s[0:1, 0:1], ALU.is_equal),
               use=["mg", "cidf"], make=["gsel"])
        t.step("tensor", lambda e: e.matmul(ps_h[0][0:128, 0:E], ones_row1[0:1, :], gsel[0:1, :], start=True, stop=True),
               use=["ones1", "gsel"], make=["ps_h0"])
        t.step("vector", lambda e: e.tensor_copy(gselB[:], ps_h[0][0:128, 0:E]),
               use=["ps_h0"], make=["gselB"])
        def gather(e):
            last = None
            with e.register("cidr") as cidr, e.register("der") as der:
                e.reg_load(cidr, cid_s[0:1, 0:1])
                for g in range(NG):
                    with e.If_eq(cidr, g):
                        e.reg_load(der, de_s[0:1, g:g+1])
                for ex in range(E):
                    with e.If_eq(der, ex):
                        for q in range(4):
                            e.dma_start(out=gusc[q*512:(q+1)*512, :], in_=i_gupT[ex, q*512:(q+1)*512, :]).then_inc(gsem, 16)
                        for q in range(2):
                            last = e.dma_start(out=dnsc[q*384:(q+1)*384, :], in_=i_dnT[ex, q*384:(q+1)*384, :]).then_inc(gsem, 16)
            return last
        t.step("gpsimd", gather, use=["cid", "de"], raw=True, tag="gather")
        t.step("gpsimd", lambda e: e.wait_ge(gsem, 96), raw=True, tag="gwait")
        for k in range(16):
            t.step("gpsimd", lambda e, k=k: e.dma_start(out=guw[:, k*I2:(k+1)*I2], in_=gusc[k*128:(k+1)*128, :]),
                   make=["guw"], dma=True)
        for k in range(6):
            t.step("gpsimd", lambda e, k=k: e.dma_start(out=dnw[:, k*D:(k+1)*D], in_=dnsc[k*128:(k+1)*128, :]),
                   make=["dnw"], dma=True)

        for tc in range(NTC):
            for k in range(16):
                t.step("sync", lambda e, k=k, tc=tc: e.dma_start(out=xn_s[k][:], in_=i_xn[k*128:(k+1)*128, tc*TC:(tc+1)*TC]),
                       make=[f"xn{k}"], dma=True)
            for tt in range(4):
                for k in range(16):
                    t.step("tensor", lambda e, k=k, tt=tt: e.matmul(
                            ps_h[1][0:128, 0:E], xn_s[k][:, tt*128:(tt+1)*128],
                            gw_s[:].rearrange("p (k e) -> p k e", k=16)[:, k, :],
                            start=(k == 0), stop=(k == 15)),
                           use=[f"xn{k}", "gw"], make=["ps_h1"])
                t.step("vector", lambda e: e.reduce_max(prow[:], ps_h[1][0:128, 0:E], AX.X),
                       use=["ps_h1"], make=["prow"])
                t.step("vector", lambda e: e.tensor_scalar_mul(prow[:], prow[:], -1.0),
                       use=["prow"], make=["prow"])
                t.step("scalar", lambda e: e.activation(probs[:], ps_h[1][0:128, 0:E], AF.Exp, bias=prow[:, 0:1]),
                       use=["ps_h1", "prow"], make=["probs"])
                t.step("vector", lambda e: e.reduce_sum(psum_r[:], probs[:], AX.X),
                       use=["probs"], make=["psr"])
                t.step("vector", lambda e: e.reciprocal(psum_r[:], psum_r[:]), use=["psr"], make=["psr"])
                t.step("vector", lambda e: e.tensor_scalar_mul(probs[:], probs[:], psum_r[:, 0:1]),
                       use=["probs", "psr"], make=["probs"])
                t.step("vector", lambda e: e.tensor_copy(porig[:], probs[:]), use=["probs"], make=["porig"])
                t.step("gpsimd", lambda e: e.memset(scol[:], 0.0), make=["scol"])
                for it in range(8):
                    t.step("vector", lambda e: e.reduce_max(mcol[:], probs[:], AX.X),
                           use=["probs"], make=["mcol"])
                    t.step("vector", lambda e: e.tensor_add(scol[:], scol[:], mcol[:]),
                           use=["scol", "mcol"], make=["scol"])
                    t.step("vector", lambda e: e.tensor_scalar(maskb[:], probs[:], mcol[:, 0:1], None, ALU.is_ge),
                           use=["probs", "mcol"], make=["maskb"])
                    t.step("vector", lambda e: e.scalar_tensor_tensor(probs[:], maskb[:], 1e30, probs[:], ALU.mult, ALU.subtract),
                           use=["maskb", "probs"], make=["probs"]) if False else None
                    t.step("vector", lambda e: e.scalar_tensor_tensor(probs[:], maskb[:], -1e30, probs[:], ALU.mult, ALU.add),
                           use=["maskb", "probs"], make=["probs"])
                t.step("vector", lambda e: e.tensor_single_scalar(mask8[:], probs[:], -1e29, ALU.is_lt),
                       use=["probs"], make=["mask8"])
                t.step("vector", lambda e: e.reciprocal(rscol[:], scol[:]), use=["scol"], make=["rscol"])
                t.step("vector", lambda e: e.scalar_tensor_tensor(ptop[:], porig[:], rscol[:, 0:1], mask8[:], ALU.mult, ALU.mult),
                       use=["porig", "rscol", "mask8"], make=["ptop"])
                t.step("vector", lambda e: e.tensor_tensor_reduce(wcol[:], ptop[:], gselB[:], ALU.mult),
                       use=["ptop", "gselB"], make=["wcol"]) if hasattr(nc.vector, "tensor_tensor_reduce_XXX") else \
                t.step("vector", lambda e: e.tensor_mul(maskb[:], ptop[:], gselB[:]), use=["ptop", "gselB"], make=["maskb"])
                t.step("vector", lambda e: e.reduce_sum(wcol[:], maskb[:], AX.X), use=["maskb"], make=["wcol"])
                t.step("tensor", lambda e: e.transpose(ps_h[2][0:1, 0:128], wcol[:], eye_s[:]),
                       use=["wcol", "eye"], make=["ps_h2"])
                t.step("scalar", lambda e, tc=tc, tt=tt: e.activation(wrow[0:1, tc*TC+tt*128:tc*TC+(tt+1)*128], ps_h[2][0:1, 0:128], AF.Copy),
                       use=["ps_h2"], make=["wrow"])
            t.step("tensor", lambda e, tc=tc: e.matmul(ps_h[3][:], ones_row1[0:1, :], wrow[0:1, tc*TC:(tc+1)*TC], start=True, stop=True),
                   use=["ones1", "wrow"], make=["ps_h3"])
            for og in range(6):
                for k in range(16):
                    for to in range(2):
                        ot = og * 2 + to
                        t.step("tensor", lambda e, k=k, ot=ot, to=to: e.matmul(
                                ps_h[to][:], guw[:, k*I2 + ot*128: k*I2 + (ot+1)*128],
                                xn_s[k][:], start=(k == 0), stop=(k == 15)),
                               use=["guw", f"xn{k}"], make=[f"ps_h{to}"])
                for to in range(2):
                    ot = og * 2 + to
                    if ot < 6:
                        t.step("scalar", lambda e, ot=ot, to=to: e.activation(sg_sb[ot][:], ps_h[to][:], AF.Silu),
                               use=[f"ps_h{to}"], make=[f"sg{ot}"])
                    else:
                        t.step("vector", lambda e, ot=ot, to=to: e.tensor_copy(up_sb[ot-6][:], ps_h[to][:]),
                               use=[f"ps_h{to}"], make=[f"up{ot-6}"])
            for u in range(6):
                t.step("vector", lambda e, u=u: e.tensor_mul(act_sb[u][:], sg_sb[u][:], up_sb[u][:]),
                       use=[f"sg{u}", f"up{u}"], make=[f"act{u}"])
                t.step("vector", lambda e, u=u: e.tensor_mul(act_sb[u][:], act_sb[u][:], ps_h[3][:]),
                       use=[f"act{u}", "ps_h3"], make=[f"act{u}"])
            for og in range(8):
                for k in range(6):
                    for to in range(2):
                        ot2 = og * 2 + to
                        t.step("tensor", lambda e, k=k, ot2=ot2, to=to: e.matmul(
                                ps_dn[to][:], dnw[:, k*D + ot2*128: k*D + (ot2+1)*128],
                                act_sb[k][:], start=(k == 0), stop=(k == 5)),
                               use=["dnw", f"act{k}"], make=[f"ps_dn{to}"])
                for to in range(2):
                    ot = og * 2 + to
                    t.step("vector", lambda e, to=to: e.tensor_copy(ev_sb[to][:], ps_dn[to][:]),
                           use=[f"ps_dn{to}"], make=[f"ev{to}"])
                    t.step("sync", lambda e, ot=ot, tc=tc, to=to: e.dma_start(
                            out=o_contrib[ot*128:(ot+1)*128, tc*TC:(tc+1)*TC], in_=ev_sb[to][:]),
                           use=[f"ev{to}"], dma=True)
    es.close()





"""Launch 6: x1f = x1 + sum(contribs); LN -> xn1; Q/K (RoPE) feature-major; V token-major."""


TOK, D, KT = 1024, 2048, 16

def emit_launch6(fctx):
    nc, t, IO = fctx
    i_x1 = IO["x1own"]
    i_cn = IO["contribs"]
    i_wq = IO["WqT"]
    i_wk = IO["WkT"]
    i_wv = IO["WvT"]
    i_cos = IO["cosm"]
    i_sin = IO["sinm"]
    o_x1f = IO["o_x1f"]
    o_xn1 = IO["o_xn1"]
    o_q = IO["o_q"]
    o_k = IO["o_k"]
    o_v = IO["o_v"]

    es = ExitStack()
    def sb(name, shp, dt=F32): return es.enter_context(nc.sbuf_tensor("L6_" + name, shp, dt))
    def psa(name, shp, dt=F32): return es.enter_context(nc.psum_tensor("L6_" + name, shp, dt))
    x1_s = [sb(f"x1_{k}", [128, TOK]) for k in range(KT)]
    xb_s = [sb(f"xb{k}", [128, TOK], BF16) for k in range(KT)]
    cbuf = [sb(f"cbuf{i}", [128, TOK]) for i in range(2)]
    x2_s = [sb(f"x2_{i}", [128, TOK], BF16) for i in range(2)]
    tmpa = sb("tmpa", [128, TOK]); tmpb = sb("tmpb", [128, TOK])
    ones_col = sb("ones_col", [128, 1], BF16); ones_row1 = sb("ones_row1", [1, 128])
    statm = sb("statm", [1, TOK]); stat2 = sb("stat2", [1, TOK])
    statv = sb("statv", [1, TOK]); statr = sb("statr", [1, TOK])
    eps1 = sb("eps1", [1, 1])
    cos_s = sb("cos_s", [128, TOK]); sin_s = sb("sin_s", [128, TOK])
    wbuf = [sb(f"wbuf{i}", [128, 256], BF16) for i in range(2)]
    ra = sb("ra", [64, 512]); rb = sb("rb", [64, 512])
    qbf = sb("qbf", [128, TOK], BF16)
    vbf = [sb(f"vbf{i}", [128, 512], BF16) for i in range(2)]

    ps_a = psa("ps_a", [128, TOK]); ps_b = psa("ps_b", [128, TOK])
    ps_p = [psa("ps_p0", [128, TOK])]
    ps_v = psa("ps_v", [128, 512])

    if True:
        pass
        t.step("gpsimd", lambda e: e.dma_start(out=cos_s[:], in_=i_cos), make=["cos"], dma=True)
        t.step("gpsimd", lambda e: e.dma_start(out=sin_s[:], in_=i_sin), make=["sin"], dma=True)
        t.step("gpsimd", lambda e: e.memset(ones_col[:], 1.0), make=["onesc"])
        t.step("gpsimd", lambda e: e.memset(ones_row1[:], 1.0), make=["ones1"])
        t.step("gpsimd", lambda e: e.memset(eps1[:], 1e-5), make=["eps1"])
        for k in range(KT):
            t.step("sync", lambda e, k=k: e.dma_start(out=x1_s[k][:], in_=i_x1[k*128:(k+1)*128, :]),
                   make=[f"x1_{k}"], dma=True)
            for g in range(1):
                cb = cbuf[g % 2]
                t.step("sync", lambda e, k=k, g=g, cb=cb: e.dma_start(out=cb[:], in_=i_cn[g, k*128:(k+1)*128, :]),
                       make=[f"cb{g%2}"], dma=True)
                t.step("vector", lambda e, k=k, cb=cb, g=g: e.tensor_add(x1_s[k][:], x1_s[k][:], cb[:]),
                       use=[f"x1_{k}", f"cb{g%2}"], make=[f"x1_{k}"])
            t.step("sync", lambda e, k=k: e.dma_start(out=o_x1f[k*128:(k+1)*128, :], in_=x1_s[k][:]),
                   use=[f"x1_{k}"], dma=True)
        for k in range(KT):
            t.step("vector", lambda e, k=k: e.tensor_copy(xb_s[k][:], x1_s[k][:]),
                   use=[f"x1_{k}"], make=[f"xb{k}"])
            xi = k % 2
            t.step("scalar", lambda e, k=k, xi=xi: e.activation(x2_s[xi][:], xb_s[k][:], AF.Square),
                   use=[f"xb{k}"], make=[f"x2_{xi}"])
            for (o, n) in [(0, 512), (512, 512)]:
                t.step("tensor", lambda e, k=k, o=o, n=n: e.matmul(
                        ps_a[0:1, o:o+n], ones_col[:, 0:1], xb_s[k][:, o:o+n],
                        start=(k == 0), stop=(k == KT-1)),
                       use=["onesc", f"xb{k}"], make=["ps_a"])
                t.step("tensor", lambda e, k=k, o=o, n=n, xi=xi: e.matmul(
                        ps_b[0:1, o:o+n], ones_col[:, 0:1], x2_s[xi][:, o:o+n],
                        start=(k == 0), stop=(k == KT-1)),
                       use=["onesc", f"x2_{xi}"], make=["ps_b"])
        t.step("scalar", lambda e: e.activation(statm[:], ps_a[0:1, :], AF.Copy, scale=1.0/D),
               use=["ps_a"], make=["statm"])
        t.step("scalar", lambda e: e.activation(stat2[:], statm[:], AF.Square),
               use=["statm"], make=["stat2"])
        t.step("vector", lambda e: e.scalar_tensor_tensor(statv[:], ps_b[0:1, :], 1.0/D, stat2[:], ALU.mult, ALU.subtract),
               use=["ps_b", "stat2"], make=["statv"])
        t.step("scalar", lambda e: e.activation(statr[:], statv[:], AF.Sqrt, bias=eps1[0:1, 0:1]),
               use=["statv", "eps1"], make=["statr"])
        t.step("vector", lambda e: e.reciprocal(statr[:], statr[:]), use=["statr"], make=["statr"])
        for (o, n) in [(0, 512), (512, 512)]:
            t.step("tensor", lambda e, o=o, n=n: e.matmul(ps_a[:, o:o+n], ones_row1[0:1, :], statm[0:1, o:o+n], start=True, stop=True),
                   use=["ones1", "statm"], make=["ps_a"])
            t.step("tensor", lambda e, o=o, n=n: e.matmul(ps_b[:, o:o+n], ones_row1[0:1, :], statr[0:1, o:o+n], start=True, stop=True),
                   use=["ones1", "statr"], make=["ps_b"])
        for k in range(KT):
            tb = [tmpa, tmpb][k % 2]; tbn = f"tmp{k%2}"
            t.step("vector", lambda e, k=k, tb=tb: e.tensor_sub(tb[:], x1_s[k][:], ps_a[:]),
                   use=[f"x1_{k}", "ps_a"], make=[tbn])
            t.step("vector", lambda e, k=k, tb=tb: e.tensor_mul(xb_s[k][:], tb[:], ps_b[:]),
                   use=[tbn, "ps_b"], make=[f"xb{k}"])
            t.step("sync", lambda e, k=k: e.dma_start(out=o_xn1[k*128:(k+1)*128, :], in_=xb_s[k][:]),
                   use=[f"xb{k}"], dma=True)
        def proj_rope(wT, nrows, out_ap, name):
            nt = nrows // 128
            for ot in range(nt):
                pi = 0
                for k in range(KT):
                    wi = k % 2
                    t.step("gpsimd", lambda e, k=k, ot=ot, wi=wi, wT=wT: e.dma_start(
                            out=wbuf[wi][:, 0:128], in_=wT[k*128:(k+1)*128, ot*128:(ot+1)*128]),
                           make=[f"wb{wi}"], dma=True)
                    for (o, n) in [(0, 512), (512, 512)]:
                        t.step("tensor", lambda e, k=k, o=o, n=n, pi=pi, wi=wi: e.matmul(
                                ps_p[pi][:, o:o+n], wbuf[wi][:, 0:128], xb_s[k][:, o:o+n],
                                start=(k == 0), stop=(k == KT-1)),
                               use=[f"wb{wi}", f"xb{k}"], make=[f"ps_p{pi}"])
                for half in range(2):
                    hb, ob = (0, 64) if half else (64, 0)
                    ob = half * 64; hb = 64 - ob
                    for (o, n) in [(0, 512), (512, 512)]:
                        t.step("vector", lambda e, ob=ob, o=o, n=n, pi=pi: e.tensor_mul(
                                ra[:, 0:n], ps_p[pi][ob:ob+64, o:o+n], cos_s[ob:ob+64, o:o+n]),
                               use=[f"ps_p{pi}", "cos"], make=["ra"])
                        t.step("vector", lambda e, ob=ob, hb=hb, o=o, n=n, pi=pi: e.tensor_mul(
                                rb[:, 0:n], ps_p[pi][hb:hb+64, o:o+n], sin_s[ob:ob+64, o:o+n]),
                               use=[f"ps_p{pi}", "sin"], make=["rb"])
                        op = ALU.subtract if half == 0 else ALU.add
                        t.step("vector", lambda e, ob=ob, o=o, n=n, op=op: e.tensor_tensor(
                                qbf[ob:ob+64, o:o+n], ra[:, 0:n], rb[:, 0:n], op),
                               use=["ra", "rb"], make=["qbf"])
                t.step("sync", lambda e, ot=ot, out_ap=out_ap: e.dma_start(
                        out=out_ap[ot*128:(ot+1)*128, :], in_=qbf[:]),
                       use=["qbf"], dma=True)
        proj_rope(i_wq, 2048, o_q, "q")
        proj_rope(i_wk, 512, o_k, "k")
        for tt in range(8):
            for k in range(KT):
                t.step("gpsimd", lambda e, k=k: e.dma_start(out=wbuf[0][:, 0:256], in_=i_wv[k*128:(k+1)*128, 0:256]),
                       make=["wb0"], dma=True)
                t.step("gpsimd", lambda e, k=k: e.dma_start(out=wbuf[1][:, 0:256], in_=i_wv[k*128:(k+1)*128, 256:512]),
                       make=["wb1"], dma=True)
                t.step("tensor", lambda e, k=k, tt=tt: e.matmul(
                        ps_v[:, 0:256], xb_s[k][:, tt*128:(tt+1)*128], wbuf[0][:, 0:256],
                        start=(k == 0), stop=False),
                       use=["wb0", f"xb{k}"], make=["ps_v"])
                t.step("tensor", lambda e, k=k, tt=tt: e.matmul(
                        ps_v[:, 256:512], xb_s[k][:, tt*128:(tt+1)*128], wbuf[1][:, 0:256],
                        start=False, stop=(k == KT-1)),
                       use=["wb1", f"xb{k}"], make=["ps_v"])
            vb = vbf[tt % 2]
            t.step("vector", lambda e, vb=vb: e.tensor_copy(vb[:], ps_v[:]),
                   use=["ps_v"], make=[f"vb{tt%2}"])
            t.step("sync", lambda e, tt=tt, vb=vb: e.dma_start(out=o_v[tt*128:(tt+1)*128, :], in_=vb[:]),
                   use=[f"vb{tt%2}"], dma=True)
    es.close()





"""Launch 7: attention (flash, bound-stabilized, full-key-range uniform w/ mask) + Wo."""


TOK, D, S, H, KVH, DH = 1024, 2048, 8192, 16, 4, 128
QC = 512
NKT = S // 128
SCALE = float(DH ** -0.5)

def emit_launch7(fctx):
    nc, t, IO = fctx
    i_q = IO["qown"]
    i_k = IO["Kf"]
    i_v = IO["Vtm"]
    i_mask = IO["maskf"]
    i_wo = IO["WoT"]
    attnraw = nc.dram_tensor("attnraw", [2048, TOK], BF16)
    o_attnp = IO["o_attnp"]

    es = ExitStack()
    def sb(name, shp, dt=F32): return es.enter_context(nc.sbuf_tensor("L7_" + name, shp, dt))
    def psa(name, shp, dt=F32): return es.enter_context(nc.psum_tensor("L7_" + name, shp, dt))
    kbuf = [sb("kbuf0", [128, S], BF16)]
    V_s = [sb(f"V{i}", [128, 512], BF16) for i in range(NKT)]
    q_s = sb("q_s", [128, QC * H], BF16)
    mk_s = [sb(f"mk{i}", [128, QC], BF16) for i in range(NKT)]
    sq_b = sb("sq_b", [128, 512], BF16)
    nrow = sb("nrow", [1, 512])
    ntmp = sb("ntmp", [1, 1])
    one1 = sb("one1", [1, 1]); sc1 = sb("sc1", [1, 1])
    ones_col = sb("ones_col", [128, 1], BF16); ones_row1 = sb("ones_row1", [1, 128])
    negm = [sb(f"negm{i}", [128, 1]) for i in range(H)]
    qmax2 = sb("qmax2", [1, H]); kmax2 = sb("kmax2", [1, KVH])
    mh = sb("mh", [1, 1])
    Pb = [sb(f"Pb{i}", [128, QC], BF16) for i in range(2)]
    Pm = [sb(f"Pm{i}", [128, QC], BF16) for i in range(2)]
    srow = sb("srow", [1, QC])
    oev = sb("oev", [128, QC], BF16)
    abf = sb("abf", [128, QC], BF16)
    attr_s = [sb(f"attr{i}", [128, TOK], BF16) for i in range(16)]
    wbuf = [sb(f"wbuf{i}", [128, 128], BF16) for i in range(2)]
    evp = [sb("evp0", [128, 512], BF16)]

    ps_s = [psa(f"ps_s{i}", [128, QC]) for i in range(2)]
    ps_o = psa("ps_o", [128, QC])
    ps_sum = psa("ps_sum", [1, QC])
    ps_bc = psa("ps_bc", [128, QC])
    ps_n = psa("ps_n", [1, 512])
    ps_w = [psa(f"ps_w{i}", [128, 512]) for i in range(2)]

    if True:
        pass
        for i in range(NKT):
            t.step("sync", lambda e, i=i: e.dma_start(out=V_s[i][:], in_=i_v[i*128:(i+1)*128, :]),
                   make=[f"V{i}"], dma=True)
        t.step("gpsimd", lambda e: e.memset(ones_col[:], 1.0), make=["onesc"])
        t.step("gpsimd", lambda e: e.memset(ones_row1[:], 1.0), make=["ones1"])
        for kv in range(KVH):
            for qch in range(S // 512):
                t.step("sync", lambda e, kv=kv, qch=qch: e.dma_start(
                        out=sq_b[:, 0:512], in_=i_k[kv*128:(kv+1)*128, qch*512:(qch+1)*512]),
                       make=["sqb"], dma=True)
                t.step("scalar", lambda e: e.activation(sq_b[:, 0:512], sq_b[:, 0:512], AF.Square),
                       use=["sqb"], make=["sqb"])
                t.step("tensor", lambda e, qch=qch: e.matmul(ps_n[:], ones_col[:, 0:1],
                        sq_b[:, 0:512], start=True, stop=True),
                       use=["onesc", "sqb"], make=["ps_n"])
                t.step("scalar", lambda e, qch=qch: e.activation(nrow[0:1, 0:512], ps_n[:], AF.Copy),
                       use=["ps_n"], make=["nrow"])
                t.step("vector", lambda e: e.reduce_max(ntmp[:], nrow[:], AX.X),
                       use=["nrow"], make=["ntmp"])
                if qch == 0:
                    t.step("vector", lambda e, kv=kv: e.tensor_copy(kmax2[0:1, kv:kv+1], ntmp[:]),
                           use=["ntmp"], make=["kmax2"])
                else:
                    t.step("vector", lambda e, kv=kv: e.tensor_max(kmax2[0:1, kv:kv+1], kmax2[0:1, kv:kv+1], ntmp[:]),
                           use=["ntmp", "kmax2"], make=["kmax2"])
        for h in range(H):
            for qch in range(2):
                t.step("gpsimd", lambda e, h=h, qch=qch: e.dma_start(out=sq_b[:, 0:512], in_=i_q[h*128:(h+1)*128, qch*512:(qch+1)*512]),
                       make=["sqb"], dma=True)
                t.step("scalar", lambda e: e.activation(sq_b[:, 0:512], sq_b[:, 0:512], AF.Square),
                       use=["sqb"], make=["sqb"])
                t.step("tensor", lambda e: e.matmul(ps_n[:], ones_col[:, 0:1],
                        sq_b[:, 0:512], start=True, stop=True),
                       use=["onesc", "sqb"], make=["ps_n"])
                t.step("scalar", lambda e, qch=qch: e.activation(nrow[0:1, 0:512], ps_n[:], AF.Copy),
                       use=["ps_n"], make=["nrow"])
                t.step("vector", lambda e: e.reduce_max(ntmp[:], nrow[:], AX.X),
                       use=["nrow"], make=["ntmp"])
                if qch == 0:
                    t.step("vector", lambda e, h=h: e.tensor_copy(qmax2[0:1, h:h+1], ntmp[:]),
                           use=["ntmp"], make=["qmax2"])
                else:
                    t.step("vector", lambda e, h=h: e.tensor_max(qmax2[0:1, h:h+1], qmax2[0:1, h:h+1], ntmp[:]),
                           use=["ntmp", "qmax2"], make=["qmax2"])
            kv = h // 4
            t.step("vector", lambda e, h=h, kv=kv: e.tensor_mul(mh[:], qmax2[0:1, h:h+1], kmax2[0:1, kv:kv+1]),
                   use=["qmax2", "kmax2"], make=["mh"])
            t.step("scalar", lambda e: e.activation(mh[:], mh[:], AF.Sqrt), use=["mh"], make=["mh"])
            t.step("vector", lambda e: e.tensor_scalar_mul(mh[:], mh[:], -SCALE), use=["mh"], make=["mh"])
            t.step("tensor", lambda e: e.matmul(ps_bc[:, 0:1], ones_row1[0:1, :], mh[0:1, 0:1], start=True, stop=True),
                   use=["ones1", "mh"], make=["ps_bc"])
            t.step("vector", lambda e, h=h: e.tensor_copy(negm[h][:], ps_bc[:, 0:1]),
                   use=["ps_bc"], make=[f"negm{h}"])
        for qc in range(TOK // QC):
            t.step("gpsimd", lambda e, qc=qc: e.dma_start(
                    out=q_s[:].rearrange("p (h q) -> p h q", h=H),
                    in_=i_q.rearrange("(h p) t -> p h t", p=128)[:, :, qc*QC:(qc+1)*QC]),
                   make=["q_s"], dma=True)
            for i in range(NKT):
                t.step("sync", lambda e, i=i, qc=qc: e.dma_start(out=mk_s[i][:], in_=i_mask[i*128:(i+1)*128, qc*QC:(qc+1)*QC]),
                       make=[f"mk{i}"], dma=True)
            for kv in range(KVH):
              kb = kbuf[0]
              for rr in range(8):
                  t.step("sync", lambda e, kv=kv, kb=kb, rr=rr: e.dma_start(
                          out=kb[:, rr*1024:(rr+1)*1024], in_=i_k[kv*128:(kv+1)*128, rr*1024:(rr+1)*1024]),
                         make=["kb0"], dma=True)
              for h in range(4*kv, 4*kv+4):
                for kt in range(NKT):
                    si = kt % 2
                    t.step("tensor", lambda e, kt=kt, h=h, kv=kv, si=si, qc=qc, kb=kb: e.matmul(
                            ps_s[si][:], kb[:, kt*128:(kt+1)*128],
                            q_s[:].rearrange("p (h q) -> p h q", h=H)[:, h, :],
                            start=True, stop=True),
                           use=["kb0", "q_s"], make=[f"ps_s{si}"])
                    t.step("scalar", lambda e, si=si, h=h: e.activation(
                            Pb[si][:], ps_s[si][:], AF.Exp, bias=negm[h][:, 0:1], scale=SCALE),
                           use=[f"ps_s{si}", f"negm{h}"], make=[f"Pb{si}"])
                    t.step("vector", lambda e, kt=kt, si=si: e.tensor_mul(Pm[si][:], Pb[si][:], mk_s[kt][:]),
                           use=[f"Pb{si}", f"mk{kt}"], make=[f"Pm{si}"])
                    t.step("tensor", lambda e, kt=kt, si=si: e.matmul(
                            ps_sum[:], ones_col[:, 0:1], Pm[si][:],
                            start=(kt == 0), stop=(kt == NKT-1)),
                           use=["onesc", f"Pm{si}"], make=["ps_sum"])
                    t.step("tensor", lambda e, kt=kt, kv=kv, si=si: e.matmul(
                            ps_o[:], V_s[kt][:, kv*128:(kv+1)*128], Pm[si][:],
                            start=(kt == 0), stop=(kt == NKT-1)),
                           use=[f"V{kt}", f"Pm{si}"], make=["ps_o"])
                t.step("scalar", lambda e: e.activation(srow[:], ps_sum[:], AF.Copy),
                       use=["ps_sum"], make=["srow"])
                t.step("vector", lambda e: e.reciprocal(srow[:], srow[:]), use=["srow"], make=["srow"])
                t.step("tensor", lambda e: e.matmul(ps_bc[:], ones_row1[0:1, :], srow[0:1, :], start=True, stop=True),
                       use=["ones1", "srow"], make=["ps_bc"])
                t.step("vector", lambda e: e.tensor_copy(oev[:], ps_o[:]), use=["ps_o"], make=["oev"])
                t.step("vector", lambda e: e.tensor_mul(abf[:], oev[:], ps_bc[:]),
                       use=["oev", "ps_bc"], make=["abf"])
                t.step("sync", lambda e, h=h, qc=qc: e.dma_start(
                        out=attnraw.ap()[h*128:(h+1)*128, qc*QC:(qc+1)*QC], in_=abf[:]),
                       use=["abf"], dma=True)
        for k in range(16):
            ai = k % 16
            t.step("sync", lambda e, k=k: e.dma_start(out=attr_s[k][:], in_=attnraw.ap()[k*128:(k+1)*128, :]),
                   make=[f"attr{k}"], dma=True)
        for ot in range(16):
            for k in range(16):
                t.step("gpsimd", lambda e, k=k, ot=ot: e.dma_start(
                        out=wbuf[k % 2][:], in_=i_wo[k*128:(k+1)*128, ot*128:(ot+1)*128]),
                       make=[f"wbuf{k%2}"], dma=True)
                for ni, (o, n) in enumerate([(0, 512), (512, 512)]):
                    t.step("tensor", lambda e, k=k, o=o, n=n, ni=ni: e.matmul(
                            ps_w[ni][:], wbuf[k % 2][:], attr_s[k][:, o:o+n],
                            start=(k == 0), stop=(k == 15)),
                           use=[f"wbuf{k%2}", f"attr{k}"], make=[f"ps_w{ni}"])
            for ni, (o, n) in enumerate([(0, 512), (512, 512)]):
                t.step("vector", lambda e, ni=ni: e.tensor_copy(evp[0][:], ps_w[ni][:]),
                       use=[f"ps_w{ni}"], make=["evp0"])
                t.step("sync", lambda e, ot=ot, o=o, n=n: e.dma_start(
                        out=o_attnp[ot*128:(ot+1)*128, o:o+n], in_=evp[0][:]),
                       use=["evp0"], dma=True)
    es.close()



"""Launch 8: L1 FFN (W0/gelu-exact/W1 with f16 rounds) + final residual sum."""


TOK, D, N = 1024, 2048, 8192

def emit_launch8(fctx):
    nc, t, IO = fctx
    i_xn1 = IO["xn1"]
    i_w0 = IO["W0T"]
    i_b0 = IO["b0c"]
    i_w1 = IO["W1T"]
    i_b1 = IO["b1c"]
    i_x1f = IO["x1f"]
    i_att = IO["attnp"]
    o_out = IO["o_out"]

    es = ExitStack()
    def sb(name, shp, dt=F32): return es.enter_context(nc.sbuf_tensor("L8_" + name, shp, dt))
    def psa(name, shp, dt=F32): return es.enter_context(nc.psum_tensor("L8_" + name, shp, dt))
    xb_s = [sb(f"xb{k}", [128, TOK], BF16) for k in range(16)]
    h1_s = [sb(f"h1_{i}", [128, TOK], BF16) for i in range(64)]
    h16 = [sb(f"h16_{i}", [128, TOK], F16) for i in range(2)]
    b0_s = sb("b0_s", [128, 64]); b1_s = sb("b1_s", [128, 16])
    wbuf = [sb(f"wbuf{i}", [128, 512], BF16) for i in range(2)]
    e16 = [sb(f"e16_{i}", [128, TOK], F16) for i in range(2)]
    resid = [sb(f"res{i}", [128, TOK]) for i in range(2)]
    rb = [sb(f"rb{i}", [128, TOK]) for i in range(2)]
    outb = [sb(f"outb{i}", [128, TOK]) for i in range(2)]
    ps_p = [psa(f"ps_p{i}", [128, TOK]) for i in range(4)]

    if True:
        pass
        t.step("gpsimd", lambda e: e.dma_start(out=b0_s[:], in_=i_b0), make=["b0"], dma=True)
        t.step("gpsimd", lambda e: e.dma_start(out=b1_s[:], in_=i_b1), make=["b1"], dma=True)
        for k in range(16):
            t.step("sync", lambda e, k=k: e.dma_start(out=xb_s[k][:], in_=i_xn1[k*128:(k+1)*128, :]),
                   make=[f"xb{k}"], dma=True)
        for og in range(16):
            for k in range(16):
                wb = wbuf[k % 2]
                t.step("gpsimd", lambda e, k=k, og=og, wb=wb: e.dma_start(
                        out=wb[:], in_=i_w0[k*128:(k+1)*128, og*512:(og+1)*512]),
                       make=[f"wb{k%2}"], dma=True)
                for to in range(4):
                    for (o, n) in [(0, 512), (512, 512)]:
                        t.step("tensor", lambda e, k=k, o=o, n=n, to=to, wb=wb: e.matmul(
                                ps_p[to][:, o:o+n], wb[:, to*128:(to+1)*128], xb_s[k][:, o:o+n],
                                start=(k == 0), stop=(k == 15)),
                               use=[f"wb{k%2}", f"xb{k}"], make=[f"ps_p{to}"])
            for to in range(4):
                ot = og * 4 + to
                pi = to % 2
                t.step("scalar", lambda e, ot=ot, to=to, pi=pi: e.activation(
                        h16[pi][:], ps_p[to][:], AF.Gelu, bias=b0_s[:, ot:ot+1]),
                       use=[f"ps_p{to}", "b0"], make=[f"h16_{pi}"])
                t.step("vector", lambda e, ot=ot, pi=pi: e.tensor_copy(h1_s[ot][:], h16[pi][:]),
                       use=[f"h16_{pi}"], make=[f"h1_{ot}"])
        for og in range(4):
          for to4 in range(4):
            pass
          for k in range(64):
            wb = wbuf[k % 2]
            t.step("gpsimd", lambda e, k=k, og=og, wb=wb: e.dma_start(
                    out=wb[:], in_=i_w1[k*128:(k+1)*128, og*512:(og+1)*512]),
                   make=[f"wb{k%2}"], dma=True)
            for to in range(4):
                for (o, n) in [(0, 512), (512, 512)]:
                    t.step("tensor", lambda e, k=k, o=o, n=n, to=to, wb=wb: e.matmul(
                            ps_p[to][:, o:o+n], wb[:, to*128:(to+1)*128], h1_s[k][:, o:o+n],
                            start=(k == 0), stop=(k == 63)),
                           use=[f"wb{k%2}", f"h1_{k}"], make=[f"ps_p{to}"])
          for to in range(4):
            ot = og * 4 + to
            pi = to % 2
            t.step("vector", lambda e, ot=ot, to=to, pi=pi: e.tensor_scalar_add(
                    e16[pi][:], ps_p[to][:], b1_s[:, ot:ot+1]),
                   use=[f"ps_p{to}", "b1"], make=[f"e16_{pi}"])
            t.step("sync", lambda e, ot=ot, pi=pi: e.dma_start(out=resid[pi][:], in_=i_x1f[ot*128:(ot+1)*128, :]),
                   make=[f"res{pi}"], dma=True)
            t.step("gpsimd", lambda e, ot=ot, pi=pi: e.dma_start(out=rb[pi][:], in_=i_att[ot*128:(ot+1)*128, :]),
                   make=[f"rb{pi}"], dma=True)
            t.step("vector", lambda e, pi=pi: e.tensor_add(resid[pi][:], resid[pi][:], rb[pi][:]),
                   use=[f"res{pi}", f"rb{pi}"], make=[f"res{pi}"])
            t.step("vector", lambda e, pi=pi: e.tensor_add(outb[pi][:], resid[pi][:], e16[pi][:]),
                   use=[f"res{pi}", f"e16_{pi}"], make=[f"out{pi}"])
            t.step("sync", lambda e, ot=ot, pi=pi: e.dma_start(out=o_out[ot*128:(ot+1)*128, :], in_=outb[pi][:]),
                   use=[f"out{pi}"], dma=True)
    es.close()





class BlockedCols:
    def __init__(self, ap3, B):
        self.ap3 = ap3
        self.B = B
    def __getitem__(self, idx):
        rs, cs = idx
        r = cs.start // self.B
        assert cs.stop <= (r + 1) * self.B, (cs, self.B)
        return self.ap3[rs, r, cs.start - r * self.B: cs.stop - r * self.B]


def emit_l3_fused(fctx):
    nc, t, IO = fctx
    i_ccout = IO["ccst_out"]
    i_ccin = IO["ccst_in"]
    i_flags = IO["flags"]
    i_C = IO["Cm"]; i_acum = IO["acum"]; i_ypart = IO["ypart"]; i_gate = IO["gate"]
    i_normw = IO["normw"]; i_opwT = IO["opwT"]; i_xT = IO["xTown"]; i_rep = IO["repmat"]
    o_x1 = IO["o_x1"]
    TOK, CH, NCH, NH, INTER, D = 1024, 256, 4, 64, 4096, 2048

    es = ExitStack()
    def sb(name, shp, dt=F32): return es.enter_context(nc.sbuf_tensor(name, shp, dt))
    def psa(name, shp, dt=F32): return es.enter_context(nc.psum_tensor(name, shp, dt))
    P = sb("f3P", [128, INTER])
    stbuf = [sb(f"f3st{i}", [128, INTER]) for i in range(2)]
    Tcol = sb("f3Tcol", [64, 1]); eT = sb("f3eT", [64, 1])
    flags_s = sb("f3flags", [1, 32]); flagB = sb("f3flagB", [128, 32])
    repf_s = sb("f3rep", [64, INTER])
    reprow = sb("f3reprow", [1, INTER])
    C_s = sb("f3C", [128, TOK], BF16)
    acum_s = sb("f3acum", [64, TOK]); eacum_s = sb("f3eacum", [64, TOK])
    erow = sb("f3erow", [1, TOK])
    ones_row1 = sb("f3ones1", [1, 128])
    ones_colb = sb("f3onesc", [128, 1], BF16)
    Csc = [sb(f"f3Csc{i}", [128, CH], BF16) for i in range(2)]
    Pbf = [sb(f"f3Pbf{i}", [128, 64], BF16) for i in range(2)]
    ych = [sb(f"f3ych{i}", [128, CH]) for i in range(32)]
    ypbuf = [sb(f"f3ypb{i}", [128, CH]) for i in range(2)]
    gbuf = [sb(f"f3gb{i}", [128, CH], BF16) for i in range(2)]
    sgb = [sb(f"f3sg{i}", [128, CH]) for i in range(2)]
    x2b = [sb(f"f3x2{i}", [128, CH], BF16) for i in range(2)]
    ynb = [sb(f"f3yn{i}", [128, CH], BF16) for i in range(32)]
    rmsrow = sb("f3rms", [1, CH]); eps1 = sb("f3eps", [1, 1])
    normw_s = sb("f3nw", [128, 32])
    wbuf = [sb(f"f3wb{i}", [128, 256], BF16) for i in range(2)]
    xtb = [sb(f"f3xt{i}", [128, CH]) for i in range(2)]
    x1b = [sb(f"f3x1{i}", [128, CH]) for i in range(2)]
    Ptmp = sb("f3Ptmp", [128, 512])
    ps_rep = psa("f3ps_rep", [1, 512])
    ps_bc = psa("f3ps_bc", [128, 512])
    ps_yo = [psa(f"f3ps_yo{i}", [64, CH]) for i in range(2)]
    ps_rms = psa("f3ps_rms", [1, CH])
    ps_bcr = psa("f3ps_bcr", [128, CH])
    ps_o = [psa(f"f3ps_o{i}", [128, CH]) for i in range(2)]

    t.step("gpsimd", lambda e: e.dma_start(out=flags_s[:], in_=i_flags), make=["f3flags"], dma=True)
    t.step("gpsimd", lambda e: e.dma_start(out=C_s[:], in_=i_C), make=["C"], dma=True)
    t.step("gpsimd", lambda e: e.dma_start(out=acum_s[:], in_=i_acum), make=["acum"], dma=True)
    t.step("gpsimd", lambda e: e.dma_start(out=repf_s[:], in_=i_rep), make=["rep"], dma=True)
    t.step("gpsimd", lambda e: e.dma_start(out=normw_s[:], in_=i_normw), make=["normw"], dma=True)
    t.step("gpsimd", lambda e: e.memset(ones_row1[:], 1.0), make=["ones1"])
    t.step("gpsimd", lambda e: e.memset(ones_colb[:], 1.0), make=["onesc"])
    t.step("gpsimd", lambda e: e.memset(eps1[:], 1e-5), make=["eps1"])
    t.step("gpsimd", lambda e: e.memset(P[:], 0.0), make=["P"])
    t.step("scalar", lambda e: e.activation(eacum_s[:], acum_s[:], AF.Exp), use=["acum"], make=["eacum"])
    t.step("tensor", lambda e: e.matmul(ps_bc[:, 0:32], ones_row1[0:1, :], flags_s[0:1, :], start=True, stop=True),
           use=["ones1", "f3flags"], make=["ps_bc"])
    t.step("vector", lambda e: e.tensor_copy(flagB[:], ps_bc[:, 0:32]), use=["ps_bc"], make=["flagB"])

    def p_update(j, own_k=None):
        sbn = f"f3s{j%2}"; stb = stbuf[j % 2]
        if own_k is None:
            r, kk = j // 4, j % 4
            t.step("sync", lambda e, r=r, kk=kk, stb=stb: e.dma_start(
                    out=stb[:], in_=i_ccout[r*576 + kk*128 : r*576 + (kk+1)*128, :]), make=[sbn], dma=True)
            t.step("sync", lambda e, r=r, kk=kk: e.dma_start(
                    out=Tcol[:], in_=i_ccout[r*576 + 512 : r*576 + 576, kk:kk+1], allow_slow_non_contiguous=True), make=["Tcol"], dma=True)
            t.step("vector", lambda e, j=j: e.tensor_scalar_mul(Tcol[:], Tcol[:], flagB[0:64, j:j+1]),
                   use=["Tcol", "flagB"], make=["Tcol"])
        else:
            kk = own_k
            t.step("sync", lambda e, kk=kk, stb=stb: e.dma_start(
                    out=stb[:], in_=i_ccin[kk*128:(kk+1)*128, :]), make=[sbn], dma=True)
            t.step("vector", lambda e, kk=kk: e.tensor_copy(Tcol[:], acum_s[:, (kk+1)*CH-1:(kk+1)*CH]),
                   use=["acum"], make=["Tcol"])
        t.step("scalar", lambda e: e.activation(eT[:], Tcol[:, 0:1], AF.Exp), use=["Tcol"], make=["eT"])
        for q in range(8):
            t.step("tensor", lambda e, q=q: e.matmul(ps_rep[:], eT[:, 0:1], repf_s[:, q*512:(q+1)*512],
                                                     start=True, stop=True),
                   use=["eT", "rep"], make=["ps_rep"])
            t.step("scalar", lambda e, q=q: e.activation(reprow[0:1, q*512:(q+1)*512], ps_rep[:], AF.Copy),
                   use=["ps_rep"], make=["reprow"])
        for q in range(8):
            t.step("tensor", lambda e, q=q: e.matmul(ps_bc[:], ones_row1[0:1, :],
                                                     reprow[0:1, q*512:(q+1)*512], start=True, stop=True),
                   use=["ones1", "reprow"], make=["ps_bc"])
            t.step("vector", lambda e, q=q: e.tensor_mul(Ptmp[:], P[:, q*512:(q+1)*512], ps_bc[:]),
                   use=["P", "ps_bc"], make=["Ptmp"])
            if own_k is None:
                t.step("vector", lambda e, q=q, j=j: e.scalar_tensor_tensor(
                        P[:, q*512:(q+1)*512], stb[:, q*512:(q+1)*512], flagB[:, j:j+1], Ptmp[:],
                        ALU.mult, ALU.add),
                       use=["Ptmp", sbn, "flagB"], make=["P"])
            else:
                t.step("vector", lambda e, q=q: e.tensor_add(P[:, q*512:(q+1)*512], Ptmp[:], stb[:, q*512:(q+1)*512]),
                       use=["Ptmp", sbn], make=["P"])

    for j in range(32):
        p_update(j)

    for k in range(NCH):
        for h in range(64):
            hi = h % 2
            t.step("gpsimd", lambda e, h=h: e.dma_start(out=erow[:], in_=eacum_s[h:h+1, :]),
                   use=["eacum"], make=["erow"], dma=True)
            t.step("tensor", lambda e, k=k: e.matmul(ps_bc[:, 0:CH], ones_row1[0:1, :],
                                                     erow[0:1, k*CH:(k+1)*CH], start=True, stop=True),
                   use=["ones1", "erow"], make=["ps_bc"])
            t.step("vector", lambda e, k=k, hi=hi: e.tensor_mul(Csc[hi][:], C_s[:, k*CH:(k+1)*CH], ps_bc[:, 0:CH]),
                   use=["C", "ps_bc"], make=[f"Csc{hi}"])
            t.step("vector", lambda e, h=h, hi=hi: e.tensor_copy(Pbf[hi][:], P[:, h*64:(h+1)*64]),
                   use=["P"], make=[f"Pbf{hi}"])
            t.step("tensor", lambda e, hi=hi: e.matmul(ps_yo[hi][:], Pbf[hi][:], Csc[hi][:], start=True, stop=True),
                   use=[f"Pbf{hi}", f"Csc{hi}"], make=[f"ps_yo{hi}"])
            ctile = h // 2; cbase = (h % 2) * 64
            ypb = ypbuf[hi]
            t.step("sync", lambda e, h=h, k=k, ypb=ypb: e.dma_start(
                    out=ypb[0:64, :], in_=i_ypart[h*64:(h+1)*64, k*CH:(k+1)*CH]),
                   make=[f"ypb{hi}"], dma=True)
            t.step("vector", lambda e, hi=hi, ctile=ctile, cbase=cbase, ypb=ypb: e.tensor_add(
                    ych[ctile][cbase:cbase+64, :], ypb[0:64, :], ps_yo[hi][:]),
                   use=[f"ypb{hi}", f"ps_yo{hi}"], make=[f"ych{ctile}"])
        for i in range(32):
            gi = i % 2
            t.step("sync", lambda e, i=i, k=k, gi=gi: e.dma_start(
                    out=gbuf[gi][:], in_=i_gate[i*128:(i+1)*128, k*CH:(k+1)*CH]),
                   make=[f"gb{gi}"], dma=True)
            t.step("scalar", lambda e, gi=gi: e.activation(sgb[gi][:], gbuf[gi][:], AF.Silu),
                   use=[f"gb{gi}"], make=[f"sg{gi}"])
            t.step("vector", lambda e, i=i, gi=gi: e.tensor_mul(ych[i][:], ych[i][:], sgb[gi][:]),
                   use=[f"ych{i}", f"sg{gi}"], make=[f"ych{i}"])
            t.step("scalar", lambda e, i=i, gi=gi: e.activation(x2b[gi][:], ych[i][:], AF.Square),
                   use=[f"ych{i}"], make=[f"x2{gi}"])
            t.step("tensor", lambda e, i=i, gi=gi: e.matmul(ps_rms[:], ones_colb[:, 0:1], x2b[gi][:],
                                                            start=(i == 0), stop=(i == 31)),
                   use=["onesc", f"x2{gi}"], make=["ps_rms"])
        t.step("scalar", lambda e: e.activation(rmsrow[:], ps_rms[:], AF.Copy, scale=1.0/INTER),
               use=["ps_rms"], make=["rmsrow"])
        t.step("scalar", lambda e: e.activation(rmsrow[:], rmsrow[:], AF.Sqrt, bias=eps1[0:1, 0:1]),
               use=["rmsrow", "eps1"], make=["rmsrow"])
        t.step("vector", lambda e: e.reciprocal(rmsrow[:], rmsrow[:]), use=["rmsrow"], make=["rmsrow"])
        t.step("tensor", lambda e: e.matmul(ps_bcr[:], ones_row1[0:1, :], rmsrow[0:1, :], start=True, stop=True),
               use=["ones1", "rmsrow"], make=["ps_bcr"])
        for i in range(32):
            t.step("vector", lambda e, i=i: e.scalar_tensor_tensor(
                    ynb[i][:], ych[i][:], normw_s[:, i:i+1], ps_bcr[:], ALU.mult, ALU.mult),
                   use=[f"ych{i}", "normw", "ps_bcr"], make=[f"ynb{i}"])
        for og in range(8):
            for kk in range(32):
                wi = kk % 2
                t.step("gpsimd", lambda e, og=og, kk=kk, wi=wi: e.dma_start(
                        out=wbuf[wi][:], in_=i_opwT[kk*128:(kk+1)*128, og*256:(og+1)*256]),
                       make=[f"wb{wi}"], dma=True)
                for to in range(2):
                    t.step("tensor", lambda e, kk=kk, to=to, wi=wi: e.matmul(
                            ps_o[to][:], wbuf[wi][:, to*128:(to+1)*128], ynb[kk][:],
                            start=(kk == 0), stop=(kk == 31)),
                           use=[f"wb{wi}", f"ynb{kk}"], make=[f"ps_o{to}"])
            for to in range(2):
                ot = og * 2 + to
                t.step("sync", lambda e, ot=ot, k=k, to=to: e.dma_start(
                        out=xtb[to][:], in_=i_xT[ot*128:(ot+1)*128, k*CH:(k+1)*CH]),
                       make=[f"xt{to}"], dma=True)
                t.step("vector", lambda e, to=to: e.tensor_add(x1b[to][:], xtb[to][:], ps_o[to][:]),
                       use=[f"xt{to}", f"ps_o{to}"], make=[f"x1{to}"])
                t.step("sync", lambda e, ot=ot, k=k, to=to: e.dma_start(
                        out=o_x1[ot*128:(ot+1)*128, k*CH:(k+1)*CH], in_=x1b[to][:]),
                       use=[f"x1{to}"], dma=True)
        p_update(32 + k, own_k=k)
    es.close()


def full_barrier(t):
    for e in ("sync", "gpsimd", "tensor", "vector", "scalar"):
        t.wait_all(e)


def build_fused():
    nc = bass.Bass("TRN2", target_bir_lowering=False)
    dr = lambda n, s, d: nc.dram_tensor(n, s, d, kind="ExternalInput").ap()
    IO = {}
    IO["xT"] = dr("xT", [2048, 1028], F32)
    IO["ipwT"] = dr("ipwT", [2048, 8512], F32)
    IO["convw"] = dr("convw", [128, 136], F32)
    IO["convb"] = dr("convb", [128, 34], F32)
    IO["dtb"] = dr("dtb", [64, 1], F32)
    IO["alog"] = dr("alog", [64, 1], F32)
    IO["selmat"] = dr("selmat", [64, 4096], F32)
    IO["eye"] = dr("eye", [128, 128], F32)
    IO["dchan"] = dr("dchan", [128, 32], F32)
    IO["maskst"] = dr("maskst", [128, 512], F32)
    IO["flags"] = dr("flags", [1, 32], F32)
    IO["normw"] = dr("normw", [128, 32], F32)
    IO["opwT"] = dr("opwT", [4096, 2048], F32)
    IO["repmat"] = dr("repmat", [64, 4096], F32)
    IO["gwT"] = dr("gwT", [2048, 32], F32)
    IO["gupT"] = dr("gupT", [32, 2048, 1536], F32)
    IO["dnT"] = dr("dnT", [32, 768, 2048], F32)
    IO["mgf"] = dr("mgf", [1, 32], F32)
    IO["dei"] = dr("dei", [1, 8], I32)
    IO["cid"] = dr("cid", [1, 1], I32)
    IO["cidf"] = dr("cidf", [1, 1], F32)
    IO["WqT"] = dr("WqT", [2048, 2048], F32)
    IO["WkT"] = dr("WkT", [2048, 512], F32)
    IO["WvT"] = dr("WvT", [2048, 512], F32)
    IO["cosm"] = dr("cosm", [128, 1024], F32)
    IO["sinm"] = dr("sinm", [128, 1024], F32)
    IO["maskf"] = dr("maskf", [8192, 1024], BF16)
    IO["WoT"] = dr("WoT", [2048, 2048], F32)
    IO["W0T"] = dr("W0T", [2048, 8192], F32)
    IO["b0c"] = dr("b0c", [128, 64], F32)
    IO["W1T"] = dr("W1T", [8192, 2048], F32)
    IO["b1c"] = dr("b1c", [128, 16], F32)
    IO["o_out"] = nc.dram_tensor("o_out", [2048, 1024], F32, kind="ExternalOutput").ap()
    xn_d = nc.dram_tensor("xn_d", [2048, 1024], BF16)
    xnall_d = nc.dram_tensor("xnall_d", [16384, 1024], BF16)
    gate_d = nc.dram_tensor("gate_d", [4096, 1024], BF16)
    xpre_d = nc.dram_tensor("xpre_d", [4096, 1024], BF16)
    xdtm_d = nc.dram_tensor("xdtm_d", [1024, 4096], BF16)
    B_d = nc.dram_tensor("B_d", [128, 1024], BF16)
    C_d = nc.dram_tensor("C_d", [128, 1024], BF16)
    acum_d = nc.dram_tensor("acum_d", [64, 1024], F32)
    dt_d = nc.dram_tensor("dt_d", [64, 1024], F32)
    ypart_d = nc.dram_tensor("ypart_d", [4096, 1024], F32)
    ccst_in = nc.dram_tensor("ccst_in", [576, 4096], F32)
    ccst_out = nc.dram_tensor("ccst_out", [576 * 8, 4096], F32)
    x1_d = nc.dram_tensor("x1_d", [2048, 1024], F32)
    contrib_in = nc.dram_tensor("contrib_in", [16384, 1024], F32)
    moe_d = nc.dram_tensor("moe_d", [2048, 1024], F32)
    x1f_d = nc.dram_tensor("x1f_d", [2048, 1024], F32)
    xn1_d = nc.dram_tensor("xn1_d", [2048, 1024], BF16)
    q_d = nc.dram_tensor("q_d", [2048, 1024], BF16)
    k_d = nc.dram_tensor("k_d", [512, 1024], BF16)
    v_d = nc.dram_tensor("v_d", [1024, 512], BF16)
    kall_d = nc.dram_tensor("kall_d", [4096, 1024], BF16)
    vall_d = nc.dram_tensor("vall_d", [8192, 512], BF16)
    attnp_d = nc.dram_tensor("attnp_d", [2048, 1024], BF16)

    rg = [list(range(8))]
    es = ExitStack()
    with nc.Block() as block:
        t = Tape(nc, es)
        io1 = dict(IO)
        io1.update({"o_xn": xn_d.ap(), "o_gate": gate_d.ap(), "o_xpre": xpre_d.ap(),
                    "o_xdtm": xdtm_d.ap(), "o_B": B_d.ap(), "o_C": C_d.ap(),
                    "o_acum": acum_d.ap(), "o_dt": dt_d.ap()})
        emit_launch1((nc, t, io1))
        full_barrier(t)
        t.step("gpsimd", lambda e: e.collective_compute(
                "AllGather", ALU.bypass, rg, [xn_d.ap().opt()], [xnall_d.ap().opt()]),
               make=["xnall"], tag="ag_xn", cc=True)
        zz = es.enter_context(nc.sbuf_tensor("mz_zz", [64, 512], F32))
        t.step("gpsimd", lambda e: e.memset(zz[:], 0.0), make=["mzz"])
        for zq in range(8):
            t.step("gpsimd", lambda e, zq=zq: e.dma_start(out=ccst_in.ap()[512:576, zq*512:(zq+1)*512], in_=zz[:]),
                   use=["mzz"], dma=True)
        io2 = dict(IO)
        io2.update({"xdtm": xdtm_d.ap(), "Bm": B_d.ap(), "Cm": C_d.ap(), "acum": acum_d.ap(),
                    "xpre": xpre_d.ap(),
                    "o_ypart": ypart_d.ap(),
                    "o_states": ccst_in.ap()[0:512, :].rearrange("(c p) n -> c p n", p=128),
                    "o_T": ccst_in.ap()[512:576, 0:4]})
        emit_launch2((nc, t, io2))
        full_barrier(t)
        t.step("gpsimd", lambda e: e.collective_compute(
                "AllGather", ALU.bypass, rg, [ccst_in.ap().opt()], [ccst_out.ap().opt()]),
               make=["ccstout"], tag="ag_st", cc=True)
        full_barrier(t)
        io3 = dict(IO)
        io3.update({"ccst_out": ccst_out.ap(), "ccst_in": ccst_in.ap()[0:512, :],
                    "Cm": C_d.ap(), "acum": acum_d.ap(), "ypart": ypart_d.ap(),
                    "gate": gate_d.ap(), "xTown": IO["xT"][:, 4:1028], "o_x1": x1_d.ap()})
        emit_l3_fused((nc, t, io3))
        full_barrier(t)
        io4 = dict(IO)
        io4.update({"xnfull": BlockedCols(xnall_d.ap().rearrange("(r d) t -> d r t", r=8), 1024),
                    "o_contrib": BlockedCols(contrib_in.ap().rearrange("(r d) t -> d r t", r=8), 1024)})
        emit_launch4((nc, t, io4))
        full_barrier(t)
        t.step("gpsimd", lambda e: e.collective_compute(
                "ReduceScatter", ALU.add, rg, [contrib_in.ap().opt()], [moe_d.ap().opt()]),
               make=["moed"], tag="rs_moe", cc=True)
        full_barrier(t)
        io6 = dict(IO)
        io6.update({"x1own": x1_d.ap(),
                    "contribs": moe_d.ap().rearrange("(g d) t -> g d t", g=1),
                    "o_x1f": x1f_d.ap(), "o_xn1": xn1_d.ap(), "o_q": q_d.ap(),
                    "o_k": k_d.ap(), "o_v": v_d.ap()})
        emit_launch6((nc, t, io6))
        full_barrier(t)
        t.step("gpsimd", lambda e: e.collective_compute(
                "AllGather", ALU.bypass, rg, [k_d.ap().opt()], [kall_d.ap().opt()]),
               make=["kall"], tag="ag_k", cc=True)
        t.step("gpsimd", lambda e: e.collective_compute(
                "AllGather", ALU.bypass, rg, [v_d.ap().opt()], [vall_d.ap().opt()]),
               make=["vall"], tag="ag_v", cc=True)
        full_barrier(t)
        io7 = dict(IO)
        io7.update({"qown": q_d.ap(),
                    "Kf": BlockedCols(kall_d.ap().rearrange("(r kd) t -> kd r t", r=8), 1024),
                    "Vtm": vall_d.ap(), "o_attnp": attnp_d.ap()})
        emit_launch7((nc, t, io7))
        full_barrier(t)
        io8 = dict(IO)
        io8.update({"xn1": xn1_d.ap(), "x1f": x1f_d.ap(), "attnp": attnp_d.ap()})
        emit_launch8((nc, t, io8))
        full_barrier(t)
        t.flush(block)
    es.close()
    return nc

_CACHE = {}

def _stage(inp):
    f32 = np.float32
    x = np.asarray(inp['L0_attn_hidden_states'][0], f32)
    xTf = np.ascontiguousarray(x.T)
    ipwT = np.ascontiguousarray(np.asarray(inp['L0_attn_in_proj_w'], f32).T)
    cw = np.asarray(inp['L0_attn_conv_w'], f32)[:, 0, :]
    convw = np.ascontiguousarray(cw.reshape(34, 128, 4).transpose(1, 0, 2).reshape(128, 136))
    convb = np.ascontiguousarray(np.asarray(inp['L0_attn_conv_b'], f32).reshape(34, 128).T)
    dtb = np.ascontiguousarray(np.asarray(inp['L0_attn_dt_bias'], f32).reshape(64, 1))
    alog = np.ascontiguousarray(np.asarray(inp['L0_attn_A_log'], f32).reshape(64, 1))
    sel = np.zeros((64, 32 * 128), f32)
    for ti in range(32):
        for m in range(128):
            sel[2 * ti + (m // 64), ti * 128 + m] = 1.0
    eye = np.eye(128, dtype=f32)
    D = np.asarray(inp['L0_attn_D'], f32)
    dchan = np.ascontiguousarray(np.repeat(D, 64).reshape(32, 128).T)
    maskst = np.zeros((128, 512), f32)
    for s in range(2):
        for sr in range(128):
            maskst[sr, s*256:(s+1)*256] = (np.arange(256) >= s * 128 + sr)
    nw = np.asarray(inp['L0_attn_norm_w'], f32)
    normw = np.ascontiguousarray(nw.reshape(32, 128).T)
    opwT = np.ascontiguousarray(np.asarray(inp['L0_attn_out_proj_w'], f32).T)
    rep = np.zeros((64, 4096), f32)
    for col in range(4096):
        rep[col // 64, col] = 1.0
    gwT = np.ascontiguousarray(np.asarray(inp['L0_ffn_gate_weight'], f32).T)
    gupT = np.ascontiguousarray(np.asarray(inp['L0_ffn_gate_up_proj'], f32).transpose(0, 2, 1))
    dnT = np.ascontiguousarray(np.asarray(inp['L0_ffn_down_proj'], f32).transpose(0, 2, 1))
    mgf = np.asarray(inp['L0_ffn_merge_groups']).astype(f32).reshape(1, 32)
    de = np.asarray(inp['L0_ffn_dominant_experts']).astype(np.int32).reshape(1, 8)
    WqT = np.ascontiguousarray(np.asarray(inp['L1_attn_Wq'], f32).T)
    WkT = np.ascontiguousarray(np.asarray(inp['L1_attn_Wk'], f32).T)
    WvT = np.ascontiguousarray(np.asarray(inp['L1_attn_Wv'], f32).T)
    WoT = np.ascontiguousarray(np.asarray(inp['L1_attn_Wo'], f32).T)
    inv = (1.0 / 10000.0 ** (np.arange(0, 128, 2, dtype=f32) / 128)).astype(f32)
    freqs = np.arange(8192, dtype=f32)[:, None] * inv[None, :]
    cosf = np.repeat(np.cos(freqs), 2, axis=-1).astype(f32)
    sinf = np.repeat(np.sin(freqs), 2, axis=-1).astype(f32)
    W0T = np.ascontiguousarray(np.asarray(inp['L1_ffn_W0'], f32).T)
    b0c = np.ascontiguousarray(np.asarray(inp['L1_ffn_bias0'], f32).reshape(64, 128).T)
    W1T = np.ascontiguousarray(np.asarray(inp['L1_ffn_W1'], f32).T)
    b1c = np.ascontiguousarray(np.asarray(inp['L1_ffn_bias1'], f32).reshape(16, 128).T)
    kk = np.arange(8192)[:, None]
    maps = []
    for c in range(8):
        t0 = c * 1024
        xs = np.zeros((2048, 1028), f32)
        if c > 0:
            xs[:, 0:4] = xTf[:, t0-4:t0]
        xs[:, 4:] = xTf[:, t0:t0+1024]
        flags = np.zeros((1, 32), f32); flags[0, :4*c] = 1.0
        qq = np.arange(t0, t0 + 1024)[None, :]
        maskf = (kk <= qq + 128).astype(ml_dtypes.bfloat16)
        maps.append({
            "xT": np.ascontiguousarray(xs), "ipwT": ipwT, "convw": convw, "convb": convb,
            "dtb": dtb, "alog": alog, "selmat": sel, "eye": eye, "dchan": dchan,
            "maskst": maskst, "flags": flags, "normw": normw, "opwT": opwT, "repmat": rep,
            "gwT": gwT, "gupT": gupT, "dnT": dnT, "mgf": mgf, "dei": de,
            "cid": np.array([[c]], np.int32), "cidf": np.array([[c]], f32),
            "WqT": WqT, "WkT": WkT, "WvT": WvT, "WoT": WoT,
            "cosm": np.ascontiguousarray(cosf[t0:t0+1024].T), "sinm": np.ascontiguousarray(sinf[t0:t0+1024].T),
            "maskf": maskf, "W0T": W0T, "b0c": b0c, "W1T": W1T, "b1c": b1c,
        })
    return maps


def kernel(**inp):
    if "nc" not in _CACHE:
        _CACHE["nc"] = build_fused()
    nc = _CACHE["nc"]
    maps = _stage(inp)
    res = run_bass_kernel_spmd(nc, maps, core_ids=NC8)
    out = np.concatenate([np.asarray(res.results[c]["o_out"]) for c in range(8)], axis=1)
    return np.ascontiguousarray(out.T).reshape(1, 8192, 2048).astype(np.float32)
